# version 13
# baseline (speedup 1.0000x reference)
import os
import numpy as np
import ml_dtypes
from contextlib import ExitStack

import concourse.bass as bass
import concourse.mybir as mybir
from concourse.bass_utils import run_bass_kernel_spmd

F32 = mybir.dt.float32
BF16 = mybir.dt.bfloat16
AF = mybir.ActivationFunctionType
ALU = mybir.AluOpType
AX = mybir.AxisListType

S = 2048
E = 2048
NH = 16
NKV = 4
NGH = 12
RR = 32
INTER = 5632
NIC = INTER // 128
NEC = E // 128
NTT = S // 128
NG = 4
EPS_LN = 1e-5
EPS_RMS = 1e-6
INV_SQRT_D = float(1.0 / np.sqrt(128.0))
NEG_BIG = -1.0e38
BMARGIN = 0.006


class Sem:
    def __init__(self, ctx, nc, name):
        self.h = ctx.enter_context(nc.semaphore(name))
        self.v = 0

    def inc(self, n=1):
        self.v += n
        return self.v


def _ap(h, off, dims):
    return bass.AP(h, off, dims)


def build_program(taps=(), stop_after=None):
    PH = ["p0", "p1v", "p1k", "p1r", "p1", "p2", "p3", "p4", "p5", "p6"]
    go = lambda p: stop_after is None or PH.index(p) <= PH.index(stop_after)
    taps = set(taps)
    nc = bass.Bass("TRN2", target_bir_lowering=False)

    def dram(name, shape, dt, kind):
        return nc.dram_tensor(name, shape, dt, kind=kind)

    x_h = dram("x", [S, E], F32, "ExternalInput")
    wq_h = dram("wq", [NH, E, 128], F32, "ExternalInput")
    wk_h = dram("wk", [NKV, E, 128], F32, "ExternalInput")
    wv_h = dram("wv", [E, NKV * 128], F32, "ExternalInput")
    wo_h = dram("wo", [NEC, E, 128], F32, "ExternalInput")
    wrq_h = dram("wrq", [3, E, 128], F32, "ExternalInput")
    wrk_h = dram("wrk", [3, E, 128], F32, "ExternalInput")
    wg_h = dram("wg", [NIC, E, 128], F32, "ExternalInput")
    wu_h = dram("wu", [NIC, E, 128], F32, "ExternalInput")
    wd_h = dram("wd", [4, NIC, 128, 512], F32, "ExternalInput")
    qnw_h = dram("qnw", [128, 1], F32, "ExternalInput")
    knw_h = dram("knw", [128, 1], F32, "ExternalInput")
    gsrep_h = dram("gsrep", [3, 128], F32, "ExternalInput")
    gsrow_h = dram("gsrow", [1, 12], F32, "ExternalInput")
    gates_h = dram("gates_c", [128, NIC], F32, "ExternalInput")
    ups_h = dram("ups_c", [128, NIC], F32, "ExternalInput")
    downs_h = dram("downs_c", [128, NEC], F32, "ExternalInput")
    ident_h = dram("ident", [128, 128], BF16, "ExternalInput")
    rtm_h = dram("rtm", [128, 128], BF16, "ExternalInput")
    cosf_h = dram("cosf", [128, S], F32, "ExternalInput")
    sinf_h = dram("sinf", [128, S], F32, "ExternalInput")
    masks_h = dram("masks", [128, 4 * 512], F32, "ExternalInput")
    onesc_h = dram("ones_col", [128, 1], BF16, "ExternalInput")
    onesr_h = dram("ones_rowf", [1, 128], F32, "ExternalInput")
    blk4_h = dram("blk4", [128, 4], BF16, "ExternalInput")

    y_h = dram("y", [S, E], F32, "ExternalOutput")

    def spill(name, shape, tapname):
        kind = "ExternalOutput" if tapname in taps else "Internal"
        return dram(name, shape, BF16, kind)

    spk_h = spill("sp_k", [NKV, 128, S], "k")
    spv_h = spill("sp_v", [128, NTT * 512], "v")
    sprku_h = spill("sp_rku", [3, 128, S], "r")
    sprqs_h = spill("sp_rqs", [3, 128, S], "r")
    sprqu_h = spill("sp_rqu", [128, S], "r")
    sprksh_h = spill("sp_rksh", [128, S], "r")
    spq_h = spill("sp_q", [NG, NH, 128, 512], "q")
    spg_h = spill("sp_g", [NEC, 128, S], "g")
    spo_h = spill("sp_o", [NH, 128, S], "o")
    txn_h = dram("t_xn", [128, NEC * S], BF16, "ExternalOutput") if "xn" in taps else None
    ts_h = dram("t_s", [128, NEC * S], BF16, "ExternalOutput") if "s" in taps else None
    tb_h = dram("t_b", [128, 64], F32, "ExternalOutput") if "b" in taps else None

    with ExitStack() as L0:
        sbuf = lambda n, sh, dt: L0.enter_context(nc.sbuf_tensor(n, sh, dt))
        sem = lambda n: Sem(L0, nc, n)

        ident = sbuf("c_ident", [128, 128], BF16)
        rtm = sbuf("c_rtm", [128, 128], BF16)
        onesc = sbuf("c_onesc", [128, 1], BF16)
        onesr = sbuf("c_onesr", [1, 128], F32)
        blk4 = sbuf("c_blk4", [128, 4], BF16)
        qnw = sbuf("c_qnw", [128, 1], F32)
        knw = sbuf("c_knw", [128, 1], F32)
        qnw_eff = sbuf("c_qnweff", [128, 1], F32)
        gs_cols = sbuf("c_gscols", [128, 3], F32)
        gs_half = sbuf("c_gshalf", [128, 1], F32)
        gsrow = sbuf("c_gsrow", [1, 12], F32)
        gates_c = sbuf("c_gates", [128, NIC], F32)
        ups_c = sbuf("c_ups", [128, NIC], F32)
        downs_c = sbuf("c_downs", [128, NEC], F32)
        actcst = sbuf("c_actcst", [128, 8], F32)
        maxq2 = sbuf("st_maxq2", [1, 64], F32)
        kvmax2 = sbuf("st_kvmax2", [1, 8], F32)
        rqmax2 = sbuf("st_rqmax2", [4, 6], F32)
        rkmax2 = sbuf("st_rkmax2", [4, 6], F32)
        bneg_b = sbuf("st_bnegb", [128, 64], F32)

        iL = sem("iL")
        with nc.Block("init") as block:
            @block.sync
            def _(sync):
                for dst, src in (
                    (ident, ident_h), (rtm, rtm_h), (onesc, onesc_h),
                    (onesr, onesr_h), (blk4, blk4_h), (qnw, qnw_h),
                    (knw, knw_h), (gsrow, gsrow_h), (gates_c, gates_h),
                    (ups_c, ups_h), (downs_c, downs_h),
                ):
                    sync.dma_start(out=dst[:, :], in_=src.ap()).then_inc(iL.h, 16)
                    iL.inc(16)
                sync.dma_start(
                    out=gs_cols[:, :], in_=_ap(gsrep_h, 0, [[1, 128], [128, 3]]),
                    allow_slow_non_contiguous=True,
                ).then_inc(iL.h, 16)
                iL.inc(16)

            @block.vector
            def _(vector):
                for i, v in enumerate(
                    (EPS_LN, EPS_RMS, 1e-30, -0.5, 0.5, -1.0, BMARGIN, 1.0 / 128.0)
                ):
                    vector.memset(actcst[:, i:i + 1], v)
                    nc.const_aps.register(F32, v, actcst[:, i:i + 1])
                vector.wait_ge(iL.h, iL.v)
                vector.tensor_scalar(qnw_eff[:, :], qnw[:, :], INV_SQRT_D, None, ALU.mult)
                vector.tensor_scalar(gs_half[:, :], gs_cols[:, 0:1], 0.5, None, ALU.mult)

        with ExitStack() as SA:
            xn = SA.enter_context(nc.sbuf_tensor("xn_fm", [128, NEC * S], BF16))

            def xn_cols(ec, c0, n):
                return xn[:, ec * S + c0: ec * S + c0 + n]

            with ExitStack() as SP:
                sb = lambda n, sh, dt: SP.enter_context(nc.sbuf_tensor(n, sh, dt))
                xt = sb("p0_xt", [128, 2 * E], F32)
                xnt = sb("p0_xnt", [128, E], BF16)
                xsq = sb("p0_xsq", [128, E], F32)
                st0 = sb("p0_st", [128, 16], F32)
                tp_ps = SP.enter_context(nc.psum_tensor("p0_tp", [128, E], BF16))
                aLp = [sem("aLa"), sem("aLb")]; aS = sem("aS"); aR = sem("aR")
                aX = sem("aX"); aT = sem("aT"); aE = sem("aE")
                with nc.Block("p0") as block:
                    @block.sync
                    def _(sync):
                        for tt in range(NTT):
                            if tt >= 2:
                                sync.wait_ge(aX.h, tt - 1)
                            sync.dma_start(
                                out=xt[:, (tt % 2) * E:(tt % 2) * E + E],
                                in_=_ap(x_h, tt * 128 * E, [[E, 128], [1, E]]),
                            ).then_inc(aLp[tt % 2].h, 16)
                            aLp[tt % 2].inc(16)

                    @block.vector
                    def _(vector):
                        for tt in range(NTT):
                            b = (tt % 2) * 8
                            xv = xt[:, (tt % 2) * E:(tt % 2) * E + E]
                            vector.wait_ge(aLp[tt % 2].h, 16 * (tt // 2 + 1))
                            if tt >= 2:
                                vector.wait_ge(aX.h, tt - 1)
                            vector.reduce_sum(st0[:, b + 0:b + 1], xv, AX.X)
                            vector.scalar_tensor_tensor(
                                xsq[:, :], xv, 1.0, xv, ALU.bypass, ALU.mult,
                                accum_out=st0[:, b + 1:b + 2])
                            vector.drain()
                            vector.tensor_scalar(st0[:, b + 2:b + 3], st0[:, b + 0:b + 1], 1.0 / E, None, ALU.mult)
                            vector.tensor_scalar(st0[:, b + 4:b + 5], st0[:, b + 1:b + 2], 1.0 / E, None, ALU.mult)
                            vector.drain()
                            vector.tensor_tensor(st0[:, b + 3:b + 4], st0[:, b + 2:b + 3], st0[:, b + 2:b + 3], ALU.mult)
                            vector.drain()
                            vector.tensor_tensor(st0[:, b + 5:b + 6], st0[:, b + 4:b + 5], st0[:, b + 3:b + 4], ALU.subtract).then_inc(aS.h, 1)
                            aS.inc()

                    @block.gpsimd
                    def _(gpsimd):
                        for tt in range(NTT):
                            b = (tt % 2) * 8
                            xv = xt[:, (tt % 2) * E:(tt % 2) * E + E]
                            gpsimd.wait_ge(aR.h, tt + 1)
                            if tt >= 1:
                                gpsimd.wait_ge(aT.h, tt)
                            gpsimd.tensor_scalar(
                                xnt[:, :], xv, st0[:, b + 2:b + 3], st0[:, b + 6:b + 7],
                                ALU.subtract, ALU.mult).then_inc(aX.h, 1)
                            aX.inc()

                    @block.scalar
                    def _(scalar):
                        for tt in range(NTT):
                            b = (tt % 2) * 8
                            scalar.wait_ge(aS.h, tt + 1)
                            if tt >= 1:
                                scalar.wait_ge(aX.h, tt)
                            scalar.activation(st0[:, b + 7:b + 8], st0[:, b + 5:b + 6], AF.Ln, bias=EPS_LN)
                            scalar.drain()
                            scalar.activation(st0[:, b + 6:b + 7], st0[:, b + 7:b + 8], AF.Exp, scale=-0.5).then_inc(aR.h, 1)
                            aR.inc()
                            if tt >= 1:
                                scalar.wait_ge(aT.h, tt)
                                dst = _ap(xn, (tt - 1) * 128,
                                          [[NEC * S, 128], [S, NEC], [1, 128]])
                                src = _ap(tp_ps, 0, [[E, 128], [128, NEC], [1, 128]])
                                scalar.copy(dst, src).then_inc(aE.h, 1)
                                aE.inc()
                        scalar.wait_ge(aT.h, NTT)
                        dst = _ap(xn, (NTT - 1) * 128,
                                  [[NEC * S, 128], [S, NEC], [1, 128]])
                        src = _ap(tp_ps, 0, [[E, 128], [128, NEC], [1, 128]])
                        scalar.copy(dst, src).then_inc(aE.h, 1)
                        aE.inc()

                    @block.tensor
                    def _(tensor):
                        for tt in range(NTT):
                            tensor.wait_ge(aX.h, tt + 1)
                            tensor.wait_ge(aE.h, tt)
                            for j in range(NEC):
                                mm = tensor.transpose(
                                    tp_ps[:, j * 128:(j + 1) * 128],
                                    xnt[:, j * 128:(j + 1) * 128],
                                    ident[:, :])
                            mm.then_inc(aT.h, 1)
                            aT.inc()

            if txn_h is not None:
                tS = sem("tSxn")
                with nc.Block("tapxn") as block:
                    @block.sync
                    def _(sync):
                        sync.dma_start(out=txn_h.ap(), in_=xn[:, :]).then_inc(tS.h, 16)
                        sync.wait_ge(tS.h, tS.inc(16))

            if go("p1v"):
                with ExitStack() as SP:
                    sb = lambda n, sh, dt: SP.enter_context(nc.sbuf_tensor(n, sh, dt))
                    kfm = sb("p1_kfm", [128, NKV * S], BF16)
                    vsb = sb("p1_vsb", [128, NTT * 512], BF16)
                    rqs = sb("p1_rqs", [128, 3 * S], BF16)
                    rku = sb("p1_rku", [128, 3 * S], BF16)
                    rqu = sb("p1_rqu", [128, S], BF16)
                    rksh = sb("p1_rksh", [128, S], BF16)
                    cosf = sb("p1_cosf", [128, S], F32)
                    sinf = sb("p1_sinf", [128, S], F32)
                    wvb = sb("p1_wvb", [128, NEC * 512], BF16)
                    wbuf = sb("p1_wbuf", [128, 2 * E], BF16)
                    ksq = sb("p1_ksq", [128, 1024], BF16)
                    rs_row = sb("p1_rsrow", [1, 1024], F32)
                    rsb_sb = sb("p1_rsb", [128, 1024], F32)
                    khat = sb("p1_khat", [128, 1024], BF16)
                    t1 = sb("p1_t1", [128, 1024], F32)
                    t2 = sb("p1_t2", [128, 1024], F32)

                    bL = sem("bL"); bWv = sem("bWv")
                    bWk = [sem("bWka"), sem("bWkb")]
                    bWr = [sem("bWra"), sem("bWrb")]
                    SPv = SP.enter_context(ExitStack())
                    vps = SPv.enter_context(nc.psum_tensor("p1_vps", [128, 2 * 512], F32))
                    bVM = sem("bVM"); bVE = sem("bVE")
                    if go('p1v'):
                        with nc.Block("p1v") as block:
                            @block.sync
                            def _(sync):
                                sync.dma_start(out=cosf[:, :], in_=cosf_h.ap()).then_inc(bL.h, 16)
                                sync.dma_start(out=sinf[:, :], in_=sinf_h.ap()).then_inc(bL.h, 16)
                                bL.inc(32)
                                sync.wait_ge(bL.h, bL.v)

                            @block.gpsimd
                            def _(gpsimd):
                                gpsimd.dma_start(
                                    out=wvb[:, :],
                                    in_=_ap(wv_h, 0, [[512, 128], [512 * 128, NEC], [1, 512]]),
                                ).then_inc(bWv.h, 16)
                                bWv.inc(16)

                            @block.tensor
                            def _(tensor):
                                tensor.wait_ge(bWv.h, 16)
                                for tt in range(NTT):
                                    if tt >= 2:
                                        tensor.wait_ge(bVE.h, tt - 1)
                                    po = (tt % 2) * 512
                                    for ec in range(NEC):
                                        mm = tensor.matmul(
                                            vps[:, po:po + 512],
                                            xn_cols(ec, tt * 128, 128),
                                            wvb[:, ec * 512:(ec + 1) * 512],
                                            start=(ec == 0), stop=(ec == NEC - 1))
                                    mm.then_inc(bVM.h, 1)
                                    bVM.inc()

                            @block.scalar
                            def _(scalar):
                                for tt in range(NTT):
                                    scalar.wait_ge(bVM.h, tt + 1)
                                    po = (tt % 2) * 512
                                    scalar.copy(
                                        vsb[:, tt * 512:(tt + 1) * 512], vps[:, po:po + 512]
                                    ).then_inc(bVE.h, 1)
                                    bVE.inc()

                    SPv.close()
                    kps = SP.enter_context(nc.psum_tensor("p1_kps", [128, 2 * 1024], F32))
                    sums_ps = SP.enter_context(nc.psum_tensor("p1_sums", [4, 1024], F32))
                    rot_ps = SP.enter_context(nc.psum_tensor("p1_rot", [128, 1024], F32))
                    bM = sem("bM"); bQ = sem("bQ"); bS = sem("bS"); bR = sem("bR")
                    bB = sem("bB"); bE = sem("bE"); bH = sem("bH"); bO = sem("bO")
                    bF = sem("bF"); bN1 = sem("bN1"); bNS = sem("bNS"); bNX = sem("bNX")

                    if go('p1k'):
                        with nc.Block("p1k") as block:
                            units = [(oc, hh) for oc in range(NKV) for hh in range(2)]

                            @block.gpsimd
                            def _(gpsimd):
                                for oc in range(NKV):
                                    if oc >= 2:
                                        gpsimd.wait_ge(bM.h, 2 * (oc - 1))
                                    gpsimd.dma_start(
                                        out=wbuf[:, (oc % 2) * E:(oc % 2) * E + E],
                                        in_=_ap(wk_h, oc * E * 128,
                                                [[128, 128], [128 * 128, NEC], [1, 128]]),
                                    ).then_inc(bWk[oc % 2].h, 16)
                                    bWk[oc % 2].inc(16)

                            def _kpost(tensor, u):
                                tensor.wait_ge(bQ.h, u + 1)
                                if u >= 1:
                                    tensor.wait_ge(bNX.h, u)
                                for q4 in range(2):
                                    mm = tensor.matmul(
                                        sums_ps[0:1, q4 * 512:(q4 + 1) * 512],
                                        onesc[:, :], ksq[:, q4 * 512:(q4 + 1) * 512],
                                        start=True, stop=True)
                                mm.then_inc(bS.h, 1)
                                bS.inc()
                                tensor.wait_ge(bR.h, u + 1)
                                for q4 in range(2):
                                    mm = tensor.matmul(
                                        rot_ps[:, q4 * 512:(q4 + 1) * 512],
                                        onesr[:, :], rs_row[0:1, q4 * 512:(q4 + 1) * 512],
                                        start=True, stop=True)
                                mm.then_inc(bB.h, 1)
                                bB.inc()
                                tensor.wait_ge(bH.h, u + 1)
                                for q4 in range(2):
                                    mm = tensor.matmul(
                                        rot_ps[:, q4 * 512:(q4 + 1) * 512],
                                        rtm[:, :], khat[:, q4 * 512:(q4 + 1) * 512],
                                        start=True, stop=True)
                                mm.then_inc(bO.h, 1)
                                bO.inc()
                                tensor.wait_ge(bN1.h, u + 1)
                                for q4 in range(2):
                                    mm = tensor.matmul(
                                        sums_ps[0:1, q4 * 512:(q4 + 1) * 512],
                                        onesc[:, :], ksq[:, q4 * 512:(q4 + 1) * 512],
                                        start=True, stop=True)
                                mm.then_inc(bNS.h, 1)
                                bNS.inc()

                            @block.tensor
                            def _(tensor):
                                for u, (oc, hh) in enumerate(units):
                                    tensor.wait_ge(bWk[oc % 2].h, 16 * (oc // 2 + 1))
                                    if u >= 2:
                                        tensor.wait_ge(bH.h, u - 1)
                                    po = (u % 2) * 1024
                                    wv_ = wbuf[:, (oc % 2) * E:(oc % 2) * E + E]
                                    for ec in range(NEC):
                                        for q4 in range(2):
                                            mm = tensor.matmul(
                                                kps[:, po + q4 * 512:po + (q4 + 1) * 512],
                                                wv_[:, ec * 128:(ec + 1) * 128],
                                                xn_cols(ec, hh * 1024 + q4 * 512, 512),
                                                start=(ec == 0), stop=(ec == NEC - 1))
                                    mm.then_inc(bM.h, 1)
                                    bM.inc()
                                    if u >= 1:
                                        _kpost(tensor, u - 1)
                                _kpost(tensor, len(units) - 1)


                            @block.vector
                            def _(vector):
                                for u, (oc, hh) in enumerate(units):
                                    po = (u % 2) * 1024
                                    vector.wait_ge(bE.h, u + 1)
                                    vector.scalar_tensor_tensor(
                                        khat[:, :], kps[:, po:po + 1024], knw[:, 0:1],
                                        rsb_sb[:, :],
                                        ALU.mult, ALU.mult).then_inc(bH.h, 1)
                                    bH.inc()
                                    vector.drain()
                                    vector.wait_ge(bQ.h, u + 1)
                                    vector.tensor_tensor(ksq[:, :], khat[:, :], khat[:, :], ALU.mult).then_inc(bN1.h, 1)
                                    bN1.inc()
                                    vector.wait_ge(bO.h, u + 1)
                                    cs = cosf[:, hh * 1024:hh * 1024 + 1024]
                                    sn = sinf[:, hh * 1024:hh * 1024 + 1024]
                                    vector.tensor_tensor(t1[:, :], rot_ps[:, :], sn, ALU.mult)
                                    vector.tensor_tensor(t2[:, :], khat[:, :], cs, ALU.mult)
                                    vector.drain()
                                    dst = kfm[:, oc * S + hh * 1024: oc * S + hh * 1024 + 1024]
                                    vector.tensor_tensor(dst, t1[:, :], t2[:, :], ALU.add).then_inc(bF.h, 1)
                                    bF.inc()
                                    vector.wait_ge(bNS.h, u + 1)
                                    vector.reduce_max(
                                        kvmax2[0:1, u:u + 1], sums_ps[0:1, 0:1024], AX.X
                                    ).then_inc(bNX.h, 1)
                                    bNX.inc()
                                    vector.drain()

                            @block.scalar
                            def _(scalar):
                                for u, (oc, hh) in enumerate(units):
                                    po = (u % 2) * 1024
                                    scalar.wait_ge(bM.h, u + 1)
                                    if u >= 1:
                                        scalar.wait_ge(bNS.h, u)
                                    scalar.activation(ksq[:, :], kps[:, po:po + 1024],
                                                      AF.Square).then_inc(bQ.h, 1)
                                    bQ.inc()
                                    scalar.wait_ge(bS.h, u + 1)
                                    scalar.activation(rs_row[0:1, :], sums_ps[0:1, :], AF.Ln,
                                                      bias=EPS_RMS, scale=1.0 / 128.0)
                                    scalar.drain()
                                    scalar.activation(rs_row[0:1, :], rs_row[0:1, :], AF.Exp,
                                                      scale=-0.5).then_inc(bR.h, 1)
                                    bR.inc()
                                    scalar.wait_ge(bB.h, u + 1)
                                    scalar.copy(rsb_sb[:, :], rot_ps[:, :]).then_inc(bE.h, 1)
                                    bE.inc()

                    rM = sem("rM"); rE = sem("rE"); rQ2 = sem("rQ2")
                    rNS = sem("rNS"); rNX = sem("rNX")
                    if go('p1r'):
                        with nc.Block("p1r") as block:
                            runits = [(src, oc, hh) for src in range(2) for oc in range(3)
                                      for hh in range(2)]


                            @block.gpsimd
                            def _(gpsimd):
                                for i, (src, oc) in enumerate(
                                        [(s_, o_) for s_ in range(2) for o_ in range(3)]):
                                    if i >= 2:
                                        gpsimd.wait_ge(rM.h, 2 * (i - 1))
                                    wsrc = wrq_h if src == 0 else wrk_h
                                    gpsimd.dma_start(
                                        out=wbuf[:, (i % 2) * E:(i % 2) * E + E],
                                        in_=_ap(wsrc, oc * E * 128,
                                                [[128, 128], [128 * 128, NEC], [1, 128]]),
                                    ).then_inc(bWr[i % 2].h, 16)
                                    bWr[i % 2].inc(16)

                            @block.tensor
                            def _(tensor):
                                for u, (src, oc, hh) in enumerate(runits):
                                    i = src * 3 + oc
                                    tensor.wait_ge(bWr[i % 2].h, 16 * (i // 2 + 1))
                                    if u >= 2:
                                        tensor.wait_ge(rE.h, u - 1)
                                        tensor.wait_ge(rQ2.h, u - 1)
                                    po = (u % 2) * 1024
                                    wv_ = wbuf[:, (i % 2) * E:(i % 2) * E + E]
                                    for ec in range(NEC):
                                        for q4 in range(2):
                                            mm = tensor.matmul(
                                                kps[:, po + q4 * 512:po + (q4 + 1) * 512],
                                                wv_[:, ec * 128:(ec + 1) * 128],
                                                xn_cols(ec, hh * 1024 + q4 * 512, 512),
                                                start=(ec == 0), stop=(ec == NEC - 1))
                                    mm.then_inc(rM.h, 1)
                                    rM.inc()
                                    tensor.wait_ge(rQ2.h, u + 1)
                                    if u >= 1:
                                        tensor.wait_ge(rNX.h, u)
                                    for q4 in range(2):
                                        mm = tensor.matmul(
                                            sums_ps[0:4, q4 * 512:(q4 + 1) * 512],
                                            blk4[:, :], ksq[:, q4 * 512:(q4 + 1) * 512],
                                            start=True, stop=True)
                                    mm.then_inc(rNS.h, 1)
                                    rNS.inc()

                            @block.scalar
                            def _(scalar):
                                for u, (src, oc, hh) in enumerate(runits):
                                    po = (u % 2) * 1024
                                    scalar.wait_ge(rE.h, u + 1)
                                    scalar.activation(ksq[:, :], kps[:, po:po + 1024],
                                                      AF.Square).then_inc(rQ2.h, 1)
                                    rQ2.inc()
                                    scalar.drain()

                            @block.vector
                            def _(vector):
                                for u, (src, oc, hh) in enumerate(runits):
                                    po = (u % 2) * 1024
                                    kv_ = kps[:, po:po + 1024]
                                    vector.wait_ge(rM.h, u + 1)
                                    sl = slice(hh * 1024, hh * 1024 + 1024)
                                    if src == 0:
                                        gc = gs_half[:, 0:1] if oc == 0 else gs_cols[:, oc:oc + 1]
                                        vector.tensor_scalar(
                                            rqs[:, oc * S + hh * 1024: oc * S + hh * 1024 + 1024],
                                            kv_, gc, None, ALU.mult)
                                        if oc == 0:
                                            vector.tensor_copy(rqu[:, sl], kv_)
                                    else:
                                        vector.tensor_copy(
                                            rku[:, oc * S + hh * 1024: oc * S + hh * 1024 + 1024],
                                            kv_)
                                        if oc == 0:
                                            vector.tensor_scalar(
                                                rksh[:, sl], kv_, gs_half[:, 0:1],
                                                None, ALU.mult)
                                    vector.engine_nop().then_inc(rE.h, 1)
                                    rE.inc()
                                    vector.wait_ge(rNS.h, u + 1)
                                    dstm = rqmax2 if src == 0 else rkmax2
                                    col = oc * 2 + hh
                                    vector.reduce_max(
                                        dstm[0:4, col:col + 1], sums_ps[0:4, 0:1024], AX.X
                                    ).then_inc(rNX.h, 1)
                                    rNX.inc()
                                    vector.drain()

                    bSp = sem("bSp")
                    with nc.Block("p1s") as block:
                        @block.sync
                        def _(sync):
                            for src_t, dst_hd, n, w in (
                                (kfm, spk_h, NKV, S), (vsb, spv_h, 1, NTT * 512),
                                (rqs, sprqs_h, 3, S), (rku, sprku_h, 3, S),
                                (rqu, sprqu_h, 1, S), (rksh, sprksh_h, 1, S),
                            ):
                                if n == 1:
                                    dst_ap = _ap(dst_hd, 0, [[w, 128], [1, w]])
                                else:
                                    dst_ap = _ap(dst_hd, 0, [[w, 128], [128 * w, n], [1, w]])
                                sync.dma_start(out=dst_ap, in_=src_t[:, :]).then_inc(bSp.h, 16)
                                bSp.inc(16)
                            sync.wait_ge(bSp.h, bSp.v)

            if go("p2"):
                with ExitStack() as SP:
                    sb = lambda n, sh, dt: SP.enter_context(nc.sbuf_tensor(n, sh, dt))
                    cosf = sb("p2_cosf", [128, S], F32)
                    sinf = sb("p2_sinf", [128, S], F32)
                    wbuf = sb("p2_wbuf", [128, 2 * E], BF16)
                    qsq = sb("p2_qsq", [128, 2 * 1024], BF16)
                    lnrow = sb("p2_lnrow", [1, 1024], F32)
                    rs_row = sb("p2_rsrow", [1, 1024], F32)
                    rs2_row = sb("p2_rs2row", [1, 1024], F32)
                    tmp_row = sb("p2_tmprow", [1, 1024], F32)
                    qnw2 = sb("p2_qnw2", [128, 1], BF16)
                    rsb_sb = sb("p2_rsb", [128, 2 * 1024], F32)
                    qhat = sb("p2_qhat", [128, 2 * 1024], BF16)
                    t1 = sb("p2_t1", [128, 2 * 1024], F32)
                    t2 = sb("p2_t2", [128, 2 * 1024], F32)
                    qout = sb("p2_qout", [128, 2 * 1024], BF16)
                    qps = SP.enter_context(nc.psum_tensor("p2_qps", [128, 2 * 1024], F32))
                    sums_ps = SP.enter_context(nc.psum_tensor("p2_sums", [64, 1024], F32))
                    rot_ps = SP.enter_context(nc.psum_tensor("p2_rot", [128, 1024], F32))
                    cL = sem("cL"); cWp = [sem("cWa"), sem("cWb")]; cM = sem("cM"); cQ = sem("cQ")
                    cS = sem("cS"); cR = sem("cR"); cB = sem("cB"); cE = sem("cE")
                    cH = sem("cH"); cO = sem("cO"); cF1 = sem("cF1"); cF = sem("cF")
                    cNX = sem("cNX")
                    cSp = [sem("cSpa"), sem("cSpb")]
                    with nc.Block("p2") as block:
                        qunits = [(h, hh) for h in range(NH) for hh in range(2)]
                        NU = len(qunits)

                        @block.sync
                        def _(sync):
                            sync.dma_start(out=cosf[:, :], in_=cosf_h.ap()).then_inc(cL.h, 16)
                            sync.dma_start(out=sinf[:, :], in_=sinf_h.ap()).then_inc(cL.h, 16)
                            cL.inc(32)
                            for u, (h, hh) in enumerate(qunits):
                                sync.wait_ge(cF.h, u + 1)
                                for k in range(2):
                                    g = hh * 2 + k
                                    sync.dma_start(
                                        out=_ap(spq_h, (g * NH + h) * 128 * 512,
                                                [[512, 128], [1, 512]]),
                                        in_=qout[:, (u % 2) * 1024 + k * 512:
                                                 (u % 2) * 1024 + (k + 1) * 512],
                                    ).then_inc(cSp[u % 2].h, 16)
                                    cSp[u % 2].inc(16)
                            sync.wait_ge(cSp[0].h, cSp[0].v)
                            sync.wait_ge(cSp[1].h, cSp[1].v)

                        @block.gpsimd
                        def _(gpsimd):
                            for h in (0, 1):
                                gpsimd.dma_start(
                                    out=wbuf[:, (h % 2) * E:(h % 2) * E + E],
                                    in_=_ap(wq_h, h * E * 128,
                                            [[128, 128], [128 * 128, NEC], [1, 128]]),
                                ).then_inc(cWp[h % 2].h, 16)
                                cWp[h % 2].inc(16)
                            for u, (h, hh) in enumerate(qunits):
                                if hh == 0 and h + 2 < NH:
                                    h2 = h + 2
                                    gpsimd.wait_ge(cM.h, 2 * (h + 1))
                                    gpsimd.dma_start(
                                        out=wbuf[:, (h2 % 2) * E:(h2 % 2) * E + E],
                                        in_=_ap(wq_h, h2 * E * 128,
                                                [[128, 128], [128 * 128, NEC], [1, 128]]),
                                    ).then_inc(cWp[h2 % 2].h, 16)
                                    cWp[h2 % 2].inc(16)
                                if u >= 2:
                                    gpsimd.wait_ge(cSp[u % 2].h, 32 * (u // 2))
                                gpsimd.wait_ge(cF1.h, u + 1)
                                gpsimd.tensor_tensor(
                                    qout[:, (u % 2) * 1024:(u % 2) * 1024 + 1024],
                                    t1[:, (u % 2) * 1024:(u % 2) * 1024 + 1024],
                                    t2[:, (u % 2) * 1024:(u % 2) * 1024 + 1024],
                                    ALU.add).then_inc(cF.h, 1)
                                cF.inc()

                        def _s1(tensor, v):
                            po = (v % 2) * 1024
                            tensor.wait_ge(cQ.h, v + 1)
                            if v >= 1:
                                tensor.wait_ge(cR.h, v)
                                tensor.wait_ge(cNX.h, 2 * v)
                            for q4 in range(2):
                                tensor.matmul(
                                    sums_ps[0:1, q4 * 512:(q4 + 1) * 512],
                                    onesc[:, :], qsq[:, po + q4 * 512:po + (q4 + 1) * 512],
                                    start=True, stop=True)
                            for q4 in range(2):
                                mm = tensor.matmul(
                                    sums_ps[32:33, q4 * 512:(q4 + 1) * 512],
                                    qnw2[:, 0:1], qsq[:, po + q4 * 512:po + (q4 + 1) * 512],
                                    start=True, stop=True)
                            mm.then_inc(cS.h, 1)
                            cS.inc()

                        def _s2(tensor, v):
                            tensor.wait_ge(cR.h, v + 1)
                            if v >= 1:
                                tensor.wait_ge(cF1.h, v)
                            for q4 in range(2):
                                mm = tensor.matmul(
                                    rot_ps[:, q4 * 512:(q4 + 1) * 512],
                                    onesr[:, :], rs_row[0:1, q4 * 512:(q4 + 1) * 512],
                                    start=True, stop=True)
                            mm.then_inc(cB.h, 1)
                            cB.inc()

                        def _s3(tensor, v):
                            po = (v % 2) * 1024
                            tensor.wait_ge(cH.h, v + 1)
                            tensor.wait_ge(cE.h, v + 1)
                            for q4 in range(2):
                                mm = tensor.matmul(
                                    rot_ps[:, q4 * 512:(q4 + 1) * 512],
                                    rtm[:, :], qhat[:, po + q4 * 512:po + (q4 + 1) * 512],
                                    start=True, stop=True)
                            mm.then_inc(cO.h, 1)
                            cO.inc()

                        @block.tensor
                        def _(tensor):
                            for u, (h, hh) in enumerate(qunits):
                                tensor.wait_ge(cWp[h % 2].h, 16 * (h // 2 + 1))
                                if u >= 2:
                                    tensor.wait_ge(cH.h, u - 1)
                                wv_ = wbuf[:, (h % 2) * E:(h % 2) * E + E]
                                po = (u % 2) * 1024
                                for ec in range(NEC):
                                    for q4 in range(2):
                                        mm = tensor.matmul(
                                            qps[:, po + q4 * 512:po + (q4 + 1) * 512],
                                            wv_[:, ec * 128:(ec + 1) * 128],
                                            xn_cols(ec, hh * 1024 + q4 * 512, 512),
                                            start=(ec == 0), stop=(ec == NEC - 1))
                                    if u >= 1:
                                        if ec == 6:
                                            _s1(tensor, u - 1)
                                        elif ec == 11:
                                            _s2(tensor, u - 1)
                                        elif ec == 15:
                                            _s3(tensor, u - 1)
                                mm.then_inc(cM.h, 1)
                                cM.inc()
                            _s1(tensor, NU - 1)
                            _s2(tensor, NU - 1)
                            _s3(tensor, NU - 1)

                        @block.vector
                        def _(vector):
                            vector.wait_ge(cL.h, 32)
                            for u, (h, hh) in enumerate(qunits):
                                po = (u % 2) * 1024
                                vector.wait_ge(cE.h, u + 1)
                                vector.scalar_tensor_tensor(
                                    qhat[:, po:po + 1024], qps[:, po:po + 1024],
                                    qnw_eff[:, 0:1], rsb_sb[:, po:po + 1024],
                                    ALU.mult, ALU.mult).then_inc(cH.h, 1)
                                cH.inc()
                                vector.wait_ge(cO.h, u + 1)
                                if u >= 2:
                                    vector.wait_ge(cF.h, u - 1)
                                cs = cosf[:, hh * 1024:hh * 1024 + 1024]
                                sn = sinf[:, hh * 1024:hh * 1024 + 1024]
                                vector.drain()
                                vector.tensor_tensor(t1[:, po:po + 1024], rot_ps[:, :],
                                                     sn, ALU.mult)
                                vector.tensor_tensor(t2[:, po:po + 1024],
                                                     qhat[:, po:po + 1024],
                                                     cs, ALU.mult).then_inc(cF1.h, 1)
                                cF1.inc()
                                vector.wait_ge(cS.h, u + 1)
                                vector.wait_ge(cR.h, u + 1)
                                vector.tensor_tensor(tmp_row[0:1, :], sums_ps[32:33, :],
                                                     rs2_row[0:1, :], ALU.mult)
                                vector.drain()
                                for k in range(2):
                                    col = 4 * h + 2 * hh + k
                                    vector.reduce_max(
                                        maxq2[0:1, col:col + 1],
                                        tmp_row[0:1, k * 512:(k + 1) * 512], AX.X
                                    ).then_inc(cNX.h, 1)
                                    cNX.inc()
                                vector.drain()

                        @block.scalar
                        def _(scalar):
                            scalar.activation(qnw2[:, 0:1], qnw_eff[:, 0:1], AF.Square)
                            for u, (h, hh) in enumerate(qunits):
                                po = (u % 2) * 1024
                                scalar.wait_ge(cM.h, u + 1)
                                if u >= 2:
                                    scalar.wait_ge(cS.h, u - 1)
                                scalar.activation(qsq[:, po:po + 1024], qps[:, po:po + 1024],
                                                  AF.Square).then_inc(cQ.h, 1)
                                cQ.inc()
                                scalar.wait_ge(cS.h, u + 1)
                                if u >= 1:
                                    scalar.wait_ge(cB.h, u)
                                scalar.activation(lnrow[0:1, :], sums_ps[0:1, :], AF.Ln,
                                                  bias=EPS_RMS, scale=1.0 / 128.0)
                                scalar.drain()
                                scalar.activation(rs_row[0:1, :], lnrow[0:1, :], AF.Exp,
                                                  scale=-0.5)
                                scalar.activation(rs2_row[0:1, :], lnrow[0:1, :], AF.Exp,
                                                  scale=-1.0).then_inc(cR.h, 1)
                                cR.inc()
                                scalar.wait_ge(cB.h, u + 1)
                                if u >= 2:
                                    scalar.wait_ge(cH.h, u - 1)
                                scalar.copy(rsb_sb[:, po:po + 1024],
                                            rot_ps[:, :]).then_inc(cE.h, 1)
                                cE.inc()

            if go("p3"):
                with ExitStack() as SP:
                    sb = lambda n, sh, dt: SP.enter_context(nc.sbuf_tensor(n, sh, dt))
                    hfm = sb("p3_hfm", [128, NIC * 1024], BF16)
                    gwb = sb("p3_gwb", [128, 2 * E], BF16)
                    uwb = sb("p3_uwb", [128, 2 * E], BF16)
                    dwb = sb("p3_dwb", [128, 2 * 2048], BF16)
                    sg = sb("p3_sg", [128, 2 * 1024], F32)
                    gout = sb("p3_gout", [128, 2 * 1024], BF16)
                    dM = sem("dM"); dSi = sem("dSi"); dH = sem("dH")
                    dDM = sem("dDM"); dGE = sem("dGE")
                    for hh in range(2):
                        dWp = [sem(f"dW{hh}a"), sem(f"dW{hh}b")]
                        dDWp = [sem(f"dDW{hh}a"), sem(f"dDW{hh}b")]
                        dSpp = [sem(f"dSp{hh}a"), sem(f"dSp{hh}b")]
                        with ExitStack() as SS:
                            gps = SS.enter_context(nc.psum_tensor(f"p3_gps{hh}", [128, 2 * 1024], F32))
                            ups = SS.enter_context(nc.psum_tensor(f"p3_ups{hh}", [128, 2 * 1024], F32))
                            with nc.Block(f"p3g{hh}") as block:
                                base_dM = dM.v
                                base_dSi = dSi.v
                                base_dH = dH.v

                                @block.gpsimd
                                def _(gpsimd):
                                    for ic in range(NIC):
                                        if ic >= 2:
                                            gpsimd.wait_ge(dM.h, base_dM + ic - 1)
                                        for wsrc, dstb in ((wg_h, gwb), (wu_h, uwb)):
                                            gpsimd.dma_start(
                                                out=dstb[:, (ic % 2) * E:(ic % 2) * E + E],
                                                in_=_ap(wsrc, ic * E * 128,
                                                        [[128, 128], [128 * 128, NEC], [1, 128]]),
                                            ).then_inc(dWp[ic % 2].h, 16)
                                            dWp[ic % 2].inc(16)

                                @block.tensor
                                def _(tensor):
                                    for ic in range(NIC):
                                        tensor.wait_ge(dWp[ic % 2].h, 32 * (ic // 2 + 1))
                                        if ic >= 2:
                                            tensor.wait_ge(dH.h, base_dH + ic - 1)
                                        po = (ic % 2) * 1024
                                        gw_ = gwb[:, (ic % 2) * E:(ic % 2) * E + E]
                                        uw_ = uwb[:, (ic % 2) * E:(ic % 2) * E + E]
                                        for ec in range(NEC):
                                            for q4 in range(2):
                                                tensor.matmul(
                                                    gps[:, po + q4 * 512:po + (q4 + 1) * 512],
                                                    gw_[:, ec * 128:(ec + 1) * 128],
                                                    xn_cols(ec, hh * 1024 + q4 * 512, 512),
                                                    start=(ec == 0), stop=(ec == NEC - 1))
                                        for ec in range(NEC):
                                            for q4 in range(2):
                                                mm = tensor.matmul(
                                                    ups[:, po + q4 * 512:po + (q4 + 1) * 512],
                                                    uw_[:, ec * 128:(ec + 1) * 128],
                                                    xn_cols(ec, hh * 1024 + q4 * 512, 512),
                                                    start=(ec == 0), stop=(ec == NEC - 1))
                                        mm.then_inc(dM.h, 1)
                                        dM.inc()

                                @block.scalar
                                def _(scalar):
                                    for ic in range(NIC):
                                        scalar.wait_ge(dM.h, base_dM + ic + 1)
                                        if ic >= 2:
                                            scalar.wait_ge(dH.h, base_dH + ic - 1)
                                        po = (ic % 2) * 1024
                                        scalar.activation(
                                            sg[:, po:po + 1024], gps[:, po:po + 1024],
                                            AF.Silu,
                                            scale=gates_c[:, ic:ic + 1]).then_inc(dSi.h, 1)
                                        dSi.inc()

                                @block.vector
                                def _(vector):
                                    for ic in range(NIC):
                                        vector.wait_ge(dSi.h, base_dSi + ic + 1)
                                        po = (ic % 2) * 1024
                                        vector.scalar_tensor_tensor(
                                            hfm[:, ic * 1024:(ic + 1) * 1024],
                                            ups[:, po:po + 1024], ups_c[:, ic:ic + 1],
                                            sg[:, po:po + 1024], ALU.mult, ALU.mult).then_inc(dH.h, 1)
                                        dH.inc()

                        with ExitStack() as SS:
                            dps = SS.enter_context(nc.psum_tensor(f"p3_dps{hh}", [128, 4 * 1024], F32))
                            with nc.Block(f"p3d{hh}") as block:
                                base_dDM = dDM.v
                                base_dGE = dGE.v

                                @block.gpsimd
                                def _(gpsimd):
                                    for og in range(4):
                                        for icb in range(NIC // 4):
                                            nb = og * 11 + icb
                                            if nb >= 2:
                                                gpsimd.wait_ge(dDM.h, base_dDM + 16 * (nb - 1))
                                            gpsimd.dma_start(
                                                out=dwb[:, (nb % 2) * 2048:(nb % 2) * 2048 + 2048],
                                                in_=_ap(wd_h,
                                                        (og * NIC + icb * 4) * 128 * 512,
                                                        [[512, 128], [512 * 128, 4], [1, 512]]),
                                            ).then_inc(dDWp[nb % 2].h, 16)
                                            dDWp[nb % 2].inc(16)

                                @block.tensor
                                def _(tensor):
                                    for og in range(4):
                                        for ic in range(NIC):
                                            nb = og * 11 + ic // 4
                                            tensor.wait_ge(dDWp[nb % 2].h, 16 * (nb // 2 + 1))
                                            if ic == 0 and og >= 1:
                                                tensor.wait_ge(dGE.h, base_dGE + 4 * og)
                                            wv_ = dwb[:, (nb % 2) * 2048 + (ic % 4) * 512:
                                                      (nb % 2) * 2048 + (ic % 4 + 1) * 512]
                                            for oc4 in range(4):
                                                for q4 in range(2):
                                                    mm = tensor.matmul(
                                                        dps[:, oc4 * 1024 + q4 * 512:
                                                            oc4 * 1024 + (q4 + 1) * 512],
                                                        wv_[:, oc4 * 128:(oc4 + 1) * 128],
                                                        hfm[:, ic * 1024 + q4 * 512:
                                                            ic * 1024 + (q4 + 1) * 512],
                                                        start=(ic == 0), stop=(ic == NIC - 1))
                                            mm.then_inc(dDM.h, 4)
                                            dDM.inc(4)

                                @block.scalar
                                def _(scalar):
                                    for og in range(4):
                                        scalar.wait_ge(dDM.h, base_dDM + 4 * NIC * (og + 1))
                                        for oc4 in range(4):
                                            oc = og * 4 + oc4
                                            if oc >= 2:
                                                scalar.wait_ge(dSpp[oc % 2].h, 16 * (oc // 2))
                                            scalar.activation(
                                                gout[:, (oc % 2) * 1024:(oc % 2) * 1024 + 1024],
                                                dps[:, oc4 * 1024:(oc4 + 1) * 1024],
                                                AF.Silu, scale=downs_c[:, oc:oc + 1]
                                            ).then_inc(dGE.h, 1)
                                            dGE.inc()

                                @block.sync
                                def _(sync):
                                    for oc in range(NEC):
                                        sync.wait_ge(dGE.h, base_dGE + oc + 1)
                                        sync.dma_start(
                                            out=_ap(spg_h, oc * 128 * S + hh * 1024,
                                                    [[S, 128], [1, 1024]]),
                                            in_=gout[:, (oc % 2) * 1024:(oc % 2) * 1024 + 1024],
                                        ).then_inc(dSpp[oc % 2].h, 16)
                                        dSpp[oc % 2].inc(16)
                                    sync.wait_ge(dSpp[0].h, dSpp[0].v)
                                    sync.wait_ge(dSpp[1].h, dSpp[1].v)


        if go("p4"):
            with ExitStack() as SP:
                sb = lambda n, sh, dt: SP.enter_context(nc.sbuf_tensor(n, sh, dt))
                kfm = sb("p4_kfm", [128, NKV * S], BF16)
                vsb = sb("p4_vsb", [128, NTT * 512], BF16)
                rqs = sb("p4_rqs", [128, 3 * S], BF16)
                rku = sb("p4_rku", [128, NGH * S], BF16)
                rqu = sb("p4_rqu", [128, 4 * S], BF16)
                rksh = sb("p4_rksh", [128, S], BF16)
                masks = sb("p4_masks", [128, 4 * 512], F32)
                qg = sb("p4_qg", [128, 8 * 512], BF16)
                ptb = sb("p4_ptb", [128, 16 * 512], BF16)
                ogs = sb("p4_ogs", [128, 4 * 512], BF16)
                sums_sb = sb("p4_sums", [1, 2 * 512], F32)
                recip = sb("p4_recip", [1, 2 * 512], F32)
                rb_sb = sb("p4_rbsb", [128, 2 * 512], F32)
                brow = sb("p4_brow", [1, 64], F32)
                btmp = sb("p4_btmp", [1, 256], F32)

                eR = sem("eR"); eP = sem("ePr"); ePs = sem("ePs")
                sc_ps = SP.enter_context(nc.psum_tensor("p4_sc", [128, 4 * 512], F32))
                o_ps = SP.enter_context(nc.psum_tensor("p4_o", [128, 2 * 512], F32))
                sm_ps = SP.enter_context(nc.psum_tensor("p4_sm", [1, 512], F32))
                rb_ps = SP.enter_context(nc.psum_tensor("p4_rb", [128, 512], F32))

                eMz = sem("eMz")
                with nc.Block("p4b") as block:
                    @block.sync
                    def _(sync):
                        sync.dma_start(out=masks[:, :], in_=masks_h.ap()).then_inc(eR.h, 16)
                        eR.inc(16)
                        for dst, src_hd, n, w in (
                            (kfm, spk_h, NKV, S), (vsb, spv_h, 1, NTT * 512),
                            (rqs, sprqs_h, 3, S), (rksh, sprksh_h, 1, S),
                        ):
                            if n == 1:
                                src_ap = _ap(src_hd, 0, [[w, 128], [1, w]])
                            else:
                                src_ap = _ap(src_hd, 0, [[w, 128], [128 * w, n], [1, w]])
                            sync.dma_start(out=dst[:, :], in_=src_ap).then_inc(eR.h, 16)
                            eR.inc(16)
                        sync.wait_ge(eMz.h, 1)
                        for h in range(NGH):
                            oc, l = h // 4, h % 4
                            sync.dma_start(
                                out=rku[32 * l:32 * (l + 1), h * S:(h + 1) * S],
                                in_=_ap(sprku_h, oc * 128 * S + 32 * l * S,
                                        [[S, 32], [1, S]])).then_inc(eR.h, 16)
                            eR.inc(16)
                        for h in range(4):
                            sync.dma_start(
                                out=rqu[32 * h:32 * (h + 1), h * S:(h + 1) * S],
                                in_=_ap(sprqu_h, 32 * h * S,
                                        [[S, 32], [1, S]])).then_inc(eR.h, 16)
                            eR.inc(16)
                        sync.wait_ge(ePs.h, 1)
                        sync.dma_start(out=btmp[0:1, 80:92],
                                       in_=_ap(rqmax2, 0, [[6, 4], [1, 3]])
                                       ).then_inc(eR.h, 16)
                        sync.dma_start(out=btmp[0:1, 96:108],
                                       in_=_ap(rkmax2, 0, [[6, 4], [1, 3]])
                                       ).then_inc(eR.h, 16)
                        eR.inc(32)

                    @block.vector
                    def _(vector):
                        vector.memset(rku[:, :], 0.0)
                        vector.memset(rqu[:, :], 0.0)
                        vector.drain()
                        vector.memset(btmp[0:1, 240:241], 0.0).then_inc(eMz.h, 1)
                        vector.tensor_tensor(
                            _ap(rqmax2, 0, [[6, 4], [2, 3]]),
                            _ap(rqmax2, 0, [[6, 4], [2, 3]]),
                            _ap(rqmax2, 1, [[6, 4], [2, 3]]), ALU.max)
                        vector.tensor_tensor(
                            _ap(rkmax2, 0, [[6, 4], [2, 3]]),
                            _ap(rkmax2, 0, [[6, 4], [2, 3]]),
                            _ap(rkmax2, 1, [[6, 4], [2, 3]]), ALU.max)
                        vector.tensor_tensor(
                            btmp[0:1, 64:68],
                            _ap(kvmax2, 0, [[8, 1], [2, 4]]),
                            _ap(kvmax2, 1, [[8, 1], [2, 4]]), ALU.max).then_inc(ePs.h, 1)
                        ePs.inc()
                        vector.wait_ge(eP.h, 4)
                        vector.tensor_tensor(
                            btmp[0:1, 144:156], btmp[0:1, 112:124], btmp[0:1, 128:140],
                            ALU.mult)
                        vector.drain()
                        vector.tensor_tensor(
                            _ap(btmp, 144, [[256, 1], [3, 4], [1, 3]]),
                            _ap(btmp, 144, [[256, 1], [3, 4], [1, 3]]),
                            _ap(gsrow, 0, [[12, 1], [1, 4], [4, 3]]), ALU.mult)
                        vector.drain()
                        vector.scalar_tensor_tensor(
                            btmp[0:1, 144:156], btmp[0:1, 144:156], -1.0,
                            btmp[0:1, 144:156], ALU.mult, ALU.max)
                        vector.memset(btmp[0:1, 160:176], 0.0)
                        vector.drain()
                        vector.tensor_copy(
                            _ap(btmp, 160, [[256, 1], [4, 3], [1, 4]]),
                            _ap(btmp, 144, [[256, 1], [1, 3], [3, 4]]))
                        vector.tensor_tensor(
                            _ap(brow, 0, [[64, 1], [16, 4], [1, 16]]),
                            _ap(btmp, 0, [[256, 1], [16, 4], [1, 16]]),
                            _ap(btmp, 70, [[256, 1], [1, 4], [0, 16]]), ALU.mult)
                        vector.drain()
                        vector.scalar_tensor_tensor(
                            _ap(brow, 0, [[64, 1], [4, 16], [1, 4]]),
                            _ap(brow, 0, [[64, 1], [4, 16], [1, 4]]), -1.0,
                            _ap(btmp, 160, [[256, 1], [1, 16], [0, 4]]),
                            ALU.mult, ALU.subtract).then_inc(ePs.h, 1)
                        ePs.inc()

                    @block.scalar
                    def _(scalar):
                        scalar.wait_ge(eR.h, eR.v)
                        for dst0, dstn, src in (
                            (0, 64, maxq2[0:1, :]), (70, 74, btmp[0:1, 64:68]),
                            (112, 124, btmp[0:1, 80:92]), (128, 140, btmp[0:1, 96:108]),
                        ):
                            scalar.activation(btmp[0:1, dst0:dstn], src, AF.Ln, bias=1e-30)
                            scalar.drain()
                            scalar.activation(btmp[0:1, dst0:dstn], btmp[0:1, dst0:dstn],
                                              AF.Exp, scale=0.5, bias=BMARGIN).then_inc(eP.h, 1)
                            eP.inc()

                    @block.tensor
                    def _(tensor):
                        tensor.wait_ge(ePs.h, 2)
                        tensor.matmul(rb_ps[:, 0:64], onesr[:, :], brow[0:1, :],
                                      start=True, stop=True).then_inc(eP.h, 1)
                        eP.inc()

                with nc.Block("p4bb") as block:
                    @block.scalar
                    def _(scalar):
                        scalar.copy(bneg_b[:, :], rb_ps[:, 0:64])

                if tb_h is not None:
                    tB = sem("tB")
                    with nc.Block("p4tap") as block:
                        @block.sync
                        def _(sync):
                            sync.dma_start(out=tb_h.ap(), in_=bneg_b[:, :]).then_inc(tB.h, 16)
                            sync.wait_ge(tB.h, tB.inc(16))

                eC = sem("eC"); eEv = sem("eEv"); eEx = sem("eEx"); eAV = sem("eAV")
                eSm = sem("eSm"); eB2 = sem("eB2"); eSv = sem("eSv"); eNo = sem("eNo")
                eQl = sem("eQl"); eSp4 = [sem(f"eSp{i}") for i in range(4)]
                scr = sb("p4_scr", [128, 4 * 1024], F32)
                for g in range(NG):
                    nch = 4 * g + 4
                    with nc.Block(f"p4g{g}") as block:
                        base_eC = eC.v
                        base_eQl = eQl.v
                        base_eSp4 = [x.v for x in eSp4]
                        base_eEv = eEv.v
                        base_eEx = eEx.v
                        base_eAV = eAV.v
                        base_eSm = eSm.v
                        base_eB2 = eB2.v
                        base_eSv = eSv.v
                        base_eNo = eNo.v

                        @block.sync
                        def _(sync):
                            sync.dma_start(
                                out=qg[:, :],
                                in_=_ap(spq_h, g * NH * 128 * 512,
                                        [[512, 128], [128 * 512, 8], [1, 512]]),
                            ).then_inc(eQl.h, 16)
                            eQl.inc(16)
                            for h in range(NH):
                                if h == 7:
                                    sync.wait_ge(eAV.h, base_eAV + 8)
                                    sync.dma_start(
                                        out=qg[:, :],
                                        in_=_ap(spq_h, (g * NH + 8) * 128 * 512,
                                                [[512, 128], [128 * 512, 8], [1, 512]]),
                                    ).then_inc(eQl.h, 16)
                                    eQl.inc(16)
                                sync.wait_ge(eNo.h, base_eNo + h + 1)
                                sync.dma_start(
                                    out=_ap(spo_h, h * 128 * S + g * 512,
                                            [[S, 128], [1, 512]]),
                                    in_=ogs[:, (h % 4) * 512:(h % 4 + 1) * 512],
                                ).then_inc(eSp4[h % 4].h, 16)
                                eSp4[h % 4].inc(16)
                            for x in eSp4:
                                sync.wait_ge(x.h, x.v)

                        @block.tensor
                        def _(tensor):
                            def bcast(tensor, hb):
                                tensor.wait_ge(eSm.h, base_eSm + hb + 1)
                                mm = tensor.matmul(
                                    rb_ps[:, 0:512], onesr[:, :],
                                    recip[0:1, (hb % 2) * 512:(hb % 2) * 512 + 512],
                                    start=True, stop=True)
                                mm.then_inc(eB2.h, 1)
                                eB2.inc()

                            ich = 0
                            for h in range(NH):
                                kv = h // 4
                                if h % 8 == 0:
                                    tensor.wait_ge(eQl.h, base_eQl + 16 * (h // 8 + 1))
                                if h >= 2:
                                    tensor.wait_ge(eNo.h, base_eNo + h - 1)
                                hbase = ich
                                pbase = ich // 2
                                for j in range(nch):
                                    pr = ich // 2
                                    if pr >= 2 and ich % 2 == 0:
                                        tensor.wait_ge(eEv.h, base_eEv + pr - 1)
                                    po = (pr % 2) * 1024 + (ich % 2) * 512
                                    last = tensor.matmul(
                                        sc_ps[:, po:po + 512],
                                        kfm[:, kv * S + j * 128: kv * S + (j + 1) * 128],
                                        qg[:, (h % 8) * 512:(h % 8 + 1) * 512],
                                        start=True, stop=(h >= NGH))
                                    if h < NGH:
                                        oc = h // 4
                                        last = tensor.matmul(
                                            sc_ps[:, po:po + 512],
                                            rku[:, h * S + j * 128: h * S + (j + 1) * 128],
                                            rqs[:, oc * S + g * 512: oc * S + (g + 1) * 512],
                                            start=False, stop=(h >= 4))
                                    if h < 4:
                                        last = tensor.matmul(
                                            sc_ps[:, po:po + 512],
                                            rqu[:, h * S + j * 128: h * S + (j + 1) * 128],
                                            rksh[:, g * 512:(g + 1) * 512],
                                            start=False, stop=True)
                                    last.then_inc(eC.h, 1)
                                    eC.inc()
                                    ich += 1
                                    if j >= 6:
                                        jj = j - 6
                                        tensor.wait_ge(eEx.h, base_eEx + pbase + jj // 2 + 1)
                                        tensor.matmul(
                                            o_ps[:, (h % 2) * 512:(h % 2) * 512 + 512],
                                            vsb[:, jj * 512 + kv * 128: jj * 512 + (kv + 1) * 128],
                                            ptb[:, jj * 512:(jj + 1) * 512],
                                            start=(jj == 0), stop=False)
                                        if jj % 4 == 3:
                                            if jj == 3 and h >= 1:
                                                tensor.wait_ge(eSv.h, base_eSv + h)
                                            for js in range(jj - 3, jj + 1):
                                                tensor.matmul(
                                                    sm_ps[0:1, 0:512], onesc[:, :],
                                                    ptb[:, js * 512:(js + 1) * 512],
                                                    start=(js == 0), stop=False)
                                for jj in range(max(0, nch - 6), nch):
                                    tensor.wait_ge(eEx.h, base_eEx + pbase + jj // 2 + 1)
                                    tensor.matmul(
                                        o_ps[:, (h % 2) * 512:(h % 2) * 512 + 512],
                                        vsb[:, jj * 512 + kv * 128: jj * 512 + (kv + 1) * 128],
                                        ptb[:, jj * 512:(jj + 1) * 512],
                                        start=(jj == 0), stop=(jj == nch - 1))
                                    if jj % 4 == 3:
                                        if jj == 3 and h >= 1:
                                            tensor.wait_ge(eSv.h, base_eSv + h)
                                        for js in range(jj - 3, jj + 1):
                                            mm2 = tensor.matmul(
                                                sm_ps[0:1, 0:512], onesc[:, :],
                                                ptb[:, js * 512:(js + 1) * 512],
                                                start=(js == 0), stop=(js == nch - 1))
                                mm2.then_inc(eAV.h, 1)
                                eAV.inc()
                                if h >= 1:
                                    bcast(tensor, h - 1)
                            bcast(tensor, NH - 1)

                        @block.vector
                        def _(vector):
                            def sums_evac(vector, hb):
                                vector.wait_ge(eAV.h, base_eAV + hb + 1)
                                vector.tensor_copy(
                                    sums_sb[0:1, (hb % 2) * 512:(hb % 2) * 512 + 512],
                                    sm_ps[0:1, 0:512]).then_inc(eSv.h, 1)
                                eSv.inc()

                            def tail(vector, hb):
                                vector.wait_ge(eB2.h, base_eB2 + hb + 1)
                                if hb >= 4:
                                    vector.wait_ge(eSp4[hb % 4].h,
                                                   base_eSp4[hb % 4] + 16 * ((hb - 4) // 4 + 1))
                                vector.tensor_copy(
                                    rb_sb[:, (hb % 2) * 512:(hb % 2) * 512 + 512],
                                    rb_ps[:, 0:512])
                                vector.drain()
                                vector.tensor_tensor(
                                    ogs[:, (hb % 4) * 512:(hb % 4 + 1) * 512],
                                    o_ps[:, (hb % 2) * 512:(hb % 2) * 512 + 512],
                                    rb_sb[:, (hb % 2) * 512:(hb % 2) * 512 + 512],
                                    ALU.mult).then_inc(eNo.h, 1)
                                eNo.inc()

                            ich = 0
                            for h in range(NH):
                                if h >= 1:
                                    sums_evac(vector, h - 1)
                                for j in range(0, nch, 2):
                                    pr = ich // 2
                                    if pr >= 4:
                                        vector.wait_ge(eEx.h, base_eEx + pr - 3)
                                    po = (pr % 2) * 1024
                                    so = (pr % 4) * 1024
                                    if j + 1 < 4 * g:
                                        vector.wait_ge(eC.h, base_eC + ich + 2)
                                        vector.tensor_copy(
                                            scr[:, so:so + 1024], sc_ps[:, po:po + 1024]
                                        ).then_inc(eEv.h, 1)
                                        eEv.inc()
                                        ich += 2
                                        continue
                                    for jc in (j, j + 1):
                                        vector.wait_ge(eC.h, base_eC + ich + 1)
                                        poc = po + (ich % 2) * 512
                                        soc = so + (ich % 2) * 512
                                        if jc >= 4 * g:
                                            k = jc - 4 * g
                                            op = vector.tensor_tensor(
                                                scr[:, soc:soc + 512], sc_ps[:, poc:poc + 512],
                                                masks[:, k * 512:(k + 1) * 512], ALU.add)
                                        else:
                                            op = vector.tensor_copy(
                                                scr[:, soc:soc + 512], sc_ps[:, poc:poc + 512])
                                        if ich % 2 == 1:
                                            op.then_inc(eEv.h, 1)
                                            eEv.inc()
                                        ich += 1
                                if h >= 1:
                                    tail(vector, h - 1)
                            sums_evac(vector, NH - 1)
                            tail(vector, NH - 1)

                        @block.scalar
                        def _(scalar):
                            def recip_chain(scalar, hb):
                                if hb >= 2:
                                    scalar.wait_ge(eB2.h, base_eB2 + hb - 1)
                                scalar.wait_ge(eSv.h, base_eSv + hb + 1)
                                so = (hb % 2) * 512
                                scalar.activation(recip[0:1, so:so + 512],
                                                  sums_sb[0:1, so:so + 512], AF.Ln)
                                scalar.drain()
                                scalar.activation(recip[0:1, so:so + 512],
                                                  recip[0:1, so:so + 512], AF.Exp,
                                                  scale=-1.0).then_inc(eSm.h, 1)
                                eSm.inc()

                            ich = 0
                            for h in range(NH):
                                for p in range(nch // 2):
                                    pr = ich // 2
                                    scalar.wait_ge(eEv.h, base_eEv + pr + 1)
                                    j0 = 2 * p
                                    col = 4 * h + g
                                    scalar.activation(
                                        ptb[:, j0 * 512:(j0 + 2) * 512],
                                        scr[:, (pr % 4) * 1024:(pr % 4) * 1024 + 1024],
                                        AF.Exp,
                                        bias=bneg_b[:, col:col + 1]).then_inc(eEx.h, 1)
                                    eEx.inc()
                                    ich += 2
                                if h >= 1:
                                    recip_chain(scalar, h - 1)
                            recip_chain(scalar, NH - 1)

        if go("p5"):
            with ExitStack() as S5:
                sfm = S5.enter_context(nc.sbuf_tensor("p5_sfm", [128, NEC * S], BF16))
                S5a = S5.enter_context(ExitStack())
                sb = lambda n, sh, dt: S5a.enter_context(nc.sbuf_tensor(n, sh, dt))
                osb = sb("p5_osb", [128, NH * S], BF16)
                wob = sb("p5_wob", [128, 2 * E], BF16)
                sps = S5a.enter_context(nc.psum_tensor("p5_sps", [128, 2 * 2048], F32))
                fR = sem("fR"); fWp = [sem("fWa"), sem("fWb")]; fM = sem("fM"); fE = sem("fE")
                with nc.Block("p5") as block:
                    @block.sync
                    def _(sync):
                        sync.dma_start(
                            out=osb[:, :],
                            in_=_ap(spo_h, 0, [[S, 128], [128 * S, NH], [1, S]]),
                        ).then_inc(fR.h, 16)
                        fR.inc(16)

                    @block.gpsimd
                    def _(gpsimd):
                        for oc in range(NEC):
                            if oc >= 2:
                                gpsimd.wait_ge(fM.h, oc - 1)
                            gpsimd.dma_start(
                                out=wob[:, (oc % 2) * E:(oc % 2) * E + E],
                                in_=_ap(wo_h, oc * E * 128,
                                        [[128, 128], [128 * 128, NEC], [1, 128]]),
                            ).then_inc(fWp[oc % 2].h, 16)
                            fWp[oc % 2].inc(16)

                    @block.tensor
                    def _(tensor):
                        tensor.wait_ge(fR.h, 16)
                        for oc in range(NEC):
                            tensor.wait_ge(fWp[oc % 2].h, 16 * (oc // 2 + 1))
                            if oc >= 2:
                                tensor.wait_ge(fE.h, oc - 1)
                            po = (oc % 2) * 2048
                            wv_ = wob[:, (oc % 2) * E:(oc % 2) * E + E]
                            for ec in range(NEC):
                                for q4 in range(4):
                                    mm = tensor.matmul(
                                        sps[:, po + q4 * 512:po + (q4 + 1) * 512],
                                        wv_[:, ec * 128:(ec + 1) * 128],
                                        osb[:, ec * S + q4 * 512: ec * S + (q4 + 1) * 512],
                                        start=(ec == 0), stop=(ec == NEC - 1))
                            mm.then_inc(fM.h, 1)
                            fM.inc()

                    @block.scalar
                    def _(scalar):
                        for oc in range(NEC):
                            scalar.wait_ge(fM.h, oc + 1)
                            po = (oc % 2) * 2048
                            scalar.copy(sfm[:, oc * S:(oc + 1) * S], sps[:, po:po + 2048]
                                        ).then_inc(fE.h, 1)
                            fE.inc()

                S5a.close()

                if ts_h is not None:
                    tS2 = sem("tS2")
                    with nc.Block("p5tap") as block:
                        @block.sync
                        def _(sync):
                            sync.dma_start(out=ts_h.ap(), in_=sfm[:, :]).then_inc(tS2.h, 16)
                            sync.wait_ge(tS2.h, tS2.inc(16))

                if go("p6"):
                    with ExitStack() as S6:
                        sb6 = lambda n, sh, dt: S6.enter_context(nc.sbuf_tensor(n, sh, dt))
                        gsb = sb6("p6_gsb", [128, NEC * S], BF16)
                        xt = sb6("p6_xt", [128, 2 * E], F32)
                        ssb = sb6("p6_ssb", [128, 2 * E], BF16)
                        psb = sb6("p6_psb", [128, E], F32)
                        outb = sb6("p6_outb", [128, 2 * E], F32)
                        s_ps = S6.enter_context(nc.psum_tensor("p6_sps", [128, 2 * E], BF16))
                        g_ps = S6.enter_context(nc.psum_tensor("p6_gps", [128, 2 * E], BF16))
                        gR = sem("gR"); gXp = [sem("gXa"), sem("gXb")]
                        gT = sem("gT"); gE = sem("gE")
                        gV = sem("gV"); gStp = [sem("gSta"), sem("gStb")]
                        with nc.Block("p6") as block:
                            @block.sync
                            def _(sync):
                                sync.dma_start(
                                    out=gsb[:, :],
                                    in_=_ap(spg_h, 0, [[S, 128], [128 * S, NEC], [1, S]]),
                                ).then_inc(gR.h, 16)
                                gR.inc(16)
                                for tt in range(NTT):
                                    if tt >= 2:
                                        sync.wait_ge(gV.h, 2 * (tt - 1))
                                    sync.dma_start(
                                        out=xt[:, (tt % 2) * E:(tt % 2) * E + E],
                                        in_=_ap(x_h, tt * 128 * E, [[E, 128], [1, E]]),
                                    ).then_inc(gXp[tt % 2].h, 16)
                                    gXp[tt % 2].inc(16)
                                    if tt >= 2:
                                        sync.wait_ge(gV.h, 2 * (tt - 1))
                                        sync.dma_start(
                                            out=_ap(y_h, (tt - 2) * 128 * E, [[E, 128], [1, E]]),
                                            in_=outb[:, (tt % 2) * E:(tt % 2) * E + E],
                                        ).then_inc(gStp[tt % 2].h, 16)
                                        gStp[tt % 2].inc(16)
                                for tt in (NTT - 2, NTT - 1):
                                    sync.wait_ge(gV.h, 2 * (tt + 1))
                                    sync.dma_start(
                                        out=_ap(y_h, tt * 128 * E, [[E, 128], [1, E]]),
                                        in_=outb[:, (tt % 2) * E:(tt % 2) * E + E],
                                    ).then_inc(gStp[tt % 2].h, 16)
                                    gStp[tt % 2].inc(16)
                                sync.wait_ge(gStp[0].h, gStp[0].v)
                                sync.wait_ge(gStp[1].h, gStp[1].v)

                            @block.tensor
                            def _(tensor):
                                tensor.wait_ge(gR.h, 16)
                                for tt in range(NTT):
                                    if tt >= 2:
                                        tensor.wait_ge(gV.h, 2 * (tt - 1))
                                    po = (tt % 2) * E
                                    for oc in range(NEC):
                                        tensor.transpose(
                                            s_ps[:, po + oc * 128:po + (oc + 1) * 128],
                                            sfm[:, oc * S + tt * 128: oc * S + (tt + 1) * 128],
                                            ident[:, :])
                                    for oc in range(NEC):
                                        mm = tensor.transpose(
                                            g_ps[:, po + oc * 128:po + (oc + 1) * 128],
                                            gsb[:, oc * S + tt * 128: oc * S + (tt + 1) * 128],
                                            ident[:, :])
                                    mm.then_inc(gT.h, 1)
                                    gT.inc()

                            @block.scalar
                            def _(scalar):
                                for tt in range(NTT):
                                    scalar.wait_ge(gT.h, tt + 1)
                                    if tt >= 2:
                                        scalar.wait_ge(gV.h, 2 * (tt - 2) + 1)
                                    po = (tt % 2) * E
                                    scalar.copy(ssb[:, po:po + E], s_ps[:, po:po + E]
                                                ).then_inc(gE.h, 1)
                                    gE.inc()

                            @block.vector
                            def _(vector):
                                for tt in range(NTT):
                                    po = (tt % 2) * E
                                    vector.wait_ge(gE.h, tt + 1)
                                    vector.drain()
                                    vector.tensor_tensor(psb[:, :], g_ps[:, po:po + E],
                                                         ssb[:, po:po + E], ALU.mult).then_inc(gV.h, 1)
                                    gV.inc()
                                    vector.wait_ge(gXp[tt % 2].h, 16 * (tt // 2 + 1))
                                    if tt >= 2:
                                        vector.wait_ge(gStp[tt % 2].h, 16 * (tt // 2))
                                    vector.drain()
                                    vector.tensor_tensor(outb[:, po:po + E], psb[:, :],
                                                         xt[:, po:po + E], ALU.add).then_inc(gV.h, 1)
                                    gV.inc()

    return nc


def _prep_host(inputs):
    f32 = np.float32
    g = {}
    Wq = np.asarray(inputs["L0_attn_Wq"], f32)
    Wk = np.asarray(inputs["L0_attn_Wk"], f32)
    Wv = np.asarray(inputs["L0_attn_Wv"], f32)
    Wo = np.asarray(inputs["L0_attn_Wo"], f32)
    Wrq = np.asarray(inputs["L0_attn_Wrq"], f32)
    Wrk = np.asarray(inputs["L0_attn_Wrk"], f32)
    gw = np.asarray(inputs["L0_ffn_gate_w"], f32)
    uw = np.asarray(inputs["L0_ffn_up_w"], f32)
    dw = np.asarray(inputs["L0_ffn_down_w"], f32)

    stripes = lambda WT, n: np.ascontiguousarray(
        WT.reshape(WT.shape[0], n, 128).transpose(1, 0, 2))
    g["wq"] = stripes(Wq.T, NH)
    g["wk"] = stripes(Wk.T, NKV)
    g["wv"] = np.ascontiguousarray(Wv.T)
    g["wo"] = stripes(Wo.T, NEC)
    g["wrq"] = stripes(Wrq.T, 3)
    g["wrk"] = stripes(Wrk.T, 3)
    g["wg"] = stripes(gw.T, NIC)
    g["wu"] = stripes(uw.T, NIC)
    g["wd"] = np.ascontiguousarray(
        dw.T.reshape(NIC, 128, 4, 512).transpose(2, 0, 1, 3))
    g["qnw"] = np.ascontiguousarray(
        np.asarray(inputs["L0_attn_q_norm_w"], f32).reshape(128, 1))
    g["knw"] = np.ascontiguousarray(
        np.asarray(inputs["L0_attn_k_norm_w"], f32).reshape(128, 1))
    gs = np.asarray(inputs["L0_attn_graph_scale"], f32)
    g["gsrep"] = np.ascontiguousarray(np.repeat(gs, 32).reshape(3, 128))
    g["gsrow"] = np.ascontiguousarray(gs.reshape(1, 12))
    g["gates_c"] = np.ascontiguousarray(
        np.asarray(inputs["L0_ffn_gate_s"], f32).reshape(NIC, 128).T)
    g["ups_c"] = np.ascontiguousarray(
        np.asarray(inputs["L0_ffn_up_s"], f32).reshape(NIC, 128).T)
    g["downs_c"] = np.ascontiguousarray(
        np.asarray(inputs["L0_ffn_down_s"], f32).reshape(NEC, 128).T)
    g["ident"] = np.eye(128, dtype=ml_dtypes.bfloat16)
    rt = np.zeros((128, 128), dtype=ml_dtypes.bfloat16)
    for d in range(64):
        rt[d + 64, d] = -1.0
        rt[d, d + 64] = 1.0
    g["rtm"] = rt
    t = np.arange(S, dtype=np.float64)
    invf = 1.0 / (10000.0 ** (np.arange(0, 128, 2, dtype=np.float64) / 128.0))
    fr = np.outer(t, invf)
    emb = np.concatenate([fr, fr], axis=-1)
    g["cosf"] = np.ascontiguousarray(np.cos(emb).T.astype(f32))
    g["sinf"] = np.ascontiguousarray(np.sin(emb).T.astype(f32))
    m = np.zeros((128, 4 * 512), f32)
    p = np.arange(128)[:, None]
    f = np.arange(512)[None, :]
    for k in range(4):
        m[:, k * 512:(k + 1) * 512] = np.where(128 * k + p > f, NEG_BIG, 0.0)
    g["masks"] = m
    g["ones_col"] = np.ones((128, 1), dtype=ml_dtypes.bfloat16)
    g["ones_rowf"] = np.ones((1, 128), f32)
    b4 = np.zeros((128, 4), dtype=ml_dtypes.bfloat16)
    for l in range(4):
        b4[32 * l:32 * (l + 1), l] = 1.0
    g["blk4"] = b4
    return g


def kernel(**inputs):
    shared = _prep_host(inputs)
    x = np.asarray(inputs["L0_attn_x"], np.float32)
    nc = build_program()
    in_maps = []
    for c in range(8):
        m = dict(shared)
        m["x"] = np.ascontiguousarray(x[c])
        in_maps.append(m)
    res = run_bass_kernel_spmd(nc, in_maps, core_ids=list(range(8)))
    return np.stack([res.results[c]["y"] for c in range(8)], axis=0)


# version 14
# speedup vs baseline: 1.1027x; 1.1027x over previous
import os
import numpy as np
import ml_dtypes
from contextlib import ExitStack

import concourse.bass as bass
import concourse.mybir as mybir
from concourse.bass_utils import run_bass_kernel_spmd

F32 = mybir.dt.float32
BF16 = mybir.dt.bfloat16
AF = mybir.ActivationFunctionType
ALU = mybir.AluOpType
AX = mybir.AxisListType

S = 2048
E = 2048
NH = 16
NKV = 4
NGH = 12
RR = 32
INTER = 5632
NIC = INTER // 128
NEC = E // 128
NTT = S // 128
NG = 4
EPS_LN = 1e-5
EPS_RMS = 1e-6
INV_SQRT_D = float(1.0 / np.sqrt(128.0))
NEG_BIG = -1.0e38
BMARGIN = 0.006


class Sem:
    def __init__(self, ctx, nc, name):
        self.h = ctx.enter_context(nc.semaphore(name))
        self.v = 0

    def inc(self, n=1):
        self.v += n
        return self.v


def _ap(h, off, dims):
    return bass.AP(h, off, dims)


def build_program(taps=(), stop_after=None):
    PH = ["p0", "p1v", "p1k", "p1r", "p1", "p2", "p3", "p4", "p5", "p6"]
    go = lambda p: stop_after is None or PH.index(p) <= PH.index(stop_after)
    taps = set(taps)
    nc = bass.Bass("TRN2", target_bir_lowering=False)

    def dram(name, shape, dt, kind):
        return nc.dram_tensor(name, shape, dt, kind=kind)

    x_h = dram("x", [S, E], F32, "ExternalInput")
    wq_h = dram("wq", [NH, E, 128], F32, "ExternalInput")
    wk_h = dram("wk", [NKV, E, 128], F32, "ExternalInput")
    wv_h = dram("wv", [E, NKV * 128], F32, "ExternalInput")
    wo_h = dram("wo", [NEC, E, 128], F32, "ExternalInput")
    wrq_h = dram("wrq", [3, E, 128], F32, "ExternalInput")
    wrk_h = dram("wrk", [3, E, 128], F32, "ExternalInput")
    wg_h = dram("wg", [NIC, E, 128], F32, "ExternalInput")
    wu_h = dram("wu", [NIC, E, 128], F32, "ExternalInput")
    wd_h = dram("wd", [4, NIC, 128, 512], F32, "ExternalInput")
    qnw_h = dram("qnw", [128, 1], F32, "ExternalInput")
    knw_h = dram("knw", [128, 1], F32, "ExternalInput")
    gsrep_h = dram("gsrep", [3, 128], F32, "ExternalInput")
    gsrow_h = dram("gsrow", [1, 12], F32, "ExternalInput")
    gates_h = dram("gates_c", [128, NIC], F32, "ExternalInput")
    ups_h = dram("ups_c", [128, NIC], F32, "ExternalInput")
    downs_h = dram("downs_c", [128, NEC], F32, "ExternalInput")
    ident_h = dram("ident", [128, 128], BF16, "ExternalInput")
    rtm_h = dram("rtm", [128, 128], BF16, "ExternalInput")
    cosf_h = dram("cosf", [128, S], F32, "ExternalInput")
    sinf_h = dram("sinf", [128, S], F32, "ExternalInput")
    masks_h = dram("masks", [128, 4 * 512], F32, "ExternalInput")
    onesc_h = dram("ones_col", [128, 1], BF16, "ExternalInput")
    onesr_h = dram("ones_rowf", [1, 128], F32, "ExternalInput")
    blk4_h = dram("blk4", [128, 4], BF16, "ExternalInput")

    y_h = dram("y", [S, E], F32, "ExternalOutput")

    def spill(name, shape, tapname):
        kind = "ExternalOutput" if tapname in taps else "Internal"
        return dram(name, shape, BF16, kind)

    spk_h = spill("sp_k", [NKV, 128, S], "k")
    spv_h = spill("sp_v", [128, NTT * 512], "v")
    sprku_h = spill("sp_rku", [3, 128, S], "r")
    sprqs_h = spill("sp_rqs", [3, 128, S], "r")
    sprqu_h = spill("sp_rqu", [128, S], "r")
    sprksh_h = spill("sp_rksh", [128, S], "r")
    spq_h = spill("sp_q", [NG, NH, 128, 512], "q")
    spg_h = spill("sp_g", [NEC, 128, S], "g")
    spo_h = spill("sp_o", [NH, 128, S], "o")
    txn_h = dram("t_xn", [128, NEC * S], BF16, "ExternalOutput") if "xn" in taps else None
    ts_h = dram("t_s", [128, NEC * S], BF16, "ExternalOutput") if "s" in taps else None
    tb_h = dram("t_b", [128, 64], F32, "ExternalOutput") if "b" in taps else None

    with ExitStack() as L0:
        sbuf = lambda n, sh, dt: L0.enter_context(nc.sbuf_tensor(n, sh, dt))
        sem = lambda n: Sem(L0, nc, n)

        ident = sbuf("c_ident", [128, 128], BF16)
        rtm = sbuf("c_rtm", [128, 128], BF16)
        onesc = sbuf("c_onesc", [128, 1], BF16)
        onesr = sbuf("c_onesr", [1, 128], F32)
        blk4 = sbuf("c_blk4", [128, 4], BF16)
        qnw = sbuf("c_qnw", [128, 1], F32)
        knw = sbuf("c_knw", [128, 1], F32)
        qnw_eff = sbuf("c_qnweff", [128, 1], F32)
        gs_cols = sbuf("c_gscols", [128, 3], F32)
        gs_half = sbuf("c_gshalf", [128, 1], F32)
        gsrow = sbuf("c_gsrow", [1, 12], F32)
        gates_c = sbuf("c_gates", [128, NIC], F32)
        ups_c = sbuf("c_ups", [128, NIC], F32)
        downs_c = sbuf("c_downs", [128, NEC], F32)
        actcst = sbuf("c_actcst", [128, 8], F32)
        maxq2 = sbuf("st_maxq2", [1, 64], F32)
        kvmax2 = sbuf("st_kvmax2", [1, 8], F32)
        rqmax2 = sbuf("st_rqmax2", [4, 6], F32)
        rkmax2 = sbuf("st_rkmax2", [4, 6], F32)
        bneg_b = sbuf("st_bnegb", [128, 64], F32)

        iL = sem("iL")
        with nc.Block("init") as block:
            @block.sync
            def _(sync):
                for dst, src in (
                    (ident, ident_h), (rtm, rtm_h), (onesc, onesc_h),
                    (onesr, onesr_h), (blk4, blk4_h), (qnw, qnw_h),
                    (knw, knw_h), (gsrow, gsrow_h), (gates_c, gates_h),
                    (ups_c, ups_h), (downs_c, downs_h),
                ):
                    sync.dma_start(out=dst[:, :], in_=src.ap()).then_inc(iL.h, 16)
                    iL.inc(16)
                sync.dma_start(
                    out=gs_cols[:, :], in_=_ap(gsrep_h, 0, [[1, 128], [128, 3]]),
                    allow_slow_non_contiguous=True,
                ).then_inc(iL.h, 16)
                iL.inc(16)

            @block.vector
            def _(vector):
                for i, v in enumerate(
                    (EPS_LN, EPS_RMS, 1e-30, -0.5, 0.5, -1.0, BMARGIN, 1.0 / 128.0)
                ):
                    vector.memset(actcst[:, i:i + 1], v)
                    nc.const_aps.register(F32, v, actcst[:, i:i + 1])
                vector.wait_ge(iL.h, iL.v)
                vector.tensor_scalar(qnw_eff[:, :], qnw[:, :], INV_SQRT_D, None, ALU.mult)
                vector.tensor_scalar(gs_half[:, :], gs_cols[:, 0:1], 0.5, None, ALU.mult)

        with ExitStack() as SA:
            xn = SA.enter_context(nc.sbuf_tensor("xn_fm", [128, NEC * S], BF16))

            def xn_cols(ec, c0, n):
                return xn[:, ec * S + c0: ec * S + c0 + n]

            with ExitStack() as SP:
                sb = lambda n, sh, dt: SP.enter_context(nc.sbuf_tensor(n, sh, dt))
                xt = sb("p0_xt", [128, 2 * E], F32)
                xnt = sb("p0_xnt", [128, E], BF16)
                xsq = sb("p0_xsq", [128, E], F32)
                st0 = sb("p0_st", [128, 16], F32)
                tp_ps = SP.enter_context(nc.psum_tensor("p0_tp", [128, E], BF16))
                aLp = [sem("aLa"), sem("aLb")]; aS = sem("aS"); aR = sem("aR")
                aX = sem("aX"); aT = sem("aT"); aE = sem("aE")
                with nc.Block("p0") as block:
                    @block.sync
                    def _(sync):
                        for tt in range(NTT):
                            if tt >= 2:
                                sync.wait_ge(aX.h, tt - 1)
                            sync.dma_start(
                                out=xt[:, (tt % 2) * E:(tt % 2) * E + E],
                                in_=_ap(x_h, tt * 128 * E, [[E, 128], [1, E]]),
                            ).then_inc(aLp[tt % 2].h, 16)
                            aLp[tt % 2].inc(16)

                    @block.vector
                    def _(vector):
                        for tt in range(NTT):
                            b = (tt % 2) * 8
                            xv = xt[:, (tt % 2) * E:(tt % 2) * E + E]
                            vector.wait_ge(aLp[tt % 2].h, 16 * (tt // 2 + 1))
                            vector.reduce_sum(st0[:, b + 0:b + 1], xv, AX.X)
                            vector.scalar_tensor_tensor(
                                xsq[:, :], xv, 1.0, xv, ALU.bypass, ALU.mult,
                                accum_out=st0[:, b + 1:b + 2])
                            vector.drain()
                            vector.tensor_scalar(st0[:, b + 2:b + 3], st0[:, b + 0:b + 1], 1.0 / E, None, ALU.mult)
                            vector.tensor_scalar(st0[:, b + 4:b + 5], st0[:, b + 1:b + 2], 1.0 / E, None, ALU.mult)
                            vector.drain()
                            vector.tensor_tensor(st0[:, b + 3:b + 4], st0[:, b + 2:b + 3], st0[:, b + 2:b + 3], ALU.mult)
                            vector.drain()
                            vector.tensor_tensor(st0[:, b + 5:b + 6], st0[:, b + 4:b + 5], st0[:, b + 3:b + 4], ALU.subtract).then_inc(aS.h, 1)
                            aS.inc()
                            vector.wait_ge(aR.h, tt + 1)
                            if tt >= 1:
                                vector.wait_ge(aT.h, tt)
                            vector.tensor_scalar(
                                xnt[:, :], xv, st0[:, b + 2:b + 3], st0[:, b + 6:b + 7],
                                ALU.subtract, ALU.mult).then_inc(aX.h, 1)
                            aX.inc()

                    @block.scalar
                    def _(scalar):
                        for tt in range(NTT):
                            b = (tt % 2) * 8
                            scalar.wait_ge(aS.h, tt + 1)
                            if tt >= 1:
                                scalar.wait_ge(aX.h, tt)
                            scalar.activation(st0[:, b + 7:b + 8], st0[:, b + 5:b + 6], AF.Ln, bias=EPS_LN)
                            scalar.drain()
                            scalar.activation(st0[:, b + 6:b + 7], st0[:, b + 7:b + 8], AF.Exp, scale=-0.5).then_inc(aR.h, 1)
                            aR.inc()
                            if tt >= 1:
                                scalar.wait_ge(aT.h, tt)
                                dst = _ap(xn, (tt - 1) * 128,
                                          [[NEC * S, 128], [S, NEC], [1, 128]])
                                src = _ap(tp_ps, 0, [[E, 128], [128, NEC], [1, 128]])
                                scalar.copy(dst, src).then_inc(aE.h, 1)
                                aE.inc()
                        scalar.wait_ge(aT.h, NTT)
                        dst = _ap(xn, (NTT - 1) * 128,
                                  [[NEC * S, 128], [S, NEC], [1, 128]])
                        src = _ap(tp_ps, 0, [[E, 128], [128, NEC], [1, 128]])
                        scalar.copy(dst, src).then_inc(aE.h, 1)
                        aE.inc()

                    @block.tensor
                    def _(tensor):
                        for tt in range(NTT):
                            tensor.wait_ge(aX.h, tt + 1)
                            tensor.wait_ge(aE.h, tt)
                            for j in range(NEC):
                                mm = tensor.transpose(
                                    tp_ps[:, j * 128:(j + 1) * 128],
                                    xnt[:, j * 128:(j + 1) * 128],
                                    ident[:, :])
                            mm.then_inc(aT.h, 1)
                            aT.inc()

            if txn_h is not None:
                tS = sem("tSxn")
                with nc.Block("tapxn") as block:
                    @block.sync
                    def _(sync):
                        sync.dma_start(out=txn_h.ap(), in_=xn[:, :]).then_inc(tS.h, 16)
                        sync.wait_ge(tS.h, tS.inc(16))

            if go("p1v"):
                with ExitStack() as SP:
                    sb = lambda n, sh, dt: SP.enter_context(nc.sbuf_tensor(n, sh, dt))
                    kfm = sb("p1_kfm", [128, NKV * S], BF16)
                    vsb = sb("p1_vsb", [128, NTT * 512], BF16)
                    rqs = sb("p1_rqs", [128, 3 * S], BF16)
                    rku = sb("p1_rku", [128, 3 * S], BF16)
                    rqu = sb("p1_rqu", [128, S], BF16)
                    rksh = sb("p1_rksh", [128, S], BF16)
                    cosf = sb("p1_cosf", [128, S], F32)
                    sinf = sb("p1_sinf", [128, S], F32)
                    wvb = sb("p1_wvb", [128, NEC * 512], BF16)
                    wbuf = sb("p1_wbuf", [128, 2 * E], BF16)
                    ksq = sb("p1_ksq", [128, 1024], BF16)
                    rs_row = sb("p1_rsrow", [1, 1024], F32)
                    rsb_sb = sb("p1_rsb", [128, 1024], F32)
                    khat = sb("p1_khat", [128, 1024], BF16)
                    t1 = sb("p1_t1", [128, 1024], F32)
                    t2 = sb("p1_t2", [128, 1024], F32)

                    bL = sem("bL"); bWv = sem("bWv")
                    bWk = [sem("bWka"), sem("bWkb")]
                    bWr = [sem("bWra"), sem("bWrb")]
                    SPv = SP.enter_context(ExitStack())
                    vps = SPv.enter_context(nc.psum_tensor("p1_vps", [128, 2 * 512], F32))
                    bVM = sem("bVM"); bVE = sem("bVE")
                    if go('p1v'):
                        with nc.Block("p1v") as block:
                            @block.sync
                            def _(sync):
                                sync.dma_start(out=cosf[:, :], in_=cosf_h.ap()).then_inc(bL.h, 16)
                                sync.dma_start(out=sinf[:, :], in_=sinf_h.ap()).then_inc(bL.h, 16)
                                bL.inc(32)
                                sync.wait_ge(bL.h, bL.v)

                            @block.gpsimd
                            def _(gpsimd):
                                gpsimd.dma_start(
                                    out=wvb[:, :],
                                    in_=_ap(wv_h, 0, [[512, 128], [512 * 128, NEC], [1, 512]]),
                                ).then_inc(bWv.h, 16)
                                bWv.inc(16)

                            @block.tensor
                            def _(tensor):
                                tensor.wait_ge(bWv.h, 16)
                                for tt in range(NTT):
                                    if tt >= 2:
                                        tensor.wait_ge(bVE.h, tt - 1)
                                    po = (tt % 2) * 512
                                    for ec in range(NEC):
                                        mm = tensor.matmul(
                                            vps[:, po:po + 512],
                                            xn_cols(ec, tt * 128, 128),
                                            wvb[:, ec * 512:(ec + 1) * 512],
                                            start=(ec == 0), stop=(ec == NEC - 1))
                                    mm.then_inc(bVM.h, 1)
                                    bVM.inc()

                            @block.scalar
                            def _(scalar):
                                for tt in range(NTT):
                                    scalar.wait_ge(bVM.h, tt + 1)
                                    po = (tt % 2) * 512
                                    scalar.copy(
                                        vsb[:, tt * 512:(tt + 1) * 512], vps[:, po:po + 512]
                                    ).then_inc(bVE.h, 1)
                                    bVE.inc()

                    SPv.close()
                    kps = SP.enter_context(nc.psum_tensor("p1_kps", [128, 2 * 1024], F32))
                    sums_ps = SP.enter_context(nc.psum_tensor("p1_sums", [4, 1024], F32))
                    rot_ps = SP.enter_context(nc.psum_tensor("p1_rot", [128, 1024], F32))
                    bM = sem("bM"); bQ = sem("bQ"); bS = sem("bS"); bR = sem("bR")
                    bB = sem("bB"); bE = sem("bE"); bH = sem("bH"); bO = sem("bO")
                    bF = sem("bF"); bN1 = sem("bN1"); bNS = sem("bNS"); bNX = sem("bNX")

                    if go('p1k'):
                        with nc.Block("p1k") as block:
                            units = [(oc, hh) for oc in range(NKV) for hh in range(2)]

                            @block.gpsimd
                            def _(gpsimd):
                                for oc in range(NKV):
                                    if oc >= 2:
                                        gpsimd.wait_ge(bM.h, 2 * (oc - 1))
                                    gpsimd.dma_start(
                                        out=wbuf[:, (oc % 2) * E:(oc % 2) * E + E],
                                        in_=_ap(wk_h, oc * E * 128,
                                                [[128, 128], [128 * 128, NEC], [1, 128]]),
                                    ).then_inc(bWk[oc % 2].h, 16)
                                    bWk[oc % 2].inc(16)

                            def _kpost(tensor, u):
                                tensor.wait_ge(bQ.h, u + 1)
                                if u >= 1:
                                    tensor.wait_ge(bNX.h, u)
                                for q4 in range(2):
                                    mm = tensor.matmul(
                                        sums_ps[0:1, q4 * 512:(q4 + 1) * 512],
                                        onesc[:, :], ksq[:, q4 * 512:(q4 + 1) * 512],
                                        start=True, stop=True)
                                mm.then_inc(bS.h, 1)
                                bS.inc()
                                tensor.wait_ge(bR.h, u + 1)
                                for q4 in range(2):
                                    mm = tensor.matmul(
                                        rot_ps[:, q4 * 512:(q4 + 1) * 512],
                                        onesr[:, :], rs_row[0:1, q4 * 512:(q4 + 1) * 512],
                                        start=True, stop=True)
                                mm.then_inc(bB.h, 1)
                                bB.inc()
                                tensor.wait_ge(bH.h, u + 1)
                                for q4 in range(2):
                                    mm = tensor.matmul(
                                        rot_ps[:, q4 * 512:(q4 + 1) * 512],
                                        rtm[:, :], khat[:, q4 * 512:(q4 + 1) * 512],
                                        start=True, stop=True)
                                mm.then_inc(bO.h, 1)
                                bO.inc()
                                tensor.wait_ge(bN1.h, u + 1)
                                for q4 in range(2):
                                    mm = tensor.matmul(
                                        sums_ps[0:1, q4 * 512:(q4 + 1) * 512],
                                        onesc[:, :], ksq[:, q4 * 512:(q4 + 1) * 512],
                                        start=True, stop=True)
                                mm.then_inc(bNS.h, 1)
                                bNS.inc()

                            @block.tensor
                            def _(tensor):
                                for u, (oc, hh) in enumerate(units):
                                    tensor.wait_ge(bWk[oc % 2].h, 16 * (oc // 2 + 1))
                                    if u >= 2:
                                        tensor.wait_ge(bH.h, u - 1)
                                    po = (u % 2) * 1024
                                    wv_ = wbuf[:, (oc % 2) * E:(oc % 2) * E + E]
                                    for ec in range(NEC):
                                        for q4 in range(2):
                                            mm = tensor.matmul(
                                                kps[:, po + q4 * 512:po + (q4 + 1) * 512],
                                                wv_[:, ec * 128:(ec + 1) * 128],
                                                xn_cols(ec, hh * 1024 + q4 * 512, 512),
                                                start=(ec == 0), stop=(ec == NEC - 1))
                                    mm.then_inc(bM.h, 1)
                                    bM.inc()
                                    if u >= 1:
                                        _kpost(tensor, u - 1)
                                _kpost(tensor, len(units) - 1)


                            @block.vector
                            def _(vector):
                                for u, (oc, hh) in enumerate(units):
                                    po = (u % 2) * 1024
                                    vector.wait_ge(bE.h, u + 1)
                                    vector.scalar_tensor_tensor(
                                        khat[:, :], kps[:, po:po + 1024], knw[:, 0:1],
                                        rsb_sb[:, :],
                                        ALU.mult, ALU.mult).then_inc(bH.h, 1)
                                    bH.inc()
                                    vector.drain()
                                    vector.wait_ge(bQ.h, u + 1)
                                    vector.tensor_tensor(ksq[:, :], khat[:, :], khat[:, :], ALU.mult).then_inc(bN1.h, 1)
                                    bN1.inc()
                                    vector.wait_ge(bO.h, u + 1)
                                    cs = cosf[:, hh * 1024:hh * 1024 + 1024]
                                    sn = sinf[:, hh * 1024:hh * 1024 + 1024]
                                    vector.tensor_tensor(t1[:, :], rot_ps[:, :], sn, ALU.mult)
                                    vector.tensor_tensor(t2[:, :], khat[:, :], cs, ALU.mult)
                                    vector.drain()
                                    dst = kfm[:, oc * S + hh * 1024: oc * S + hh * 1024 + 1024]
                                    vector.tensor_tensor(dst, t1[:, :], t2[:, :], ALU.add).then_inc(bF.h, 1)
                                    bF.inc()
                                    vector.wait_ge(bNS.h, u + 1)
                                    vector.reduce_max(
                                        kvmax2[0:1, u:u + 1], sums_ps[0:1, 0:1024], AX.X
                                    ).then_inc(bNX.h, 1)
                                    bNX.inc()
                                    vector.drain()

                            @block.scalar
                            def _(scalar):
                                for u, (oc, hh) in enumerate(units):
                                    po = (u % 2) * 1024
                                    scalar.wait_ge(bM.h, u + 1)
                                    if u >= 1:
                                        scalar.wait_ge(bNS.h, u)
                                    scalar.activation(ksq[:, :], kps[:, po:po + 1024],
                                                      AF.Square).then_inc(bQ.h, 1)
                                    bQ.inc()
                                    scalar.wait_ge(bS.h, u + 1)
                                    scalar.activation(rs_row[0:1, :], sums_ps[0:1, :], AF.Ln,
                                                      bias=EPS_RMS, scale=1.0 / 128.0)
                                    scalar.drain()
                                    scalar.activation(rs_row[0:1, :], rs_row[0:1, :], AF.Exp,
                                                      scale=-0.5).then_inc(bR.h, 1)
                                    bR.inc()
                                    scalar.wait_ge(bB.h, u + 1)
                                    scalar.copy(rsb_sb[:, :], rot_ps[:, :]).then_inc(bE.h, 1)
                                    bE.inc()

                    rM = sem("rM"); rE = sem("rE"); rQ2 = sem("rQ2")
                    rNS = sem("rNS"); rNX = sem("rNX")
                    if go('p1r'):
                        with nc.Block("p1r") as block:
                            runits = [(src, oc, hh) for src in range(2) for oc in range(3)
                                      for hh in range(2)]


                            @block.gpsimd
                            def _(gpsimd):
                                for i, (src, oc) in enumerate(
                                        [(s_, o_) for s_ in range(2) for o_ in range(3)]):
                                    if i >= 2:
                                        gpsimd.wait_ge(rM.h, 2 * (i - 1))
                                    wsrc = wrq_h if src == 0 else wrk_h
                                    gpsimd.dma_start(
                                        out=wbuf[:, (i % 2) * E:(i % 2) * E + E],
                                        in_=_ap(wsrc, oc * E * 128,
                                                [[128, 128], [128 * 128, NEC], [1, 128]]),
                                    ).then_inc(bWr[i % 2].h, 16)
                                    bWr[i % 2].inc(16)

                            @block.tensor
                            def _(tensor):
                                for u, (src, oc, hh) in enumerate(runits):
                                    i = src * 3 + oc
                                    tensor.wait_ge(bWr[i % 2].h, 16 * (i // 2 + 1))
                                    if u >= 2:
                                        tensor.wait_ge(rE.h, u - 1)
                                        tensor.wait_ge(rQ2.h, u - 1)
                                    po = (u % 2) * 1024
                                    wv_ = wbuf[:, (i % 2) * E:(i % 2) * E + E]
                                    for ec in range(NEC):
                                        for q4 in range(2):
                                            mm = tensor.matmul(
                                                kps[:, po + q4 * 512:po + (q4 + 1) * 512],
                                                wv_[:, ec * 128:(ec + 1) * 128],
                                                xn_cols(ec, hh * 1024 + q4 * 512, 512),
                                                start=(ec == 0), stop=(ec == NEC - 1))
                                    mm.then_inc(rM.h, 1)
                                    rM.inc()
                                    tensor.wait_ge(rQ2.h, u + 1)
                                    if u >= 1:
                                        tensor.wait_ge(rNX.h, u)
                                    for q4 in range(2):
                                        mm = tensor.matmul(
                                            sums_ps[0:4, q4 * 512:(q4 + 1) * 512],
                                            blk4[:, :], ksq[:, q4 * 512:(q4 + 1) * 512],
                                            start=True, stop=True)
                                    mm.then_inc(rNS.h, 1)
                                    rNS.inc()

                            @block.scalar
                            def _(scalar):
                                for u, (src, oc, hh) in enumerate(runits):
                                    po = (u % 2) * 1024
                                    scalar.wait_ge(rE.h, u + 1)
                                    scalar.activation(ksq[:, :], kps[:, po:po + 1024],
                                                      AF.Square).then_inc(rQ2.h, 1)
                                    rQ2.inc()
                                    scalar.drain()

                            @block.vector
                            def _(vector):
                                for u, (src, oc, hh) in enumerate(runits):
                                    po = (u % 2) * 1024
                                    kv_ = kps[:, po:po + 1024]
                                    vector.wait_ge(rM.h, u + 1)
                                    sl = slice(hh * 1024, hh * 1024 + 1024)
                                    if src == 0:
                                        gc = gs_half[:, 0:1] if oc == 0 else gs_cols[:, oc:oc + 1]
                                        vector.tensor_scalar(
                                            rqs[:, oc * S + hh * 1024: oc * S + hh * 1024 + 1024],
                                            kv_, gc, None, ALU.mult)
                                        if oc == 0:
                                            vector.tensor_copy(rqu[:, sl], kv_)
                                    else:
                                        vector.tensor_copy(
                                            rku[:, oc * S + hh * 1024: oc * S + hh * 1024 + 1024],
                                            kv_)
                                        if oc == 0:
                                            vector.tensor_scalar(
                                                rksh[:, sl], kv_, gs_half[:, 0:1],
                                                None, ALU.mult)
                                    vector.engine_nop().then_inc(rE.h, 1)
                                    rE.inc()
                                    vector.wait_ge(rNS.h, u + 1)
                                    dstm = rqmax2 if src == 0 else rkmax2
                                    col = oc * 2 + hh
                                    vector.reduce_max(
                                        dstm[0:4, col:col + 1], sums_ps[0:4, 0:1024], AX.X
                                    ).then_inc(rNX.h, 1)
                                    rNX.inc()
                                    vector.drain()

                    bSp = sem("bSp")
                    with nc.Block("p1s") as block:
                        @block.sync
                        def _(sync):
                            for src_t, dst_hd, n, w in (
                                (kfm, spk_h, NKV, S), (vsb, spv_h, 1, NTT * 512),
                                (rqs, sprqs_h, 3, S), (rku, sprku_h, 3, S),
                                (rqu, sprqu_h, 1, S), (rksh, sprksh_h, 1, S),
                            ):
                                if n == 1:
                                    dst_ap = _ap(dst_hd, 0, [[w, 128], [1, w]])
                                else:
                                    dst_ap = _ap(dst_hd, 0, [[w, 128], [128 * w, n], [1, w]])
                                sync.dma_start(out=dst_ap, in_=src_t[:, :]).then_inc(bSp.h, 16)
                                bSp.inc(16)
                            sync.wait_ge(bSp.h, bSp.v)

            if go("p2"):
                with ExitStack() as SP:
                    sb = lambda n, sh, dt: SP.enter_context(nc.sbuf_tensor(n, sh, dt))
                    cosf = sb("p2_cosf", [128, S], F32)
                    sinf = sb("p2_sinf", [128, S], F32)
                    wbuf = sb("p2_wbuf", [128, 2 * E], BF16)
                    qsq = sb("p2_qsq", [128, 2 * 1024], BF16)
                    lnrow = sb("p2_lnrow", [1, 1024], F32)
                    rs_row = sb("p2_rsrow", [1, 1024], F32)
                    rs2_row = sb("p2_rs2row", [1, 1024], F32)
                    tmp_row = sb("p2_tmprow", [1, 1024], F32)
                    qnw2 = sb("p2_qnw2", [128, 1], BF16)
                    rsb_sb = sb("p2_rsb", [128, 2 * 1024], F32)
                    qhat = sb("p2_qhat", [128, 2 * 1024], BF16)
                    t1 = sb("p2_t1", [128, 2 * 1024], F32)
                    t2 = sb("p2_t2", [128, 2 * 1024], F32)
                    qout = sb("p2_qout", [128, 2 * 1024], BF16)
                    qps = SP.enter_context(nc.psum_tensor("p2_qps", [128, 2 * 1024], F32))
                    sums_ps = SP.enter_context(nc.psum_tensor("p2_sums", [64, 1024], F32))
                    rot_ps = SP.enter_context(nc.psum_tensor("p2_rot", [128, 1024], F32))
                    cL = sem("cL"); cWp = [sem("cWa"), sem("cWb")]; cM = sem("cM"); cQ = sem("cQ")
                    cS = sem("cS"); cR = sem("cR"); cB = sem("cB"); cE = sem("cE")
                    cH = sem("cH"); cO = sem("cO"); cF1 = sem("cF1"); cF = sem("cF")
                    cNX = sem("cNX")
                    cSp = [sem("cSpa"), sem("cSpb")]
                    with nc.Block("p2") as block:
                        qunits = [(h, hh) for h in range(NH) for hh in range(2)]
                        NU = len(qunits)

                        @block.sync
                        def _(sync):
                            sync.dma_start(out=cosf[:, :], in_=cosf_h.ap()).then_inc(cL.h, 16)
                            sync.dma_start(out=sinf[:, :], in_=sinf_h.ap()).then_inc(cL.h, 16)
                            cL.inc(32)
                            for u, (h, hh) in enumerate(qunits):
                                sync.wait_ge(cF.h, u + 1)
                                for k in range(2):
                                    g = hh * 2 + k
                                    sync.dma_start(
                                        out=_ap(spq_h, (g * NH + h) * 128 * 512,
                                                [[512, 128], [1, 512]]),
                                        in_=qout[:, (u % 2) * 1024 + k * 512:
                                                 (u % 2) * 1024 + (k + 1) * 512],
                                    ).then_inc(cSp[u % 2].h, 16)
                                    cSp[u % 2].inc(16)
                            sync.wait_ge(cSp[0].h, cSp[0].v)
                            sync.wait_ge(cSp[1].h, cSp[1].v)

                        @block.gpsimd
                        def _(gpsimd):
                            for h in (0, 1):
                                gpsimd.dma_start(
                                    out=wbuf[:, (h % 2) * E:(h % 2) * E + E],
                                    in_=_ap(wq_h, h * E * 128,
                                            [[128, 128], [128 * 128, NEC], [1, 128]]),
                                ).then_inc(cWp[h % 2].h, 16)
                                cWp[h % 2].inc(16)
                            for u, (h, hh) in enumerate(qunits):
                                if hh == 0 and h + 2 < NH:
                                    h2 = h + 2
                                    gpsimd.wait_ge(cM.h, 2 * (h + 1))
                                    gpsimd.dma_start(
                                        out=wbuf[:, (h2 % 2) * E:(h2 % 2) * E + E],
                                        in_=_ap(wq_h, h2 * E * 128,
                                                [[128, 128], [128 * 128, NEC], [1, 128]]),
                                    ).then_inc(cWp[h2 % 2].h, 16)
                                    cWp[h2 % 2].inc(16)
                                if u >= 2:
                                    gpsimd.wait_ge(cSp[u % 2].h, 32 * (u // 2))
                                gpsimd.wait_ge(cF1.h, u + 1)
                                gpsimd.tensor_tensor(
                                    qout[:, (u % 2) * 1024:(u % 2) * 1024 + 1024],
                                    t1[:, (u % 2) * 1024:(u % 2) * 1024 + 1024],
                                    t2[:, (u % 2) * 1024:(u % 2) * 1024 + 1024],
                                    ALU.add).then_inc(cF.h, 1)
                                cF.inc()

                        def _s1(tensor, v):
                            po = (v % 2) * 1024
                            tensor.wait_ge(cQ.h, v + 1)
                            if v >= 1:
                                tensor.wait_ge(cR.h, v)
                                tensor.wait_ge(cNX.h, 2 * v)
                            for q4 in range(2):
                                tensor.matmul(
                                    sums_ps[0:1, q4 * 512:(q4 + 1) * 512],
                                    onesc[:, :], qsq[:, po + q4 * 512:po + (q4 + 1) * 512],
                                    start=True, stop=True)
                            for q4 in range(2):
                                mm = tensor.matmul(
                                    sums_ps[32:33, q4 * 512:(q4 + 1) * 512],
                                    qnw2[:, 0:1], qsq[:, po + q4 * 512:po + (q4 + 1) * 512],
                                    start=True, stop=True)
                            mm.then_inc(cS.h, 1)
                            cS.inc()

                        def _s2(tensor, v):
                            tensor.wait_ge(cR.h, v + 1)
                            if v >= 1:
                                tensor.wait_ge(cF1.h, v)
                            for q4 in range(2):
                                mm = tensor.matmul(
                                    rot_ps[:, q4 * 512:(q4 + 1) * 512],
                                    onesr[:, :], rs_row[0:1, q4 * 512:(q4 + 1) * 512],
                                    start=True, stop=True)
                            mm.then_inc(cB.h, 1)
                            cB.inc()

                        def _s3(tensor, v):
                            po = (v % 2) * 1024
                            tensor.wait_ge(cH.h, v + 1)
                            tensor.wait_ge(cE.h, v + 1)
                            for q4 in range(2):
                                mm = tensor.matmul(
                                    rot_ps[:, q4 * 512:(q4 + 1) * 512],
                                    rtm[:, :], qhat[:, po + q4 * 512:po + (q4 + 1) * 512],
                                    start=True, stop=True)
                            mm.then_inc(cO.h, 1)
                            cO.inc()

                        @block.tensor
                        def _(tensor):
                            for u, (h, hh) in enumerate(qunits):
                                tensor.wait_ge(cWp[h % 2].h, 16 * (h // 2 + 1))
                                if u >= 2:
                                    tensor.wait_ge(cH.h, u - 1)
                                wv_ = wbuf[:, (h % 2) * E:(h % 2) * E + E]
                                po = (u % 2) * 1024
                                for ec in range(NEC):
                                    for q4 in range(2):
                                        mm = tensor.matmul(
                                            qps[:, po + q4 * 512:po + (q4 + 1) * 512],
                                            wv_[:, ec * 128:(ec + 1) * 128],
                                            xn_cols(ec, hh * 1024 + q4 * 512, 512),
                                            start=(ec == 0), stop=(ec == NEC - 1))
                                    if u >= 1:
                                        if ec == 6:
                                            _s1(tensor, u - 1)
                                        elif ec == 11:
                                            _s2(tensor, u - 1)
                                        elif ec == 15:
                                            _s3(tensor, u - 1)
                                mm.then_inc(cM.h, 1)
                                cM.inc()
                            _s1(tensor, NU - 1)
                            _s2(tensor, NU - 1)
                            _s3(tensor, NU - 1)

                        @block.vector
                        def _(vector):
                            vector.wait_ge(cL.h, 32)
                            for u, (h, hh) in enumerate(qunits):
                                po = (u % 2) * 1024
                                vector.wait_ge(cE.h, u + 1)
                                vector.scalar_tensor_tensor(
                                    qhat[:, po:po + 1024], qps[:, po:po + 1024],
                                    qnw_eff[:, 0:1], rsb_sb[:, po:po + 1024],
                                    ALU.mult, ALU.mult).then_inc(cH.h, 1)
                                cH.inc()
                                vector.wait_ge(cO.h, u + 1)
                                if u >= 2:
                                    vector.wait_ge(cF.h, u - 1)
                                cs = cosf[:, hh * 1024:hh * 1024 + 1024]
                                sn = sinf[:, hh * 1024:hh * 1024 + 1024]
                                vector.drain()
                                vector.tensor_tensor(t1[:, po:po + 1024], rot_ps[:, :],
                                                     sn, ALU.mult)
                                vector.tensor_tensor(t2[:, po:po + 1024],
                                                     qhat[:, po:po + 1024],
                                                     cs, ALU.mult).then_inc(cF1.h, 1)
                                cF1.inc()
                                vector.wait_ge(cS.h, u + 1)
                                vector.wait_ge(cR.h, u + 1)
                                vector.tensor_tensor(tmp_row[0:1, :], sums_ps[32:33, :],
                                                     rs2_row[0:1, :], ALU.mult)
                                vector.drain()
                                for k in range(2):
                                    col = 4 * h + 2 * hh + k
                                    vector.reduce_max(
                                        maxq2[0:1, col:col + 1],
                                        tmp_row[0:1, k * 512:(k + 1) * 512], AX.X
                                    ).then_inc(cNX.h, 1)
                                    cNX.inc()
                                vector.drain()

                        @block.scalar
                        def _(scalar):
                            scalar.activation(qnw2[:, 0:1], qnw_eff[:, 0:1], AF.Square)
                            for u, (h, hh) in enumerate(qunits):
                                po = (u % 2) * 1024
                                scalar.wait_ge(cM.h, u + 1)
                                if u >= 2:
                                    scalar.wait_ge(cS.h, u - 1)
                                scalar.activation(qsq[:, po:po + 1024], qps[:, po:po + 1024],
                                                  AF.Square).then_inc(cQ.h, 1)
                                cQ.inc()
                                scalar.wait_ge(cS.h, u + 1)
                                if u >= 1:
                                    scalar.wait_ge(cB.h, u)
                                scalar.activation(lnrow[0:1, :], sums_ps[0:1, :], AF.Ln,
                                                  bias=EPS_RMS, scale=1.0 / 128.0)
                                scalar.drain()
                                scalar.activation(rs_row[0:1, :], lnrow[0:1, :], AF.Exp,
                                                  scale=-0.5)
                                scalar.activation(rs2_row[0:1, :], lnrow[0:1, :], AF.Exp,
                                                  scale=-1.0).then_inc(cR.h, 1)
                                cR.inc()
                                scalar.wait_ge(cB.h, u + 1)
                                if u >= 2:
                                    scalar.wait_ge(cH.h, u - 1)
                                scalar.copy(rsb_sb[:, po:po + 1024],
                                            rot_ps[:, :]).then_inc(cE.h, 1)
                                cE.inc()

            if go("p3"):
                with ExitStack() as SP:
                    sb = lambda n, sh, dt: SP.enter_context(nc.sbuf_tensor(n, sh, dt))
                    hfm = sb("p3_hfm", [128, NIC * 1024], BF16)
                    gwb = sb("p3_gwb", [128, 2 * E], BF16)
                    uwb = sb("p3_uwb", [128, 2 * E], BF16)
                    dwb = sb("p3_dwb", [128, 2 * 2048], BF16)
                    sg = sb("p3_sg", [128, 2 * 1024], F32)
                    gout = sb("p3_gout", [128, 2 * 1024], BF16)
                    dM = sem("dM"); dSi = sem("dSi"); dH = sem("dH")
                    dDM = sem("dDM"); dGE = sem("dGE")
                    for hh in range(2):
                        dWp = [sem(f"dW{hh}a"), sem(f"dW{hh}b")]
                        dDWp = [sem(f"dDW{hh}a"), sem(f"dDW{hh}b")]
                        dSpp = [sem(f"dSp{hh}a"), sem(f"dSp{hh}b")]
                        with ExitStack() as SS:
                            gps = SS.enter_context(nc.psum_tensor(f"p3_gps{hh}", [128, 2 * 1024], F32))
                            ups = SS.enter_context(nc.psum_tensor(f"p3_ups{hh}", [128, 2 * 1024], F32))
                            with nc.Block(f"p3g{hh}") as block:
                                base_dM = dM.v
                                base_dSi = dSi.v
                                base_dH = dH.v

                                @block.gpsimd
                                def _(gpsimd):
                                    for ic in range(NIC):
                                        if ic >= 2:
                                            gpsimd.wait_ge(dM.h, base_dM + ic - 1)
                                        for wsrc, dstb in ((wg_h, gwb), (wu_h, uwb)):
                                            gpsimd.dma_start(
                                                out=dstb[:, (ic % 2) * E:(ic % 2) * E + E],
                                                in_=_ap(wsrc, ic * E * 128,
                                                        [[128, 128], [128 * 128, NEC], [1, 128]]),
                                            ).then_inc(dWp[ic % 2].h, 16)
                                            dWp[ic % 2].inc(16)

                                @block.tensor
                                def _(tensor):
                                    for ic in range(NIC):
                                        tensor.wait_ge(dWp[ic % 2].h, 32 * (ic // 2 + 1))
                                        if ic >= 2:
                                            tensor.wait_ge(dH.h, base_dH + ic - 1)
                                        po = (ic % 2) * 1024
                                        gw_ = gwb[:, (ic % 2) * E:(ic % 2) * E + E]
                                        uw_ = uwb[:, (ic % 2) * E:(ic % 2) * E + E]
                                        for ec in range(NEC):
                                            for q4 in range(2):
                                                tensor.matmul(
                                                    gps[:, po + q4 * 512:po + (q4 + 1) * 512],
                                                    gw_[:, ec * 128:(ec + 1) * 128],
                                                    xn_cols(ec, hh * 1024 + q4 * 512, 512),
                                                    start=(ec == 0), stop=(ec == NEC - 1))
                                        for ec in range(NEC):
                                            for q4 in range(2):
                                                mm = tensor.matmul(
                                                    ups[:, po + q4 * 512:po + (q4 + 1) * 512],
                                                    uw_[:, ec * 128:(ec + 1) * 128],
                                                    xn_cols(ec, hh * 1024 + q4 * 512, 512),
                                                    start=(ec == 0), stop=(ec == NEC - 1))
                                        mm.then_inc(dM.h, 1)
                                        dM.inc()

                                @block.scalar
                                def _(scalar):
                                    for ic in range(NIC):
                                        scalar.wait_ge(dM.h, base_dM + ic + 1)
                                        if ic >= 2:
                                            scalar.wait_ge(dH.h, base_dH + ic - 1)
                                        po = (ic % 2) * 1024
                                        scalar.activation(
                                            sg[:, po:po + 1024], gps[:, po:po + 1024],
                                            AF.Silu,
                                            scale=gates_c[:, ic:ic + 1]).then_inc(dSi.h, 1)
                                        dSi.inc()

                                @block.vector
                                def _(vector):
                                    for ic in range(NIC):
                                        vector.wait_ge(dSi.h, base_dSi + ic + 1)
                                        po = (ic % 2) * 1024
                                        vector.scalar_tensor_tensor(
                                            hfm[:, ic * 1024:(ic + 1) * 1024],
                                            ups[:, po:po + 1024], ups_c[:, ic:ic + 1],
                                            sg[:, po:po + 1024], ALU.mult, ALU.mult).then_inc(dH.h, 1)
                                        dH.inc()

                        with ExitStack() as SS:
                            dps = SS.enter_context(nc.psum_tensor(f"p3_dps{hh}", [128, 4 * 1024], F32))
                            with nc.Block(f"p3d{hh}") as block:
                                base_dDM = dDM.v
                                base_dGE = dGE.v

                                @block.gpsimd
                                def _(gpsimd):
                                    for og in range(4):
                                        for icb in range(NIC // 4):
                                            nb = og * 11 + icb
                                            if nb >= 2:
                                                gpsimd.wait_ge(dDM.h, base_dDM + 16 * (nb - 1))
                                            gpsimd.dma_start(
                                                out=dwb[:, (nb % 2) * 2048:(nb % 2) * 2048 + 2048],
                                                in_=_ap(wd_h,
                                                        (og * NIC + icb * 4) * 128 * 512,
                                                        [[512, 128], [512 * 128, 4], [1, 512]]),
                                            ).then_inc(dDWp[nb % 2].h, 16)
                                            dDWp[nb % 2].inc(16)

                                @block.tensor
                                def _(tensor):
                                    for og in range(4):
                                        for ic in range(NIC):
                                            nb = og * 11 + ic // 4
                                            tensor.wait_ge(dDWp[nb % 2].h, 16 * (nb // 2 + 1))
                                            if ic == 0 and og >= 1:
                                                tensor.wait_ge(dGE.h, base_dGE + 4 * og)
                                            wv_ = dwb[:, (nb % 2) * 2048 + (ic % 4) * 512:
                                                      (nb % 2) * 2048 + (ic % 4 + 1) * 512]
                                            for oc4 in range(4):
                                                for q4 in range(2):
                                                    mm = tensor.matmul(
                                                        dps[:, oc4 * 1024 + q4 * 512:
                                                            oc4 * 1024 + (q4 + 1) * 512],
                                                        wv_[:, oc4 * 128:(oc4 + 1) * 128],
                                                        hfm[:, ic * 1024 + q4 * 512:
                                                            ic * 1024 + (q4 + 1) * 512],
                                                        start=(ic == 0), stop=(ic == NIC - 1))
                                            mm.then_inc(dDM.h, 4)
                                            dDM.inc(4)

                                @block.scalar
                                def _(scalar):
                                    for og in range(4):
                                        scalar.wait_ge(dDM.h, base_dDM + 4 * NIC * (og + 1))
                                        for oc4 in range(4):
                                            oc = og * 4 + oc4
                                            if oc >= 2:
                                                scalar.wait_ge(dSpp[oc % 2].h, 16 * (oc // 2))
                                            scalar.activation(
                                                gout[:, (oc % 2) * 1024:(oc % 2) * 1024 + 1024],
                                                dps[:, oc4 * 1024:(oc4 + 1) * 1024],
                                                AF.Silu, scale=downs_c[:, oc:oc + 1]
                                            ).then_inc(dGE.h, 1)
                                            dGE.inc()

                                @block.sync
                                def _(sync):
                                    for oc in range(NEC):
                                        sync.wait_ge(dGE.h, base_dGE + oc + 1)
                                        sync.dma_start(
                                            out=_ap(spg_h, oc * 128 * S + hh * 1024,
                                                    [[S, 128], [1, 1024]]),
                                            in_=gout[:, (oc % 2) * 1024:(oc % 2) * 1024 + 1024],
                                        ).then_inc(dSpp[oc % 2].h, 16)
                                        dSpp[oc % 2].inc(16)
                                    sync.wait_ge(dSpp[0].h, dSpp[0].v)
                                    sync.wait_ge(dSpp[1].h, dSpp[1].v)


        if go("p4"):
            with ExitStack() as SP:
                sb = lambda n, sh, dt: SP.enter_context(nc.sbuf_tensor(n, sh, dt))
                kfm = sb("p4_kfm", [128, NKV * S], BF16)
                vsb = sb("p4_vsb", [128, NTT * 512], BF16)
                rqs = sb("p4_rqs", [128, 3 * S], BF16)
                rku = sb("p4_rku", [128, NGH * S], BF16)
                rqu = sb("p4_rqu", [128, 4 * S], BF16)
                rksh = sb("p4_rksh", [128, S], BF16)
                masks = sb("p4_masks", [128, 4 * 512], F32)
                qg = sb("p4_qg", [128, 8 * 512], BF16)
                ptb = sb("p4_ptb", [128, 16 * 512], BF16)
                ogs = sb("p4_ogs", [128, 4 * 512], BF16)
                sums_sb = sb("p4_sums", [1, 2 * 512], F32)
                recip = sb("p4_recip", [1, 2 * 512], F32)
                rb_sb = sb("p4_rbsb", [128, 2 * 512], F32)
                brow = sb("p4_brow", [1, 64], F32)
                btmp = sb("p4_btmp", [1, 256], F32)

                eR = sem("eR"); eP = sem("ePr"); ePs = sem("ePs")
                sc_ps = SP.enter_context(nc.psum_tensor("p4_sc", [128, 4 * 512], F32))
                o_ps = SP.enter_context(nc.psum_tensor("p4_o", [128, 2 * 512], F32))
                sm_ps = SP.enter_context(nc.psum_tensor("p4_sm", [1, 512], F32))
                rb_ps = SP.enter_context(nc.psum_tensor("p4_rb", [128, 512], F32))

                eMz = sem("eMz")
                with nc.Block("p4b") as block:
                    @block.sync
                    def _(sync):
                        sync.dma_start(out=masks[:, :], in_=masks_h.ap()).then_inc(eR.h, 16)
                        eR.inc(16)
                        for dst, src_hd, n, w in (
                            (kfm, spk_h, NKV, S), (vsb, spv_h, 1, NTT * 512),
                            (rqs, sprqs_h, 3, S), (rksh, sprksh_h, 1, S),
                        ):
                            if n == 1:
                                src_ap = _ap(src_hd, 0, [[w, 128], [1, w]])
                            else:
                                src_ap = _ap(src_hd, 0, [[w, 128], [128 * w, n], [1, w]])
                            sync.dma_start(out=dst[:, :], in_=src_ap).then_inc(eR.h, 16)
                            eR.inc(16)
                        sync.wait_ge(eMz.h, 1)
                        for h in range(NGH):
                            oc, l = h // 4, h % 4
                            sync.dma_start(
                                out=rku[32 * l:32 * (l + 1), h * S:(h + 1) * S],
                                in_=_ap(sprku_h, oc * 128 * S + 32 * l * S,
                                        [[S, 32], [1, S]])).then_inc(eR.h, 16)
                            eR.inc(16)
                        for h in range(4):
                            sync.dma_start(
                                out=rqu[32 * h:32 * (h + 1), h * S:(h + 1) * S],
                                in_=_ap(sprqu_h, 32 * h * S,
                                        [[S, 32], [1, S]])).then_inc(eR.h, 16)
                            eR.inc(16)
                        sync.wait_ge(ePs.h, 1)
                        sync.dma_start(out=btmp[0:1, 80:92],
                                       in_=_ap(rqmax2, 0, [[6, 4], [1, 3]])
                                       ).then_inc(eR.h, 16)
                        sync.dma_start(out=btmp[0:1, 96:108],
                                       in_=_ap(rkmax2, 0, [[6, 4], [1, 3]])
                                       ).then_inc(eR.h, 16)
                        eR.inc(32)

                    @block.vector
                    def _(vector):
                        vector.memset(rku[:, :], 0.0)
                        vector.memset(rqu[:, :], 0.0)
                        vector.drain()
                        vector.memset(btmp[0:1, 240:241], 0.0).then_inc(eMz.h, 1)
                        vector.tensor_tensor(
                            _ap(rqmax2, 0, [[6, 4], [2, 3]]),
                            _ap(rqmax2, 0, [[6, 4], [2, 3]]),
                            _ap(rqmax2, 1, [[6, 4], [2, 3]]), ALU.max)
                        vector.tensor_tensor(
                            _ap(rkmax2, 0, [[6, 4], [2, 3]]),
                            _ap(rkmax2, 0, [[6, 4], [2, 3]]),
                            _ap(rkmax2, 1, [[6, 4], [2, 3]]), ALU.max)
                        vector.tensor_tensor(
                            btmp[0:1, 64:68],
                            _ap(kvmax2, 0, [[8, 1], [2, 4]]),
                            _ap(kvmax2, 1, [[8, 1], [2, 4]]), ALU.max).then_inc(ePs.h, 1)
                        ePs.inc()
                        vector.wait_ge(eP.h, 4)
                        vector.tensor_tensor(
                            btmp[0:1, 144:156], btmp[0:1, 112:124], btmp[0:1, 128:140],
                            ALU.mult)
                        vector.drain()
                        vector.tensor_tensor(
                            _ap(btmp, 144, [[256, 1], [3, 4], [1, 3]]),
                            _ap(btmp, 144, [[256, 1], [3, 4], [1, 3]]),
                            _ap(gsrow, 0, [[12, 1], [1, 4], [4, 3]]), ALU.mult)
                        vector.drain()
                        vector.scalar_tensor_tensor(
                            btmp[0:1, 144:156], btmp[0:1, 144:156], -1.0,
                            btmp[0:1, 144:156], ALU.mult, ALU.max)
                        vector.memset(btmp[0:1, 160:176], 0.0)
                        vector.drain()
                        vector.tensor_copy(
                            _ap(btmp, 160, [[256, 1], [4, 3], [1, 4]]),
                            _ap(btmp, 144, [[256, 1], [1, 3], [3, 4]]))
                        vector.tensor_tensor(
                            _ap(brow, 0, [[64, 1], [16, 4], [1, 16]]),
                            _ap(btmp, 0, [[256, 1], [16, 4], [1, 16]]),
                            _ap(btmp, 70, [[256, 1], [1, 4], [0, 16]]), ALU.mult)
                        vector.drain()
                        vector.scalar_tensor_tensor(
                            _ap(brow, 0, [[64, 1], [4, 16], [1, 4]]),
                            _ap(brow, 0, [[64, 1], [4, 16], [1, 4]]), -1.0,
                            _ap(btmp, 160, [[256, 1], [1, 16], [0, 4]]),
                            ALU.mult, ALU.subtract).then_inc(ePs.h, 1)
                        ePs.inc()

                    @block.scalar
                    def _(scalar):
                        scalar.wait_ge(eR.h, eR.v)
                        for dst0, dstn, src in (
                            (0, 64, maxq2[0:1, :]), (70, 74, btmp[0:1, 64:68]),
                            (112, 124, btmp[0:1, 80:92]), (128, 140, btmp[0:1, 96:108]),
                        ):
                            scalar.activation(btmp[0:1, dst0:dstn], src, AF.Ln, bias=1e-30)
                            scalar.drain()
                            scalar.activation(btmp[0:1, dst0:dstn], btmp[0:1, dst0:dstn],
                                              AF.Exp, scale=0.5, bias=BMARGIN).then_inc(eP.h, 1)
                            eP.inc()

                    @block.tensor
                    def _(tensor):
                        tensor.wait_ge(ePs.h, 2)
                        tensor.matmul(rb_ps[:, 0:64], onesr[:, :], brow[0:1, :],
                                      start=True, stop=True).then_inc(eP.h, 1)
                        eP.inc()

                with nc.Block("p4bb") as block:
                    @block.scalar
                    def _(scalar):
                        scalar.copy(bneg_b[:, :], rb_ps[:, 0:64])

                if tb_h is not None:
                    tB = sem("tB")
                    with nc.Block("p4tap") as block:
                        @block.sync
                        def _(sync):
                            sync.dma_start(out=tb_h.ap(), in_=bneg_b[:, :]).then_inc(tB.h, 16)
                            sync.wait_ge(tB.h, tB.inc(16))

                eC = sem("eC"); eEv = sem("eEv"); eEx = sem("eEx"); eAV = sem("eAV")
                eSm = sem("eSm"); eB2 = sem("eB2"); eSv = sem("eSv"); eNo = sem("eNo")
                eQl = sem("eQl"); eSp4 = [sem(f"eSp{i}") for i in range(4)]
                scr = sb("p4_scr", [128, 4 * 1024], F32)
                for g in range(NG):
                    nch = 4 * g + 4
                    with nc.Block(f"p4g{g}") as block:
                        base_eC = eC.v
                        base_eQl = eQl.v
                        base_eSp4 = [x.v for x in eSp4]
                        base_eEv = eEv.v
                        base_eEx = eEx.v
                        base_eAV = eAV.v
                        base_eSm = eSm.v
                        base_eB2 = eB2.v
                        base_eSv = eSv.v
                        base_eNo = eNo.v

                        @block.sync
                        def _(sync):
                            sync.dma_start(
                                out=qg[:, :],
                                in_=_ap(spq_h, g * NH * 128 * 512,
                                        [[512, 128], [128 * 512, 8], [1, 512]]),
                            ).then_inc(eQl.h, 16)
                            eQl.inc(16)
                            for h in range(NH):
                                if h == 7:
                                    sync.wait_ge(eAV.h, base_eAV + 8)
                                    sync.dma_start(
                                        out=qg[:, :],
                                        in_=_ap(spq_h, (g * NH + 8) * 128 * 512,
                                                [[512, 128], [128 * 512, 8], [1, 512]]),
                                    ).then_inc(eQl.h, 16)
                                    eQl.inc(16)
                                sync.wait_ge(eNo.h, base_eNo + h + 1)
                                sync.dma_start(
                                    out=_ap(spo_h, h * 128 * S + g * 512,
                                            [[S, 128], [1, 512]]),
                                    in_=ogs[:, (h % 4) * 512:(h % 4 + 1) * 512],
                                ).then_inc(eSp4[h % 4].h, 16)
                                eSp4[h % 4].inc(16)
                            for x in eSp4:
                                sync.wait_ge(x.h, x.v)

                        @block.tensor
                        def _(tensor):
                            def bcast(tensor, hb):
                                tensor.wait_ge(eSm.h, base_eSm + hb + 1)
                                mm = tensor.matmul(
                                    rb_ps[:, 0:512], onesr[:, :],
                                    recip[0:1, (hb % 2) * 512:(hb % 2) * 512 + 512],
                                    start=True, stop=True)
                                mm.then_inc(eB2.h, 1)
                                eB2.inc()

                            ich = 0
                            for h in range(NH):
                                kv = h // 4
                                if h % 8 == 0:
                                    tensor.wait_ge(eQl.h, base_eQl + 16 * (h // 8 + 1))
                                if h >= 2:
                                    tensor.wait_ge(eNo.h, base_eNo + h - 1)
                                hbase = ich
                                pbase = ich // 2
                                for j in range(nch):
                                    pr = ich // 2
                                    if pr >= 2 and ich % 2 == 0:
                                        tensor.wait_ge(eEv.h, base_eEv + pr - 1)
                                    po = (pr % 2) * 1024 + (ich % 2) * 512
                                    last = tensor.matmul(
                                        sc_ps[:, po:po + 512],
                                        kfm[:, kv * S + j * 128: kv * S + (j + 1) * 128],
                                        qg[:, (h % 8) * 512:(h % 8 + 1) * 512],
                                        start=True, stop=(h >= NGH))
                                    if h < NGH:
                                        oc = h // 4
                                        last = tensor.matmul(
                                            sc_ps[:, po:po + 512],
                                            rku[:, h * S + j * 128: h * S + (j + 1) * 128],
                                            rqs[:, oc * S + g * 512: oc * S + (g + 1) * 512],
                                            start=False, stop=(h >= 4))
                                    if h < 4:
                                        last = tensor.matmul(
                                            sc_ps[:, po:po + 512],
                                            rqu[:, h * S + j * 128: h * S + (j + 1) * 128],
                                            rksh[:, g * 512:(g + 1) * 512],
                                            start=False, stop=True)
                                    last.then_inc(eC.h, 1)
                                    eC.inc()
                                    ich += 1
                                    if j >= 6:
                                        jj = j - 6
                                        tensor.wait_ge(eEx.h, base_eEx + pbase + jj // 2 + 1)
                                        tensor.matmul(
                                            o_ps[:, (h % 2) * 512:(h % 2) * 512 + 512],
                                            vsb[:, jj * 512 + kv * 128: jj * 512 + (kv + 1) * 128],
                                            ptb[:, jj * 512:(jj + 1) * 512],
                                            start=(jj == 0), stop=False)
                                        if jj % 4 == 3:
                                            if jj == 3 and h >= 1:
                                                tensor.wait_ge(eSv.h, base_eSv + h)
                                            for js in range(jj - 3, jj + 1):
                                                tensor.matmul(
                                                    sm_ps[0:1, 0:512], onesc[:, :],
                                                    ptb[:, js * 512:(js + 1) * 512],
                                                    start=(js == 0), stop=False)
                                for jj in range(max(0, nch - 6), nch):
                                    tensor.wait_ge(eEx.h, base_eEx + pbase + jj // 2 + 1)
                                    tensor.matmul(
                                        o_ps[:, (h % 2) * 512:(h % 2) * 512 + 512],
                                        vsb[:, jj * 512 + kv * 128: jj * 512 + (kv + 1) * 128],
                                        ptb[:, jj * 512:(jj + 1) * 512],
                                        start=(jj == 0), stop=(jj == nch - 1))
                                    if jj % 4 == 3:
                                        if jj == 3 and h >= 1:
                                            tensor.wait_ge(eSv.h, base_eSv + h)
                                        for js in range(jj - 3, jj + 1):
                                            mm2 = tensor.matmul(
                                                sm_ps[0:1, 0:512], onesc[:, :],
                                                ptb[:, js * 512:(js + 1) * 512],
                                                start=(js == 0), stop=(js == nch - 1))
                                mm2.then_inc(eAV.h, 1)
                                eAV.inc()
                                if h >= 1:
                                    bcast(tensor, h - 1)
                            bcast(tensor, NH - 1)

                        @block.vector
                        def _(vector):
                            def sums_evac(vector, hb):
                                vector.wait_ge(eAV.h, base_eAV + hb + 1)
                                vector.tensor_copy(
                                    sums_sb[0:1, (hb % 2) * 512:(hb % 2) * 512 + 512],
                                    sm_ps[0:1, 0:512]).then_inc(eSv.h, 1)
                                eSv.inc()

                            def tail(vector, hb):
                                vector.wait_ge(eB2.h, base_eB2 + hb + 1)
                                if hb >= 4:
                                    vector.wait_ge(eSp4[hb % 4].h,
                                                   base_eSp4[hb % 4] + 16 * ((hb - 4) // 4 + 1))
                                vector.tensor_copy(
                                    rb_sb[:, (hb % 2) * 512:(hb % 2) * 512 + 512],
                                    rb_ps[:, 0:512])
                                vector.drain()
                                vector.tensor_tensor(
                                    ogs[:, (hb % 4) * 512:(hb % 4 + 1) * 512],
                                    o_ps[:, (hb % 2) * 512:(hb % 2) * 512 + 512],
                                    rb_sb[:, (hb % 2) * 512:(hb % 2) * 512 + 512],
                                    ALU.mult).then_inc(eNo.h, 1)
                                eNo.inc()

                            ich = 0
                            for h in range(NH):
                                if h >= 1:
                                    sums_evac(vector, h - 1)
                                for j in range(0, nch, 2):
                                    pr = ich // 2
                                    if pr >= 4:
                                        vector.wait_ge(eEx.h, base_eEx + pr - 3)
                                    po = (pr % 2) * 1024
                                    so = (pr % 4) * 1024
                                    if j + 1 < 4 * g:
                                        vector.wait_ge(eC.h, base_eC + ich + 2)
                                        vector.tensor_copy(
                                            scr[:, so:so + 1024], sc_ps[:, po:po + 1024]
                                        ).then_inc(eEv.h, 1)
                                        eEv.inc()
                                        ich += 2
                                        continue
                                    for jc in (j, j + 1):
                                        vector.wait_ge(eC.h, base_eC + ich + 1)
                                        poc = po + (ich % 2) * 512
                                        soc = so + (ich % 2) * 512
                                        if jc >= 4 * g:
                                            k = jc - 4 * g
                                            op = vector.tensor_tensor(
                                                scr[:, soc:soc + 512], sc_ps[:, poc:poc + 512],
                                                masks[:, k * 512:(k + 1) * 512], ALU.add)
                                        else:
                                            op = vector.tensor_copy(
                                                scr[:, soc:soc + 512], sc_ps[:, poc:poc + 512])
                                        if ich % 2 == 1:
                                            op.then_inc(eEv.h, 1)
                                            eEv.inc()
                                        ich += 1
                                if h >= 1:
                                    tail(vector, h - 1)
                            sums_evac(vector, NH - 1)
                            tail(vector, NH - 1)

                        @block.scalar
                        def _(scalar):
                            def recip_chain(scalar, hb):
                                if hb >= 2:
                                    scalar.wait_ge(eB2.h, base_eB2 + hb - 1)
                                scalar.wait_ge(eSv.h, base_eSv + hb + 1)
                                so = (hb % 2) * 512
                                scalar.activation(recip[0:1, so:so + 512],
                                                  sums_sb[0:1, so:so + 512], AF.Ln)
                                scalar.drain()
                                scalar.activation(recip[0:1, so:so + 512],
                                                  recip[0:1, so:so + 512], AF.Exp,
                                                  scale=-1.0).then_inc(eSm.h, 1)
                                eSm.inc()

                            ich = 0
                            for h in range(NH):
                                for p in range(nch // 2):
                                    pr = ich // 2
                                    scalar.wait_ge(eEv.h, base_eEv + pr + 1)
                                    j0 = 2 * p
                                    col = 4 * h + g
                                    scalar.activation(
                                        ptb[:, j0 * 512:(j0 + 2) * 512],
                                        scr[:, (pr % 4) * 1024:(pr % 4) * 1024 + 1024],
                                        AF.Exp,
                                        bias=bneg_b[:, col:col + 1]).then_inc(eEx.h, 1)
                                    eEx.inc()
                                    ich += 2
                                if h >= 1:
                                    recip_chain(scalar, h - 1)
                            recip_chain(scalar, NH - 1)

        if go("p5"):
            with ExitStack() as S5:
                sfm = S5.enter_context(nc.sbuf_tensor("p5_sfm", [128, NEC * S], BF16))
                S5a = S5.enter_context(ExitStack())
                sb = lambda n, sh, dt: S5a.enter_context(nc.sbuf_tensor(n, sh, dt))
                osb = sb("p5_osb", [128, NH * S], BF16)
                wob = sb("p5_wob", [128, 2 * E], BF16)
                sps = S5a.enter_context(nc.psum_tensor("p5_sps", [128, 2 * 2048], F32))
                fR = sem("fR"); fWp = [sem("fWa"), sem("fWb")]; fM = sem("fM"); fE = sem("fE")
                with nc.Block("p5") as block:
                    @block.sync
                    def _(sync):
                        sync.dma_start(
                            out=osb[:, :],
                            in_=_ap(spo_h, 0, [[S, 128], [128 * S, NH], [1, S]]),
                        ).then_inc(fR.h, 16)
                        fR.inc(16)

                    @block.gpsimd
                    def _(gpsimd):
                        for oc in range(NEC):
                            if oc >= 2:
                                gpsimd.wait_ge(fM.h, oc - 1)
                            gpsimd.dma_start(
                                out=wob[:, (oc % 2) * E:(oc % 2) * E + E],
                                in_=_ap(wo_h, oc * E * 128,
                                        [[128, 128], [128 * 128, NEC], [1, 128]]),
                            ).then_inc(fWp[oc % 2].h, 16)
                            fWp[oc % 2].inc(16)

                    @block.tensor
                    def _(tensor):
                        tensor.wait_ge(fR.h, 16)
                        for oc in range(NEC):
                            tensor.wait_ge(fWp[oc % 2].h, 16 * (oc // 2 + 1))
                            if oc >= 2:
                                tensor.wait_ge(fE.h, oc - 1)
                            po = (oc % 2) * 2048
                            wv_ = wob[:, (oc % 2) * E:(oc % 2) * E + E]
                            for ec in range(NEC):
                                for q4 in range(4):
                                    mm = tensor.matmul(
                                        sps[:, po + q4 * 512:po + (q4 + 1) * 512],
                                        wv_[:, ec * 128:(ec + 1) * 128],
                                        osb[:, ec * S + q4 * 512: ec * S + (q4 + 1) * 512],
                                        start=(ec == 0), stop=(ec == NEC - 1))
                            mm.then_inc(fM.h, 1)
                            fM.inc()

                    @block.scalar
                    def _(scalar):
                        for oc in range(NEC):
                            scalar.wait_ge(fM.h, oc + 1)
                            po = (oc % 2) * 2048
                            scalar.copy(sfm[:, oc * S:(oc + 1) * S], sps[:, po:po + 2048]
                                        ).then_inc(fE.h, 1)
                            fE.inc()

                S5a.close()

                if ts_h is not None:
                    tS2 = sem("tS2")
                    with nc.Block("p5tap") as block:
                        @block.sync
                        def _(sync):
                            sync.dma_start(out=ts_h.ap(), in_=sfm[:, :]).then_inc(tS2.h, 16)
                            sync.wait_ge(tS2.h, tS2.inc(16))

                if go("p6"):
                    with ExitStack() as S6:
                        sb6 = lambda n, sh, dt: S6.enter_context(nc.sbuf_tensor(n, sh, dt))
                        gsb = sb6("p6_gsb", [128, NEC * S], BF16)
                        xt = sb6("p6_xt", [128, 2 * E], F32)
                        ssb = sb6("p6_ssb", [128, 2 * E], BF16)
                        psb = sb6("p6_psb", [128, E], F32)
                        outb = sb6("p6_outb", [128, 2 * E], F32)
                        s_ps = S6.enter_context(nc.psum_tensor("p6_sps", [128, 2 * E], BF16))
                        g_ps = S6.enter_context(nc.psum_tensor("p6_gps", [128, 2 * E], BF16))
                        gR = sem("gR"); gXp = [sem("gXa"), sem("gXb")]
                        gT = sem("gT"); gE = sem("gE")
                        gV = sem("gV"); gStp = [sem("gSta"), sem("gStb")]
                        with nc.Block("p6") as block:
                            @block.sync
                            def _(sync):
                                sync.dma_start(
                                    out=gsb[:, :],
                                    in_=_ap(spg_h, 0, [[S, 128], [128 * S, NEC], [1, S]]),
                                ).then_inc(gR.h, 16)
                                gR.inc(16)
                                for tt in range(NTT):
                                    if tt >= 2:
                                        sync.wait_ge(gV.h, 2 * (tt - 1))
                                    sync.dma_start(
                                        out=xt[:, (tt % 2) * E:(tt % 2) * E + E],
                                        in_=_ap(x_h, tt * 128 * E, [[E, 128], [1, E]]),
                                    ).then_inc(gXp[tt % 2].h, 16)
                                    gXp[tt % 2].inc(16)
                                    if tt >= 2:
                                        sync.wait_ge(gV.h, 2 * (tt - 1))
                                        sync.dma_start(
                                            out=_ap(y_h, (tt - 2) * 128 * E, [[E, 128], [1, E]]),
                                            in_=outb[:, (tt % 2) * E:(tt % 2) * E + E],
                                        ).then_inc(gStp[tt % 2].h, 16)
                                        gStp[tt % 2].inc(16)
                                for tt in (NTT - 2, NTT - 1):
                                    sync.wait_ge(gV.h, 2 * (tt + 1))
                                    sync.dma_start(
                                        out=_ap(y_h, tt * 128 * E, [[E, 128], [1, E]]),
                                        in_=outb[:, (tt % 2) * E:(tt % 2) * E + E],
                                    ).then_inc(gStp[tt % 2].h, 16)
                                    gStp[tt % 2].inc(16)
                                sync.wait_ge(gStp[0].h, gStp[0].v)
                                sync.wait_ge(gStp[1].h, gStp[1].v)

                            @block.tensor
                            def _(tensor):
                                tensor.wait_ge(gR.h, 16)
                                for tt in range(NTT):
                                    if tt >= 2:
                                        tensor.wait_ge(gV.h, 2 * (tt - 1))
                                    po = (tt % 2) * E
                                    for oc in range(NEC):
                                        tensor.transpose(
                                            s_ps[:, po + oc * 128:po + (oc + 1) * 128],
                                            sfm[:, oc * S + tt * 128: oc * S + (tt + 1) * 128],
                                            ident[:, :])
                                    for oc in range(NEC):
                                        mm = tensor.transpose(
                                            g_ps[:, po + oc * 128:po + (oc + 1) * 128],
                                            gsb[:, oc * S + tt * 128: oc * S + (tt + 1) * 128],
                                            ident[:, :])
                                    mm.then_inc(gT.h, 1)
                                    gT.inc()

                            @block.scalar
                            def _(scalar):
                                for tt in range(NTT):
                                    scalar.wait_ge(gT.h, tt + 1)
                                    if tt >= 2:
                                        scalar.wait_ge(gV.h, 2 * (tt - 2) + 1)
                                    po = (tt % 2) * E
                                    scalar.copy(ssb[:, po:po + E], s_ps[:, po:po + E]
                                                ).then_inc(gE.h, 1)
                                    gE.inc()

                            @block.vector
                            def _(vector):
                                for tt in range(NTT):
                                    po = (tt % 2) * E
                                    vector.wait_ge(gE.h, tt + 1)
                                    vector.drain()
                                    vector.tensor_tensor(psb[:, :], g_ps[:, po:po + E],
                                                         ssb[:, po:po + E], ALU.mult).then_inc(gV.h, 1)
                                    gV.inc()
                                    vector.wait_ge(gXp[tt % 2].h, 16 * (tt // 2 + 1))
                                    if tt >= 2:
                                        vector.wait_ge(gStp[tt % 2].h, 16 * (tt // 2))
                                    vector.drain()
                                    vector.tensor_tensor(outb[:, po:po + E], psb[:, :],
                                                         xt[:, po:po + E], ALU.add).then_inc(gV.h, 1)
                                    gV.inc()

    return nc


def _prep_host(inputs):
    f32 = np.float32
    g = {}
    Wq = np.asarray(inputs["L0_attn_Wq"], f32)
    Wk = np.asarray(inputs["L0_attn_Wk"], f32)
    Wv = np.asarray(inputs["L0_attn_Wv"], f32)
    Wo = np.asarray(inputs["L0_attn_Wo"], f32)
    Wrq = np.asarray(inputs["L0_attn_Wrq"], f32)
    Wrk = np.asarray(inputs["L0_attn_Wrk"], f32)
    gw = np.asarray(inputs["L0_ffn_gate_w"], f32)
    uw = np.asarray(inputs["L0_ffn_up_w"], f32)
    dw = np.asarray(inputs["L0_ffn_down_w"], f32)

    stripes = lambda WT, n: np.ascontiguousarray(
        WT.reshape(WT.shape[0], n, 128).transpose(1, 0, 2))
    g["wq"] = stripes(Wq.T, NH)
    g["wk"] = stripes(Wk.T, NKV)
    g["wv"] = np.ascontiguousarray(Wv.T)
    g["wo"] = stripes(Wo.T, NEC)
    g["wrq"] = stripes(Wrq.T, 3)
    g["wrk"] = stripes(Wrk.T, 3)
    g["wg"] = stripes(gw.T, NIC)
    g["wu"] = stripes(uw.T, NIC)
    g["wd"] = np.ascontiguousarray(
        dw.T.reshape(NIC, 128, 4, 512).transpose(2, 0, 1, 3))
    g["qnw"] = np.ascontiguousarray(
        np.asarray(inputs["L0_attn_q_norm_w"], f32).reshape(128, 1))
    g["knw"] = np.ascontiguousarray(
        np.asarray(inputs["L0_attn_k_norm_w"], f32).reshape(128, 1))
    gs = np.asarray(inputs["L0_attn_graph_scale"], f32)
    g["gsrep"] = np.ascontiguousarray(np.repeat(gs, 32).reshape(3, 128))
    g["gsrow"] = np.ascontiguousarray(gs.reshape(1, 12))
    g["gates_c"] = np.ascontiguousarray(
        np.asarray(inputs["L0_ffn_gate_s"], f32).reshape(NIC, 128).T)
    g["ups_c"] = np.ascontiguousarray(
        np.asarray(inputs["L0_ffn_up_s"], f32).reshape(NIC, 128).T)
    g["downs_c"] = np.ascontiguousarray(
        np.asarray(inputs["L0_ffn_down_s"], f32).reshape(NEC, 128).T)
    g["ident"] = np.eye(128, dtype=ml_dtypes.bfloat16)
    rt = np.zeros((128, 128), dtype=ml_dtypes.bfloat16)
    for d in range(64):
        rt[d + 64, d] = -1.0
        rt[d, d + 64] = 1.0
    g["rtm"] = rt
    t = np.arange(S, dtype=np.float64)
    invf = 1.0 / (10000.0 ** (np.arange(0, 128, 2, dtype=np.float64) / 128.0))
    fr = np.outer(t, invf)
    emb = np.concatenate([fr, fr], axis=-1)
    g["cosf"] = np.ascontiguousarray(np.cos(emb).T.astype(f32))
    g["sinf"] = np.ascontiguousarray(np.sin(emb).T.astype(f32))
    m = np.zeros((128, 4 * 512), f32)
    p = np.arange(128)[:, None]
    f = np.arange(512)[None, :]
    for k in range(4):
        m[:, k * 512:(k + 1) * 512] = np.where(128 * k + p > f, NEG_BIG, 0.0)
    g["masks"] = m
    g["ones_col"] = np.ones((128, 1), dtype=ml_dtypes.bfloat16)
    g["ones_rowf"] = np.ones((1, 128), f32)
    b4 = np.zeros((128, 4), dtype=ml_dtypes.bfloat16)
    for l in range(4):
        b4[32 * l:32 * (l + 1), l] = 1.0
    g["blk4"] = b4
    return g


def kernel(**inputs):
    shared = _prep_host(inputs)
    x = np.asarray(inputs["L0_attn_x"], np.float32)
    nc = build_program()
    in_maps = []
    for c in range(8):
        m = dict(shared)
        m["x"] = np.ascontiguousarray(x[c])
        in_maps.append(m)
    res = run_bass_kernel_spmd(nc, in_maps, core_ids=list(range(8)))
    return np.stack([res.results[c]["y"] for c in range(8)], axis=0)


# version 15 (final)
# speedup vs baseline: 1.1041x; 1.0013x over previous
import os
import numpy as np
import ml_dtypes
from contextlib import ExitStack

import concourse.bass as bass
import concourse.mybir as mybir
from concourse.bass_utils import run_bass_kernel_spmd

F32 = mybir.dt.float32
BF16 = mybir.dt.bfloat16
AF = mybir.ActivationFunctionType
ALU = mybir.AluOpType
AX = mybir.AxisListType

S = 2048
E = 2048
NH = 16
NKV = 4
NGH = 12
RR = 32
INTER = 5632
NIC = INTER // 128
NEC = E // 128
NTT = S // 128
NG = 4
EPS_LN = 1e-5
EPS_RMS = 1e-6
INV_SQRT_D = float(1.0 / np.sqrt(128.0))
NEG_BIG = -1.0e38
BMARGIN = 0.006


class Sem:
    def __init__(self, ctx, nc, name):
        self.h = ctx.enter_context(nc.semaphore(name))
        self.v = 0

    def inc(self, n=1):
        self.v += n
        return self.v


def _ap(h, off, dims):
    return bass.AP(h, off, dims)


def build_program(taps=(), stop_after=None):
    PH = ["p0", "p1v", "p1k", "p1r", "p1", "p2", "p3", "p4", "p5", "p6"]
    go = lambda p: stop_after is None or PH.index(p) <= PH.index(stop_after)
    taps = set(taps)
    nc = bass.Bass("TRN2", target_bir_lowering=False)

    def dram(name, shape, dt, kind):
        return nc.dram_tensor(name, shape, dt, kind=kind)

    x_h = dram("x", [S, E], F32, "ExternalInput")
    wq_h = dram("wq", [NH, E, 128], F32, "ExternalInput")
    wk_h = dram("wk", [NKV, E, 128], F32, "ExternalInput")
    wv_h = dram("wv", [E, NKV * 128], F32, "ExternalInput")
    wo_h = dram("wo", [NEC, E, 128], F32, "ExternalInput")
    wrq_h = dram("wrq", [3, E, 128], F32, "ExternalInput")
    wrk_h = dram("wrk", [3, E, 128], F32, "ExternalInput")
    wg_h = dram("wg", [NIC, E, 128], F32, "ExternalInput")
    wu_h = dram("wu", [NIC, E, 128], F32, "ExternalInput")
    wd_h = dram("wd", [4, NIC, 128, 512], F32, "ExternalInput")
    qnw_h = dram("qnw", [128, 1], F32, "ExternalInput")
    knw_h = dram("knw", [128, 1], F32, "ExternalInput")
    gsrep_h = dram("gsrep", [3, 128], F32, "ExternalInput")
    gsrow_h = dram("gsrow", [1, 12], F32, "ExternalInput")
    gates_h = dram("gates_c", [128, NIC], F32, "ExternalInput")
    ups_h = dram("ups_c", [128, NIC], F32, "ExternalInput")
    downs_h = dram("downs_c", [128, NEC], F32, "ExternalInput")
    ident_h = dram("ident", [128, 128], BF16, "ExternalInput")
    rtm_h = dram("rtm", [128, 128], BF16, "ExternalInput")
    cosf_h = dram("cosf", [128, S], F32, "ExternalInput")
    sinf_h = dram("sinf", [128, S], F32, "ExternalInput")
    masks_h = dram("masks", [128, 4 * 512], F32, "ExternalInput")
    onesc_h = dram("ones_col", [128, 1], BF16, "ExternalInput")
    onesr_h = dram("ones_rowf", [1, 128], F32, "ExternalInput")
    blk4_h = dram("blk4", [128, 4], BF16, "ExternalInput")

    y_h = dram("y", [S, E], F32, "ExternalOutput")

    def spill(name, shape, tapname):
        kind = "ExternalOutput" if tapname in taps else "Internal"
        return dram(name, shape, BF16, kind)

    spk_h = spill("sp_k", [NKV, 128, S], "k")
    spv_h = spill("sp_v", [128, NTT * 512], "v")
    sprku_h = spill("sp_rku", [3, 128, S], "r")
    sprqs_h = spill("sp_rqs", [3, 128, S], "r")
    sprqu_h = spill("sp_rqu", [128, S], "r")
    sprksh_h = spill("sp_rksh", [128, S], "r")
    spq_h = spill("sp_q", [NG, NH, 128, 512], "q")
    spg_h = spill("sp_g", [NEC, 128, S], "g")
    spo_h = spill("sp_o", [NH, 128, S], "o")
    txn_h = dram("t_xn", [128, NEC * S], BF16, "ExternalOutput") if "xn" in taps else None
    ts_h = dram("t_s", [128, NEC * S], BF16, "ExternalOutput") if "s" in taps else None
    tb_h = dram("t_b", [128, 64], F32, "ExternalOutput") if "b" in taps else None

    with ExitStack() as L0:
        sbuf = lambda n, sh, dt: L0.enter_context(nc.sbuf_tensor(n, sh, dt))
        sem = lambda n: Sem(L0, nc, n)

        ident = sbuf("c_ident", [128, 128], BF16)
        rtm = sbuf("c_rtm", [128, 128], BF16)
        onesc = sbuf("c_onesc", [128, 1], BF16)
        onesr = sbuf("c_onesr", [1, 128], F32)
        blk4 = sbuf("c_blk4", [128, 4], BF16)
        qnw = sbuf("c_qnw", [128, 1], F32)
        knw = sbuf("c_knw", [128, 1], F32)
        qnw_eff = sbuf("c_qnweff", [128, 1], F32)
        gs_cols = sbuf("c_gscols", [128, 3], F32)
        gs_half = sbuf("c_gshalf", [128, 1], F32)
        gsrow = sbuf("c_gsrow", [1, 12], F32)
        gates_c = sbuf("c_gates", [128, NIC], F32)
        ups_c = sbuf("c_ups", [128, NIC], F32)
        downs_c = sbuf("c_downs", [128, NEC], F32)
        actcst = sbuf("c_actcst", [128, 8], F32)
        maxq2 = sbuf("st_maxq2", [1, 64], F32)
        kvmax2 = sbuf("st_kvmax2", [1, 8], F32)
        rqmax2 = sbuf("st_rqmax2", [4, 6], F32)
        rkmax2 = sbuf("st_rkmax2", [4, 6], F32)
        bneg_b = sbuf("st_bnegb", [128, 64], F32)

        iL = sem("iL")
        with nc.Block("init") as block:
            @block.sync
            def _(sync):
                for dst, src in (
                    (ident, ident_h), (rtm, rtm_h), (onesc, onesc_h),
                    (onesr, onesr_h), (blk4, blk4_h), (qnw, qnw_h),
                    (knw, knw_h), (gsrow, gsrow_h), (gates_c, gates_h),
                    (ups_c, ups_h), (downs_c, downs_h),
                ):
                    sync.dma_start(out=dst[:, :], in_=src.ap()).then_inc(iL.h, 16)
                    iL.inc(16)
                sync.dma_start(
                    out=gs_cols[:, :], in_=_ap(gsrep_h, 0, [[1, 128], [128, 3]]),
                    allow_slow_non_contiguous=True,
                ).then_inc(iL.h, 16)
                iL.inc(16)

            @block.vector
            def _(vector):
                for i, v in enumerate(
                    (EPS_LN, EPS_RMS, 1e-30, -0.5, 0.5, -1.0, BMARGIN, 1.0 / 128.0)
                ):
                    vector.memset(actcst[:, i:i + 1], v)
                    nc.const_aps.register(F32, v, actcst[:, i:i + 1])
                vector.wait_ge(iL.h, iL.v)
                vector.tensor_scalar(qnw_eff[:, :], qnw[:, :], INV_SQRT_D, None, ALU.mult)
                vector.tensor_scalar(gs_half[:, :], gs_cols[:, 0:1], 0.5, None, ALU.mult)

        with ExitStack() as SA:
            xn = SA.enter_context(nc.sbuf_tensor("xn_fm", [128, NEC * S], BF16))

            def xn_cols(ec, c0, n):
                return xn[:, ec * S + c0: ec * S + c0 + n]

            with ExitStack() as SP:
                sb = lambda n, sh, dt: SP.enter_context(nc.sbuf_tensor(n, sh, dt))
                xt = sb("p0_xt", [128, 2 * E], F32)
                xnt = sb("p0_xnt", [128, E], BF16)
                xsq = sb("p0_xsq", [128, E], F32)
                st0 = sb("p0_st", [128, 8], F32)
                tp_ps = SP.enter_context(nc.psum_tensor("p0_tp", [128, E], BF16))
                aLp = [sem("aLa"), sem("aLb")]; aS = sem("aS"); aR = sem("aR")
                aX = sem("aX"); aT = sem("aT"); aE = sem("aE")
                with nc.Block("p0") as block:
                    @block.sync
                    def _(sync):
                        for tt in range(NTT):
                            if tt >= 2:
                                sync.wait_ge(aX.h, tt - 1)
                            sync.dma_start(
                                out=xt[:, (tt % 2) * E:(tt % 2) * E + E],
                                in_=_ap(x_h, tt * 128 * E, [[E, 128], [1, E]]),
                            ).then_inc(aLp[tt % 2].h, 16)
                            aLp[tt % 2].inc(16)

                    @block.vector
                    def _(vector):
                        for tt in range(NTT):
                            xv = xt[:, (tt % 2) * E:(tt % 2) * E + E]
                            vector.wait_ge(aLp[tt % 2].h, 16 * (tt // 2 + 1))
                            vector.reduce_sum(st0[:, 0:1], xv, AX.X)
                            vector.scalar_tensor_tensor(
                                xsq[:, :], xv, 1.0, xv, ALU.bypass, ALU.mult,
                                accum_out=st0[:, 1:2])
                            vector.drain()
                            vector.tensor_scalar(st0[:, 2:3], st0[:, 0:1], 1.0 / E, None, ALU.mult)
                            vector.tensor_scalar(st0[:, 4:5], st0[:, 1:2], 1.0 / E, None, ALU.mult)
                            vector.drain()
                            vector.tensor_tensor(st0[:, 3:4], st0[:, 2:3], st0[:, 2:3], ALU.mult)
                            vector.drain()
                            vector.tensor_tensor(st0[:, 5:6], st0[:, 4:5], st0[:, 3:4], ALU.subtract).then_inc(aS.h, 1)
                            aS.inc()
                            vector.wait_ge(aR.h, tt + 1)
                            if tt >= 1:
                                vector.wait_ge(aT.h, tt)
                            vector.tensor_scalar(
                                xnt[:, :], xv, st0[:, 2:3], st0[:, 6:7],
                                ALU.subtract, ALU.mult).then_inc(aX.h, 1)
                            aX.inc()

                    @block.scalar
                    def _(scalar):
                        for tt in range(NTT):
                            scalar.wait_ge(aS.h, tt + 1)
                            if tt >= 1:
                                scalar.wait_ge(aX.h, tt)
                            scalar.activation(st0[:, 7:8], st0[:, 5:6], AF.Ln, bias=EPS_LN)
                            scalar.drain()
                            scalar.activation(st0[:, 6:7], st0[:, 7:8], AF.Exp, scale=-0.5).then_inc(aR.h, 1)
                            aR.inc()
                            if tt >= 1:
                                scalar.wait_ge(aT.h, tt)
                                dst = _ap(xn, (tt - 1) * 128,
                                          [[NEC * S, 128], [S, NEC], [1, 128]])
                                src = _ap(tp_ps, 0, [[E, 128], [128, NEC], [1, 128]])
                                scalar.copy(dst, src).then_inc(aE.h, 1)
                                aE.inc()
                        scalar.wait_ge(aT.h, NTT)
                        dst = _ap(xn, (NTT - 1) * 128,
                                  [[NEC * S, 128], [S, NEC], [1, 128]])
                        src = _ap(tp_ps, 0, [[E, 128], [128, NEC], [1, 128]])
                        scalar.copy(dst, src).then_inc(aE.h, 1)
                        aE.inc()

                    @block.tensor
                    def _(tensor):
                        for tt in range(NTT):
                            tensor.wait_ge(aX.h, tt + 1)
                            tensor.wait_ge(aE.h, tt)
                            for j in range(NEC):
                                mm = tensor.transpose(
                                    tp_ps[:, j * 128:(j + 1) * 128],
                                    xnt[:, j * 128:(j + 1) * 128],
                                    ident[:, :])
                            mm.then_inc(aT.h, 1)
                            aT.inc()

            if txn_h is not None:
                tS = sem("tSxn")
                with nc.Block("tapxn") as block:
                    @block.sync
                    def _(sync):
                        sync.dma_start(out=txn_h.ap(), in_=xn[:, :]).then_inc(tS.h, 16)
                        sync.wait_ge(tS.h, tS.inc(16))

            if go("p1v"):
                with ExitStack() as SP:
                    sb = lambda n, sh, dt: SP.enter_context(nc.sbuf_tensor(n, sh, dt))
                    kfm = sb("p1_kfm", [128, NKV * S], BF16)
                    vsb = sb("p1_vsb", [128, NTT * 512], BF16)
                    rqs = sb("p1_rqs", [128, 3 * S], BF16)
                    rku = sb("p1_rku", [128, 3 * S], BF16)
                    rqu = sb("p1_rqu", [128, S], BF16)
                    rksh = sb("p1_rksh", [128, S], BF16)
                    cosf = sb("p1_cosf", [128, S], F32)
                    sinf = sb("p1_sinf", [128, S], F32)
                    wvb = sb("p1_wvb", [128, NEC * 512], BF16)
                    wbuf = sb("p1_wbuf", [128, 2 * E], BF16)
                    ksq = sb("p1_ksq", [128, 1024], BF16)
                    rs_row = sb("p1_rsrow", [1, 1024], F32)
                    rsb_sb = sb("p1_rsb", [128, 1024], F32)
                    khat = sb("p1_khat", [128, 1024], BF16)
                    t1 = sb("p1_t1", [128, 1024], F32)
                    t2 = sb("p1_t2", [128, 1024], F32)

                    bL = sem("bL"); bWv = sem("bWv")
                    bWk = [sem("bWka"), sem("bWkb")]
                    bWr = [sem("bWra"), sem("bWrb")]
                    SPv = SP.enter_context(ExitStack())
                    vps = SPv.enter_context(nc.psum_tensor("p1_vps", [128, 2 * 512], F32))
                    bVM = sem("bVM"); bVE = sem("bVE")
                    if go('p1v'):
                        with nc.Block("p1v") as block:
                            @block.sync
                            def _(sync):
                                sync.dma_start(out=cosf[:, :], in_=cosf_h.ap()).then_inc(bL.h, 16)
                                sync.dma_start(out=sinf[:, :], in_=sinf_h.ap()).then_inc(bL.h, 16)
                                bL.inc(32)
                                sync.wait_ge(bL.h, bL.v)

                            @block.gpsimd
                            def _(gpsimd):
                                gpsimd.dma_start(
                                    out=wvb[:, :],
                                    in_=_ap(wv_h, 0, [[512, 128], [512 * 128, NEC], [1, 512]]),
                                ).then_inc(bWv.h, 16)
                                bWv.inc(16)

                            @block.tensor
                            def _(tensor):
                                tensor.wait_ge(bWv.h, 16)
                                for tt in range(NTT):
                                    if tt >= 2:
                                        tensor.wait_ge(bVE.h, tt - 1)
                                    po = (tt % 2) * 512
                                    for ec in range(NEC):
                                        mm = tensor.matmul(
                                            vps[:, po:po + 512],
                                            xn_cols(ec, tt * 128, 128),
                                            wvb[:, ec * 512:(ec + 1) * 512],
                                            start=(ec == 0), stop=(ec == NEC - 1))
                                    mm.then_inc(bVM.h, 1)
                                    bVM.inc()

                            @block.scalar
                            def _(scalar):
                                for tt in range(NTT):
                                    scalar.wait_ge(bVM.h, tt + 1)
                                    po = (tt % 2) * 512
                                    scalar.copy(
                                        vsb[:, tt * 512:(tt + 1) * 512], vps[:, po:po + 512]
                                    ).then_inc(bVE.h, 1)
                                    bVE.inc()

                    SPv.close()
                    kps = SP.enter_context(nc.psum_tensor("p1_kps", [128, 2 * 1024], F32))
                    sums_ps = SP.enter_context(nc.psum_tensor("p1_sums", [4, 1024], F32))
                    rot_ps = SP.enter_context(nc.psum_tensor("p1_rot", [128, 1024], F32))
                    bM = sem("bM"); bQ = sem("bQ"); bS = sem("bS"); bR = sem("bR")
                    bB = sem("bB"); bE = sem("bE"); bH = sem("bH"); bO = sem("bO")
                    bF = sem("bF"); bN1 = sem("bN1"); bNS = sem("bNS"); bNX = sem("bNX")

                    if go('p1k'):
                        with nc.Block("p1k") as block:
                            units = [(oc, hh) for oc in range(NKV) for hh in range(2)]

                            @block.gpsimd
                            def _(gpsimd):
                                for oc in range(NKV):
                                    if oc >= 2:
                                        gpsimd.wait_ge(bM.h, 2 * (oc - 1))
                                    gpsimd.dma_start(
                                        out=wbuf[:, (oc % 2) * E:(oc % 2) * E + E],
                                        in_=_ap(wk_h, oc * E * 128,
                                                [[128, 128], [128 * 128, NEC], [1, 128]]),
                                    ).then_inc(bWk[oc % 2].h, 16)
                                    bWk[oc % 2].inc(16)

                            def _kpost(tensor, u):
                                tensor.wait_ge(bQ.h, u + 1)
                                if u >= 1:
                                    tensor.wait_ge(bNX.h, u)
                                for q4 in range(2):
                                    mm = tensor.matmul(
                                        sums_ps[0:1, q4 * 512:(q4 + 1) * 512],
                                        onesc[:, :], ksq[:, q4 * 512:(q4 + 1) * 512],
                                        start=True, stop=True)
                                mm.then_inc(bS.h, 1)
                                bS.inc()
                                tensor.wait_ge(bR.h, u + 1)
                                for q4 in range(2):
                                    mm = tensor.matmul(
                                        rot_ps[:, q4 * 512:(q4 + 1) * 512],
                                        onesr[:, :], rs_row[0:1, q4 * 512:(q4 + 1) * 512],
                                        start=True, stop=True)
                                mm.then_inc(bB.h, 1)
                                bB.inc()
                                tensor.wait_ge(bH.h, u + 1)
                                for q4 in range(2):
                                    mm = tensor.matmul(
                                        rot_ps[:, q4 * 512:(q4 + 1) * 512],
                                        rtm[:, :], khat[:, q4 * 512:(q4 + 1) * 512],
                                        start=True, stop=True)
                                mm.then_inc(bO.h, 1)
                                bO.inc()
                                tensor.wait_ge(bN1.h, u + 1)
                                for q4 in range(2):
                                    mm = tensor.matmul(
                                        sums_ps[0:1, q4 * 512:(q4 + 1) * 512],
                                        onesc[:, :], ksq[:, q4 * 512:(q4 + 1) * 512],
                                        start=True, stop=True)
                                mm.then_inc(bNS.h, 1)
                                bNS.inc()

                            @block.tensor
                            def _(tensor):
                                for u, (oc, hh) in enumerate(units):
                                    tensor.wait_ge(bWk[oc % 2].h, 16 * (oc // 2 + 1))
                                    if u >= 2:
                                        tensor.wait_ge(bH.h, u - 1)
                                    po = (u % 2) * 1024
                                    wv_ = wbuf[:, (oc % 2) * E:(oc % 2) * E + E]
                                    for ec in range(NEC):
                                        for q4 in range(2):
                                            mm = tensor.matmul(
                                                kps[:, po + q4 * 512:po + (q4 + 1) * 512],
                                                wv_[:, ec * 128:(ec + 1) * 128],
                                                xn_cols(ec, hh * 1024 + q4 * 512, 512),
                                                start=(ec == 0), stop=(ec == NEC - 1))
                                    mm.then_inc(bM.h, 1)
                                    bM.inc()
                                    if u >= 1:
                                        _kpost(tensor, u - 1)
                                _kpost(tensor, len(units) - 1)


                            @block.vector
                            def _(vector):
                                for u, (oc, hh) in enumerate(units):
                                    po = (u % 2) * 1024
                                    vector.wait_ge(bE.h, u + 1)
                                    vector.scalar_tensor_tensor(
                                        khat[:, :], kps[:, po:po + 1024], knw[:, 0:1],
                                        rsb_sb[:, :],
                                        ALU.mult, ALU.mult).then_inc(bH.h, 1)
                                    bH.inc()
                                    vector.drain()
                                    vector.wait_ge(bQ.h, u + 1)
                                    vector.tensor_tensor(ksq[:, :], khat[:, :], khat[:, :], ALU.mult).then_inc(bN1.h, 1)
                                    bN1.inc()
                                    vector.wait_ge(bO.h, u + 1)
                                    cs = cosf[:, hh * 1024:hh * 1024 + 1024]
                                    sn = sinf[:, hh * 1024:hh * 1024 + 1024]
                                    vector.tensor_tensor(t1[:, :], rot_ps[:, :], sn, ALU.mult)
                                    vector.tensor_tensor(t2[:, :], khat[:, :], cs, ALU.mult)
                                    vector.drain()
                                    dst = kfm[:, oc * S + hh * 1024: oc * S + hh * 1024 + 1024]
                                    vector.tensor_tensor(dst, t1[:, :], t2[:, :], ALU.add).then_inc(bF.h, 1)
                                    bF.inc()
                                    vector.wait_ge(bNS.h, u + 1)
                                    vector.reduce_max(
                                        kvmax2[0:1, u:u + 1], sums_ps[0:1, 0:1024], AX.X
                                    ).then_inc(bNX.h, 1)
                                    bNX.inc()
                                    vector.drain()

                            @block.scalar
                            def _(scalar):
                                for u, (oc, hh) in enumerate(units):
                                    po = (u % 2) * 1024
                                    scalar.wait_ge(bM.h, u + 1)
                                    if u >= 1:
                                        scalar.wait_ge(bNS.h, u)
                                    scalar.activation(ksq[:, :], kps[:, po:po + 1024],
                                                      AF.Square).then_inc(bQ.h, 1)
                                    bQ.inc()
                                    scalar.wait_ge(bS.h, u + 1)
                                    scalar.activation(rs_row[0:1, :], sums_ps[0:1, :], AF.Ln,
                                                      bias=EPS_RMS, scale=1.0 / 128.0)
                                    scalar.drain()
                                    scalar.activation(rs_row[0:1, :], rs_row[0:1, :], AF.Exp,
                                                      scale=-0.5).then_inc(bR.h, 1)
                                    bR.inc()
                                    scalar.wait_ge(bB.h, u + 1)
                                    scalar.copy(rsb_sb[:, :], rot_ps[:, :]).then_inc(bE.h, 1)
                                    bE.inc()

                    rM = sem("rM"); rE = sem("rE"); rQ2 = sem("rQ2")
                    rNS = sem("rNS"); rNX = sem("rNX")
                    if go('p1r'):
                        with nc.Block("p1r") as block:
                            runits = [(src, oc, hh) for src in range(2) for oc in range(3)
                                      for hh in range(2)]


                            @block.gpsimd
                            def _(gpsimd):
                                for i, (src, oc) in enumerate(
                                        [(s_, o_) for s_ in range(2) for o_ in range(3)]):
                                    if i >= 2:
                                        gpsimd.wait_ge(rM.h, 2 * (i - 1))
                                    wsrc = wrq_h if src == 0 else wrk_h
                                    gpsimd.dma_start(
                                        out=wbuf[:, (i % 2) * E:(i % 2) * E + E],
                                        in_=_ap(wsrc, oc * E * 128,
                                                [[128, 128], [128 * 128, NEC], [1, 128]]),
                                    ).then_inc(bWr[i % 2].h, 16)
                                    bWr[i % 2].inc(16)

                            @block.tensor
                            def _(tensor):
                                for u, (src, oc, hh) in enumerate(runits):
                                    i = src * 3 + oc
                                    tensor.wait_ge(bWr[i % 2].h, 16 * (i // 2 + 1))
                                    if u >= 2:
                                        tensor.wait_ge(rE.h, u - 1)
                                        tensor.wait_ge(rQ2.h, u - 1)
                                    po = (u % 2) * 1024
                                    wv_ = wbuf[:, (i % 2) * E:(i % 2) * E + E]
                                    for ec in range(NEC):
                                        for q4 in range(2):
                                            mm = tensor.matmul(
                                                kps[:, po + q4 * 512:po + (q4 + 1) * 512],
                                                wv_[:, ec * 128:(ec + 1) * 128],
                                                xn_cols(ec, hh * 1024 + q4 * 512, 512),
                                                start=(ec == 0), stop=(ec == NEC - 1))
                                    mm.then_inc(rM.h, 1)
                                    rM.inc()
                                    tensor.wait_ge(rQ2.h, u + 1)
                                    if u >= 1:
                                        tensor.wait_ge(rNX.h, u)
                                    for q4 in range(2):
                                        mm = tensor.matmul(
                                            sums_ps[0:4, q4 * 512:(q4 + 1) * 512],
                                            blk4[:, :], ksq[:, q4 * 512:(q4 + 1) * 512],
                                            start=True, stop=True)
                                    mm.then_inc(rNS.h, 1)
                                    rNS.inc()

                            @block.scalar
                            def _(scalar):
                                for u, (src, oc, hh) in enumerate(runits):
                                    po = (u % 2) * 1024
                                    scalar.wait_ge(rE.h, u + 1)
                                    scalar.activation(ksq[:, :], kps[:, po:po + 1024],
                                                      AF.Square).then_inc(rQ2.h, 1)
                                    rQ2.inc()
                                    scalar.drain()

                            @block.vector
                            def _(vector):
                                for u, (src, oc, hh) in enumerate(runits):
                                    po = (u % 2) * 1024
                                    kv_ = kps[:, po:po + 1024]
                                    vector.wait_ge(rM.h, u + 1)
                                    sl = slice(hh * 1024, hh * 1024 + 1024)
                                    if src == 0:
                                        gc = gs_half[:, 0:1] if oc == 0 else gs_cols[:, oc:oc + 1]
                                        vector.tensor_scalar(
                                            rqs[:, oc * S + hh * 1024: oc * S + hh * 1024 + 1024],
                                            kv_, gc, None, ALU.mult)
                                        if oc == 0:
                                            vector.tensor_copy(rqu[:, sl], kv_)
                                    else:
                                        vector.tensor_copy(
                                            rku[:, oc * S + hh * 1024: oc * S + hh * 1024 + 1024],
                                            kv_)
                                        if oc == 0:
                                            vector.tensor_scalar(
                                                rksh[:, sl], kv_, gs_half[:, 0:1],
                                                None, ALU.mult)
                                    vector.engine_nop().then_inc(rE.h, 1)
                                    rE.inc()
                                    vector.wait_ge(rNS.h, u + 1)
                                    dstm = rqmax2 if src == 0 else rkmax2
                                    col = oc * 2 + hh
                                    vector.reduce_max(
                                        dstm[0:4, col:col + 1], sums_ps[0:4, 0:1024], AX.X
                                    ).then_inc(rNX.h, 1)
                                    rNX.inc()
                                    vector.drain()

                    bSp = sem("bSp")
                    with nc.Block("p1s") as block:
                        @block.sync
                        def _(sync):
                            for src_t, dst_hd, n, w in (
                                (kfm, spk_h, NKV, S), (vsb, spv_h, 1, NTT * 512),
                                (rqs, sprqs_h, 3, S), (rku, sprku_h, 3, S),
                                (rqu, sprqu_h, 1, S), (rksh, sprksh_h, 1, S),
                            ):
                                if n == 1:
                                    dst_ap = _ap(dst_hd, 0, [[w, 128], [1, w]])
                                else:
                                    dst_ap = _ap(dst_hd, 0, [[w, 128], [128 * w, n], [1, w]])
                                sync.dma_start(out=dst_ap, in_=src_t[:, :]).then_inc(bSp.h, 16)
                                bSp.inc(16)
                            sync.wait_ge(bSp.h, bSp.v)

            if go("p2"):
                with ExitStack() as SP:
                    sb = lambda n, sh, dt: SP.enter_context(nc.sbuf_tensor(n, sh, dt))
                    cosf = sb("p2_cosf", [128, S], F32)
                    sinf = sb("p2_sinf", [128, S], F32)
                    wbuf = sb("p2_wbuf", [128, 2 * E], BF16)
                    qsq = sb("p2_qsq", [128, 2 * 1024], BF16)
                    lnrow = sb("p2_lnrow", [1, 1024], F32)
                    rs_row = sb("p2_rsrow", [1, 1024], F32)
                    rs2_row = sb("p2_rs2row", [1, 1024], F32)
                    tmp_row = sb("p2_tmprow", [1, 1024], F32)
                    qnw2 = sb("p2_qnw2", [128, 1], BF16)
                    rsb_sb = sb("p2_rsb", [128, 2 * 1024], F32)
                    qhat = sb("p2_qhat", [128, 2 * 1024], BF16)
                    t1 = sb("p2_t1", [128, 2 * 1024], F32)
                    t2 = sb("p2_t2", [128, 2 * 1024], F32)
                    qout = sb("p2_qout", [128, 2 * 1024], BF16)
                    qps = SP.enter_context(nc.psum_tensor("p2_qps", [128, 2 * 1024], F32))
                    sums_ps = SP.enter_context(nc.psum_tensor("p2_sums", [64, 1024], F32))
                    rot_ps = SP.enter_context(nc.psum_tensor("p2_rot", [128, 1024], F32))
                    cL = sem("cL"); cWp = [sem("cWa"), sem("cWb")]; cM = sem("cM"); cQ = sem("cQ")
                    cS = sem("cS"); cR = sem("cR"); cB = sem("cB"); cE = sem("cE")
                    cH = sem("cH"); cO = sem("cO"); cF1 = sem("cF1"); cF = sem("cF")
                    cNX = sem("cNX")
                    cSp = [sem("cSpa"), sem("cSpb")]
                    with nc.Block("p2") as block:
                        qunits = [(h, hh) for h in range(NH) for hh in range(2)]
                        NU = len(qunits)

                        @block.sync
                        def _(sync):
                            sync.dma_start(out=cosf[:, :], in_=cosf_h.ap()).then_inc(cL.h, 16)
                            sync.dma_start(out=sinf[:, :], in_=sinf_h.ap()).then_inc(cL.h, 16)
                            cL.inc(32)
                            for u, (h, hh) in enumerate(qunits):
                                sync.wait_ge(cF.h, u + 1)
                                for k in range(2):
                                    g = hh * 2 + k
                                    sync.dma_start(
                                        out=_ap(spq_h, (g * NH + h) * 128 * 512,
                                                [[512, 128], [1, 512]]),
                                        in_=qout[:, (u % 2) * 1024 + k * 512:
                                                 (u % 2) * 1024 + (k + 1) * 512],
                                    ).then_inc(cSp[u % 2].h, 16)
                                    cSp[u % 2].inc(16)
                            sync.wait_ge(cSp[0].h, cSp[0].v)
                            sync.wait_ge(cSp[1].h, cSp[1].v)

                        @block.gpsimd
                        def _(gpsimd):
                            for h in (0, 1):
                                gpsimd.dma_start(
                                    out=wbuf[:, (h % 2) * E:(h % 2) * E + E],
                                    in_=_ap(wq_h, h * E * 128,
                                            [[128, 128], [128 * 128, NEC], [1, 128]]),
                                ).then_inc(cWp[h % 2].h, 16)
                                cWp[h % 2].inc(16)
                            for u, (h, hh) in enumerate(qunits):
                                if hh == 0 and h + 2 < NH:
                                    h2 = h + 2
                                    gpsimd.wait_ge(cM.h, 2 * (h + 1))
                                    gpsimd.dma_start(
                                        out=wbuf[:, (h2 % 2) * E:(h2 % 2) * E + E],
                                        in_=_ap(wq_h, h2 * E * 128,
                                                [[128, 128], [128 * 128, NEC], [1, 128]]),
                                    ).then_inc(cWp[h2 % 2].h, 16)
                                    cWp[h2 % 2].inc(16)
                                if u >= 2:
                                    gpsimd.wait_ge(cSp[u % 2].h, 32 * (u // 2))
                                gpsimd.wait_ge(cF1.h, u + 1)
                                gpsimd.tensor_tensor(
                                    qout[:, (u % 2) * 1024:(u % 2) * 1024 + 1024],
                                    t1[:, (u % 2) * 1024:(u % 2) * 1024 + 1024],
                                    t2[:, (u % 2) * 1024:(u % 2) * 1024 + 1024],
                                    ALU.add).then_inc(cF.h, 1)
                                cF.inc()

                        def _s1(tensor, v):
                            po = (v % 2) * 1024
                            tensor.wait_ge(cQ.h, v + 1)
                            if v >= 1:
                                tensor.wait_ge(cR.h, v)
                                tensor.wait_ge(cNX.h, 2 * v)
                            for q4 in range(2):
                                tensor.matmul(
                                    sums_ps[0:1, q4 * 512:(q4 + 1) * 512],
                                    onesc[:, :], qsq[:, po + q4 * 512:po + (q4 + 1) * 512],
                                    start=True, stop=True)
                            for q4 in range(2):
                                mm = tensor.matmul(
                                    sums_ps[32:33, q4 * 512:(q4 + 1) * 512],
                                    qnw2[:, 0:1], qsq[:, po + q4 * 512:po + (q4 + 1) * 512],
                                    start=True, stop=True)
                            mm.then_inc(cS.h, 1)
                            cS.inc()

                        def _s2(tensor, v):
                            tensor.wait_ge(cR.h, v + 1)
                            if v >= 1:
                                tensor.wait_ge(cF1.h, v)
                            for q4 in range(2):
                                mm = tensor.matmul(
                                    rot_ps[:, q4 * 512:(q4 + 1) * 512],
                                    onesr[:, :], rs_row[0:1, q4 * 512:(q4 + 1) * 512],
                                    start=True, stop=True)
                            mm.then_inc(cB.h, 1)
                            cB.inc()

                        def _s3(tensor, v):
                            po = (v % 2) * 1024
                            tensor.wait_ge(cH.h, v + 1)
                            tensor.wait_ge(cE.h, v + 1)
                            for q4 in range(2):
                                mm = tensor.matmul(
                                    rot_ps[:, q4 * 512:(q4 + 1) * 512],
                                    rtm[:, :], qhat[:, po + q4 * 512:po + (q4 + 1) * 512],
                                    start=True, stop=True)
                            mm.then_inc(cO.h, 1)
                            cO.inc()

                        @block.tensor
                        def _(tensor):
                            for u, (h, hh) in enumerate(qunits):
                                tensor.wait_ge(cWp[h % 2].h, 16 * (h // 2 + 1))
                                if u >= 2:
                                    tensor.wait_ge(cH.h, u - 1)
                                wv_ = wbuf[:, (h % 2) * E:(h % 2) * E + E]
                                po = (u % 2) * 1024
                                for ec in range(NEC):
                                    for q4 in range(2):
                                        mm = tensor.matmul(
                                            qps[:, po + q4 * 512:po + (q4 + 1) * 512],
                                            wv_[:, ec * 128:(ec + 1) * 128],
                                            xn_cols(ec, hh * 1024 + q4 * 512, 512),
                                            start=(ec == 0), stop=(ec == NEC - 1))
                                    if u >= 1:
                                        if ec == 6:
                                            _s1(tensor, u - 1)
                                        elif ec == 11:
                                            _s2(tensor, u - 1)
                                        elif ec == 15:
                                            _s3(tensor, u - 1)
                                mm.then_inc(cM.h, 1)
                                cM.inc()
                            _s1(tensor, NU - 1)
                            _s2(tensor, NU - 1)
                            _s3(tensor, NU - 1)

                        @block.vector
                        def _(vector):
                            vector.wait_ge(cL.h, 32)
                            for u, (h, hh) in enumerate(qunits):
                                po = (u % 2) * 1024
                                vector.wait_ge(cE.h, u + 1)
                                vector.scalar_tensor_tensor(
                                    qhat[:, po:po + 1024], qps[:, po:po + 1024],
                                    qnw_eff[:, 0:1], rsb_sb[:, po:po + 1024],
                                    ALU.mult, ALU.mult).then_inc(cH.h, 1)
                                cH.inc()
                                vector.wait_ge(cO.h, u + 1)
                                if u >= 2:
                                    vector.wait_ge(cF.h, u - 1)
                                cs = cosf[:, hh * 1024:hh * 1024 + 1024]
                                sn = sinf[:, hh * 1024:hh * 1024 + 1024]
                                vector.drain()
                                vector.tensor_tensor(t1[:, po:po + 1024], rot_ps[:, :],
                                                     sn, ALU.mult)
                                vector.tensor_tensor(t2[:, po:po + 1024],
                                                     qhat[:, po:po + 1024],
                                                     cs, ALU.mult).then_inc(cF1.h, 1)
                                cF1.inc()
                                vector.wait_ge(cS.h, u + 1)
                                vector.wait_ge(cR.h, u + 1)
                                vector.tensor_tensor(tmp_row[0:1, :], sums_ps[32:33, :],
                                                     rs2_row[0:1, :], ALU.mult)
                                vector.drain()
                                for k in range(2):
                                    col = 4 * h + 2 * hh + k
                                    vector.reduce_max(
                                        maxq2[0:1, col:col + 1],
                                        tmp_row[0:1, k * 512:(k + 1) * 512], AX.X
                                    ).then_inc(cNX.h, 1)
                                    cNX.inc()
                                vector.drain()

                        @block.scalar
                        def _(scalar):
                            scalar.activation(qnw2[:, 0:1], qnw_eff[:, 0:1], AF.Square)
                            for u, (h, hh) in enumerate(qunits):
                                po = (u % 2) * 1024
                                scalar.wait_ge(cM.h, u + 1)
                                if u >= 2:
                                    scalar.wait_ge(cS.h, u - 1)
                                scalar.activation(qsq[:, po:po + 1024], qps[:, po:po + 1024],
                                                  AF.Square).then_inc(cQ.h, 1)
                                cQ.inc()
                                scalar.wait_ge(cS.h, u + 1)
                                if u >= 1:
                                    scalar.wait_ge(cB.h, u)
                                scalar.activation(lnrow[0:1, :], sums_ps[0:1, :], AF.Ln,
                                                  bias=EPS_RMS, scale=1.0 / 128.0)
                                scalar.drain()
                                scalar.activation(rs_row[0:1, :], lnrow[0:1, :], AF.Exp,
                                                  scale=-0.5)
                                scalar.activation(rs2_row[0:1, :], lnrow[0:1, :], AF.Exp,
                                                  scale=-1.0).then_inc(cR.h, 1)
                                cR.inc()
                                scalar.wait_ge(cB.h, u + 1)
                                if u >= 2:
                                    scalar.wait_ge(cH.h, u - 1)
                                scalar.copy(rsb_sb[:, po:po + 1024],
                                            rot_ps[:, :]).then_inc(cE.h, 1)
                                cE.inc()

            if go("p3"):
                with ExitStack() as SP:
                    sb = lambda n, sh, dt: SP.enter_context(nc.sbuf_tensor(n, sh, dt))
                    hfm = sb("p3_hfm", [128, NIC * 1024], BF16)
                    gwb = sb("p3_gwb", [128, 2 * E], BF16)
                    uwb = sb("p3_uwb", [128, 2 * E], BF16)
                    dwb = sb("p3_dwb", [128, 2 * 2048], BF16)
                    sg = sb("p3_sg", [128, 2 * 1024], F32)
                    gout = sb("p3_gout", [128, 2 * 1024], BF16)
                    dM = sem("dM"); dSi = sem("dSi"); dH = sem("dH")
                    dDM = sem("dDM"); dGE = sem("dGE")
                    for hh in range(2):
                        dWp = [sem(f"dW{hh}a"), sem(f"dW{hh}b")]
                        dDWp = [sem(f"dDW{hh}a"), sem(f"dDW{hh}b")]
                        dSpp = [sem(f"dSp{hh}a"), sem(f"dSp{hh}b")]
                        with ExitStack() as SS:
                            gps = SS.enter_context(nc.psum_tensor(f"p3_gps{hh}", [128, 2 * 1024], F32))
                            ups = SS.enter_context(nc.psum_tensor(f"p3_ups{hh}", [128, 2 * 1024], F32))
                            with nc.Block(f"p3g{hh}") as block:
                                base_dM = dM.v
                                base_dSi = dSi.v
                                base_dH = dH.v

                                @block.gpsimd
                                def _(gpsimd):
                                    for ic in range(NIC):
                                        if ic >= 2:
                                            gpsimd.wait_ge(dM.h, base_dM + ic - 1)
                                        for wsrc, dstb in ((wg_h, gwb), (wu_h, uwb)):
                                            gpsimd.dma_start(
                                                out=dstb[:, (ic % 2) * E:(ic % 2) * E + E],
                                                in_=_ap(wsrc, ic * E * 128,
                                                        [[128, 128], [128 * 128, NEC], [1, 128]]),
                                            ).then_inc(dWp[ic % 2].h, 16)
                                            dWp[ic % 2].inc(16)

                                @block.tensor
                                def _(tensor):
                                    for ic in range(NIC):
                                        tensor.wait_ge(dWp[ic % 2].h, 32 * (ic // 2 + 1))
                                        if ic >= 2:
                                            tensor.wait_ge(dH.h, base_dH + ic - 1)
                                        po = (ic % 2) * 1024
                                        gw_ = gwb[:, (ic % 2) * E:(ic % 2) * E + E]
                                        uw_ = uwb[:, (ic % 2) * E:(ic % 2) * E + E]
                                        for ec in range(NEC):
                                            for q4 in range(2):
                                                tensor.matmul(
                                                    gps[:, po + q4 * 512:po + (q4 + 1) * 512],
                                                    gw_[:, ec * 128:(ec + 1) * 128],
                                                    xn_cols(ec, hh * 1024 + q4 * 512, 512),
                                                    start=(ec == 0), stop=(ec == NEC - 1))
                                        for ec in range(NEC):
                                            for q4 in range(2):
                                                mm = tensor.matmul(
                                                    ups[:, po + q4 * 512:po + (q4 + 1) * 512],
                                                    uw_[:, ec * 128:(ec + 1) * 128],
                                                    xn_cols(ec, hh * 1024 + q4 * 512, 512),
                                                    start=(ec == 0), stop=(ec == NEC - 1))
                                        mm.then_inc(dM.h, 1)
                                        dM.inc()

                                @block.scalar
                                def _(scalar):
                                    for ic in range(NIC):
                                        scalar.wait_ge(dM.h, base_dM + ic + 1)
                                        if ic >= 2:
                                            scalar.wait_ge(dH.h, base_dH + ic - 1)
                                        po = (ic % 2) * 1024
                                        scalar.activation(
                                            sg[:, po:po + 1024], gps[:, po:po + 1024],
                                            AF.Silu,
                                            scale=gates_c[:, ic:ic + 1]).then_inc(dSi.h, 1)
                                        dSi.inc()

                                @block.vector
                                def _(vector):
                                    for ic in range(NIC):
                                        vector.wait_ge(dSi.h, base_dSi + ic + 1)
                                        po = (ic % 2) * 1024
                                        vector.scalar_tensor_tensor(
                                            hfm[:, ic * 1024:(ic + 1) * 1024],
                                            ups[:, po:po + 1024], ups_c[:, ic:ic + 1],
                                            sg[:, po:po + 1024], ALU.mult, ALU.mult).then_inc(dH.h, 1)
                                        dH.inc()

                        with ExitStack() as SS:
                            dps = SS.enter_context(nc.psum_tensor(f"p3_dps{hh}", [128, 4 * 1024], F32))
                            with nc.Block(f"p3d{hh}") as block:
                                base_dDM = dDM.v
                                base_dGE = dGE.v

                                @block.gpsimd
                                def _(gpsimd):
                                    for og in range(4):
                                        for icb in range(NIC // 4):
                                            nb = og * 11 + icb
                                            if nb >= 2:
                                                gpsimd.wait_ge(dDM.h, base_dDM + 16 * (nb - 1))
                                            gpsimd.dma_start(
                                                out=dwb[:, (nb % 2) * 2048:(nb % 2) * 2048 + 2048],
                                                in_=_ap(wd_h,
                                                        (og * NIC + icb * 4) * 128 * 512,
                                                        [[512, 128], [512 * 128, 4], [1, 512]]),
                                            ).then_inc(dDWp[nb % 2].h, 16)
                                            dDWp[nb % 2].inc(16)

                                @block.tensor
                                def _(tensor):
                                    for og in range(4):
                                        for ic in range(NIC):
                                            nb = og * 11 + ic // 4
                                            tensor.wait_ge(dDWp[nb % 2].h, 16 * (nb // 2 + 1))
                                            if ic == 0 and og >= 1:
                                                tensor.wait_ge(dGE.h, base_dGE + 4 * og)
                                            wv_ = dwb[:, (nb % 2) * 2048 + (ic % 4) * 512:
                                                      (nb % 2) * 2048 + (ic % 4 + 1) * 512]
                                            for oc4 in range(4):
                                                for q4 in range(2):
                                                    mm = tensor.matmul(
                                                        dps[:, oc4 * 1024 + q4 * 512:
                                                            oc4 * 1024 + (q4 + 1) * 512],
                                                        wv_[:, oc4 * 128:(oc4 + 1) * 128],
                                                        hfm[:, ic * 1024 + q4 * 512:
                                                            ic * 1024 + (q4 + 1) * 512],
                                                        start=(ic == 0), stop=(ic == NIC - 1))
                                            mm.then_inc(dDM.h, 4)
                                            dDM.inc(4)

                                @block.scalar
                                def _(scalar):
                                    for og in range(4):
                                        scalar.wait_ge(dDM.h, base_dDM + 4 * NIC * (og + 1))
                                        for oc4 in range(4):
                                            oc = og * 4 + oc4
                                            if oc >= 2:
                                                scalar.wait_ge(dSpp[oc % 2].h, 16 * (oc // 2))
                                            scalar.activation(
                                                gout[:, (oc % 2) * 1024:(oc % 2) * 1024 + 1024],
                                                dps[:, oc4 * 1024:(oc4 + 1) * 1024],
                                                AF.Silu, scale=downs_c[:, oc:oc + 1]
                                            ).then_inc(dGE.h, 1)
                                            dGE.inc()

                                @block.sync
                                def _(sync):
                                    for oc in range(NEC):
                                        sync.wait_ge(dGE.h, base_dGE + oc + 1)
                                        sync.dma_start(
                                            out=_ap(spg_h, oc * 128 * S + hh * 1024,
                                                    [[S, 128], [1, 1024]]),
                                            in_=gout[:, (oc % 2) * 1024:(oc % 2) * 1024 + 1024],
                                        ).then_inc(dSpp[oc % 2].h, 16)
                                        dSpp[oc % 2].inc(16)
                                    sync.wait_ge(dSpp[0].h, dSpp[0].v)
                                    sync.wait_ge(dSpp[1].h, dSpp[1].v)


        if go("p4"):
            with ExitStack() as SP:
                sb = lambda n, sh, dt: SP.enter_context(nc.sbuf_tensor(n, sh, dt))
                kfm = sb("p4_kfm", [128, NKV * S], BF16)
                vsb = sb("p4_vsb", [128, NTT * 512], BF16)
                rqs = sb("p4_rqs", [128, 3 * S], BF16)
                rku = sb("p4_rku", [128, NGH * S], BF16)
                rqu = sb("p4_rqu", [128, 4 * S], BF16)
                rksh = sb("p4_rksh", [128, S], BF16)
                masks = sb("p4_masks", [128, 4 * 512], F32)
                qg = sb("p4_qg", [128, 8 * 512], BF16)
                ptb = sb("p4_ptb", [128, 16 * 512], BF16)
                ogs = sb("p4_ogs", [128, 4 * 512], BF16)
                sums_sb = sb("p4_sums", [1, 2 * 512], F32)
                recip = sb("p4_recip", [1, 2 * 512], F32)
                rb_sb = sb("p4_rbsb", [128, 2 * 512], F32)
                brow = sb("p4_brow", [1, 64], F32)
                btmp = sb("p4_btmp", [1, 256], F32)

                eR = sem("eR"); eP = sem("ePr"); ePs = sem("ePs")
                sc_ps = SP.enter_context(nc.psum_tensor("p4_sc", [128, 4 * 512], F32))
                o_ps = SP.enter_context(nc.psum_tensor("p4_o", [128, 2 * 512], F32))
                sm_ps = SP.enter_context(nc.psum_tensor("p4_sm", [1, 512], F32))
                rb_ps = SP.enter_context(nc.psum_tensor("p4_rb", [128, 512], F32))

                eMz = sem("eMz")
                with nc.Block("p4b") as block:
                    @block.sync
                    def _(sync):
                        sync.dma_start(out=masks[:, :], in_=masks_h.ap()).then_inc(eR.h, 16)
                        eR.inc(16)
                        for dst, src_hd, n, w in (
                            (kfm, spk_h, NKV, S), (vsb, spv_h, 1, NTT * 512),
                            (rqs, sprqs_h, 3, S), (rksh, sprksh_h, 1, S),
                        ):
                            if n == 1:
                                src_ap = _ap(src_hd, 0, [[w, 128], [1, w]])
                            else:
                                src_ap = _ap(src_hd, 0, [[w, 128], [128 * w, n], [1, w]])
                            sync.dma_start(out=dst[:, :], in_=src_ap).then_inc(eR.h, 16)
                            eR.inc(16)
                        sync.wait_ge(eMz.h, 1)
                        for h in range(NGH):
                            oc, l = h // 4, h % 4
                            sync.dma_start(
                                out=rku[32 * l:32 * (l + 1), h * S:(h + 1) * S],
                                in_=_ap(sprku_h, oc * 128 * S + 32 * l * S,
                                        [[S, 32], [1, S]])).then_inc(eR.h, 16)
                            eR.inc(16)
                        for h in range(4):
                            sync.dma_start(
                                out=rqu[32 * h:32 * (h + 1), h * S:(h + 1) * S],
                                in_=_ap(sprqu_h, 32 * h * S,
                                        [[S, 32], [1, S]])).then_inc(eR.h, 16)
                            eR.inc(16)
                        sync.wait_ge(ePs.h, 1)
                        sync.dma_start(out=btmp[0:1, 80:92],
                                       in_=_ap(rqmax2, 0, [[6, 4], [1, 3]])
                                       ).then_inc(eR.h, 16)
                        sync.dma_start(out=btmp[0:1, 96:108],
                                       in_=_ap(rkmax2, 0, [[6, 4], [1, 3]])
                                       ).then_inc(eR.h, 16)
                        eR.inc(32)

                    @block.vector
                    def _(vector):
                        vector.memset(rku[:, :], 0.0)
                        vector.memset(rqu[:, :], 0.0)
                        vector.drain()
                        vector.memset(btmp[0:1, 240:241], 0.0).then_inc(eMz.h, 1)
                        vector.tensor_tensor(
                            _ap(rqmax2, 0, [[6, 4], [2, 3]]),
                            _ap(rqmax2, 0, [[6, 4], [2, 3]]),
                            _ap(rqmax2, 1, [[6, 4], [2, 3]]), ALU.max)
                        vector.tensor_tensor(
                            _ap(rkmax2, 0, [[6, 4], [2, 3]]),
                            _ap(rkmax2, 0, [[6, 4], [2, 3]]),
                            _ap(rkmax2, 1, [[6, 4], [2, 3]]), ALU.max)
                        vector.tensor_tensor(
                            btmp[0:1, 64:68],
                            _ap(kvmax2, 0, [[8, 1], [2, 4]]),
                            _ap(kvmax2, 1, [[8, 1], [2, 4]]), ALU.max).then_inc(ePs.h, 1)
                        ePs.inc()
                        vector.wait_ge(eP.h, 4)
                        vector.tensor_tensor(
                            btmp[0:1, 144:156], btmp[0:1, 112:124], btmp[0:1, 128:140],
                            ALU.mult)
                        vector.drain()
                        vector.tensor_tensor(
                            _ap(btmp, 144, [[256, 1], [3, 4], [1, 3]]),
                            _ap(btmp, 144, [[256, 1], [3, 4], [1, 3]]),
                            _ap(gsrow, 0, [[12, 1], [1, 4], [4, 3]]), ALU.mult)
                        vector.drain()
                        vector.scalar_tensor_tensor(
                            btmp[0:1, 144:156], btmp[0:1, 144:156], -1.0,
                            btmp[0:1, 144:156], ALU.mult, ALU.max)
                        vector.memset(btmp[0:1, 160:176], 0.0)
                        vector.drain()
                        vector.tensor_copy(
                            _ap(btmp, 160, [[256, 1], [4, 3], [1, 4]]),
                            _ap(btmp, 144, [[256, 1], [1, 3], [3, 4]]))
                        vector.tensor_tensor(
                            _ap(brow, 0, [[64, 1], [16, 4], [1, 16]]),
                            _ap(btmp, 0, [[256, 1], [16, 4], [1, 16]]),
                            _ap(btmp, 70, [[256, 1], [1, 4], [0, 16]]), ALU.mult)
                        vector.drain()
                        vector.scalar_tensor_tensor(
                            _ap(brow, 0, [[64, 1], [4, 16], [1, 4]]),
                            _ap(brow, 0, [[64, 1], [4, 16], [1, 4]]), -1.0,
                            _ap(btmp, 160, [[256, 1], [1, 16], [0, 4]]),
                            ALU.mult, ALU.subtract).then_inc(ePs.h, 1)
                        ePs.inc()

                    @block.scalar
                    def _(scalar):
                        scalar.wait_ge(eR.h, eR.v)
                        for dst0, dstn, src in (
                            (0, 64, maxq2[0:1, :]), (70, 74, btmp[0:1, 64:68]),
                            (112, 124, btmp[0:1, 80:92]), (128, 140, btmp[0:1, 96:108]),
                        ):
                            scalar.activation(btmp[0:1, dst0:dstn], src, AF.Ln, bias=1e-30)
                            scalar.drain()
                            scalar.activation(btmp[0:1, dst0:dstn], btmp[0:1, dst0:dstn],
                                              AF.Exp, scale=0.5, bias=BMARGIN).then_inc(eP.h, 1)
                            eP.inc()

                    @block.tensor
                    def _(tensor):
                        tensor.wait_ge(ePs.h, 2)
                        tensor.matmul(rb_ps[:, 0:64], onesr[:, :], brow[0:1, :],
                                      start=True, stop=True).then_inc(eP.h, 1)
                        eP.inc()

                with nc.Block("p4bb") as block:
                    @block.scalar
                    def _(scalar):
                        scalar.copy(bneg_b[:, :], rb_ps[:, 0:64])

                if tb_h is not None:
                    tB = sem("tB")
                    with nc.Block("p4tap") as block:
                        @block.sync
                        def _(sync):
                            sync.dma_start(out=tb_h.ap(), in_=bneg_b[:, :]).then_inc(tB.h, 16)
                            sync.wait_ge(tB.h, tB.inc(16))

                eC = sem("eC"); eEv = sem("eEv"); eEx = sem("eEx"); eAV = sem("eAV")
                eSm = sem("eSm"); eB2 = sem("eB2"); eSv = sem("eSv"); eNo = sem("eNo")
                eQl = sem("eQl"); eSp4 = [sem(f"eSp{i}") for i in range(4)]
                scr = sb("p4_scr", [128, 4 * 1024], F32)
                for g in range(NG):
                    nch = 4 * g + 4
                    with nc.Block(f"p4g{g}") as block:
                        base_eC = eC.v
                        base_eQl = eQl.v
                        base_eSp4 = [x.v for x in eSp4]
                        base_eEv = eEv.v
                        base_eEx = eEx.v
                        base_eAV = eAV.v
                        base_eSm = eSm.v
                        base_eB2 = eB2.v
                        base_eSv = eSv.v
                        base_eNo = eNo.v

                        @block.sync
                        def _(sync):
                            sync.dma_start(
                                out=qg[:, :],
                                in_=_ap(spq_h, g * NH * 128 * 512,
                                        [[512, 128], [128 * 512, 8], [1, 512]]),
                            ).then_inc(eQl.h, 16)
                            eQl.inc(16)
                            for h in range(NH):
                                if h == 7:
                                    sync.wait_ge(eAV.h, base_eAV + 8)
                                    sync.dma_start(
                                        out=qg[:, :],
                                        in_=_ap(spq_h, (g * NH + 8) * 128 * 512,
                                                [[512, 128], [128 * 512, 8], [1, 512]]),
                                    ).then_inc(eQl.h, 16)
                                    eQl.inc(16)
                                sync.wait_ge(eNo.h, base_eNo + h + 1)
                                sync.dma_start(
                                    out=_ap(spo_h, h * 128 * S + g * 512,
                                            [[S, 128], [1, 512]]),
                                    in_=ogs[:, (h % 4) * 512:(h % 4 + 1) * 512],
                                ).then_inc(eSp4[h % 4].h, 16)
                                eSp4[h % 4].inc(16)
                            for x in eSp4:
                                sync.wait_ge(x.h, x.v)

                        @block.tensor
                        def _(tensor):
                            def bcast(tensor, hb):
                                tensor.wait_ge(eSm.h, base_eSm + hb + 1)
                                mm = tensor.matmul(
                                    rb_ps[:, 0:512], onesr[:, :],
                                    recip[0:1, (hb % 2) * 512:(hb % 2) * 512 + 512],
                                    start=True, stop=True)
                                mm.then_inc(eB2.h, 1)
                                eB2.inc()

                            ich = 0
                            for h in range(NH):
                                kv = h // 4
                                if h % 8 == 0:
                                    tensor.wait_ge(eQl.h, base_eQl + 16 * (h // 8 + 1))
                                if h >= 2:
                                    tensor.wait_ge(eNo.h, base_eNo + h - 1)
                                hbase = ich
                                pbase = ich // 2
                                for j in range(nch):
                                    pr = ich // 2
                                    if pr >= 2 and ich % 2 == 0:
                                        tensor.wait_ge(eEv.h, base_eEv + pr - 1)
                                    po = (pr % 2) * 1024 + (ich % 2) * 512
                                    last = tensor.matmul(
                                        sc_ps[:, po:po + 512],
                                        kfm[:, kv * S + j * 128: kv * S + (j + 1) * 128],
                                        qg[:, (h % 8) * 512:(h % 8 + 1) * 512],
                                        start=True, stop=(h >= NGH))
                                    if h < NGH:
                                        oc = h // 4
                                        last = tensor.matmul(
                                            sc_ps[:, po:po + 512],
                                            rku[:, h * S + j * 128: h * S + (j + 1) * 128],
                                            rqs[:, oc * S + g * 512: oc * S + (g + 1) * 512],
                                            start=False, stop=(h >= 4))
                                    if h < 4:
                                        last = tensor.matmul(
                                            sc_ps[:, po:po + 512],
                                            rqu[:, h * S + j * 128: h * S + (j + 1) * 128],
                                            rksh[:, g * 512:(g + 1) * 512],
                                            start=False, stop=True)
                                    last.then_inc(eC.h, 1)
                                    eC.inc()
                                    ich += 1
                                    if j >= 6:
                                        jj = j - 6
                                        tensor.wait_ge(eEx.h, base_eEx + pbase + jj // 2 + 1)
                                        tensor.matmul(
                                            o_ps[:, (h % 2) * 512:(h % 2) * 512 + 512],
                                            vsb[:, jj * 512 + kv * 128: jj * 512 + (kv + 1) * 128],
                                            ptb[:, jj * 512:(jj + 1) * 512],
                                            start=(jj == 0), stop=False)
                                        if jj % 4 == 3:
                                            if jj == 3 and h >= 1:
                                                tensor.wait_ge(eSv.h, base_eSv + h)
                                            for js in range(jj - 3, jj + 1):
                                                tensor.matmul(
                                                    sm_ps[0:1, 0:512], onesc[:, :],
                                                    ptb[:, js * 512:(js + 1) * 512],
                                                    start=(js == 0), stop=False)
                                for jj in range(max(0, nch - 6), nch):
                                    tensor.wait_ge(eEx.h, base_eEx + pbase + jj // 2 + 1)
                                    tensor.matmul(
                                        o_ps[:, (h % 2) * 512:(h % 2) * 512 + 512],
                                        vsb[:, jj * 512 + kv * 128: jj * 512 + (kv + 1) * 128],
                                        ptb[:, jj * 512:(jj + 1) * 512],
                                        start=(jj == 0), stop=(jj == nch - 1))
                                    if jj % 4 == 3:
                                        if jj == 3 and h >= 1:
                                            tensor.wait_ge(eSv.h, base_eSv + h)
                                        for js in range(jj - 3, jj + 1):
                                            mm2 = tensor.matmul(
                                                sm_ps[0:1, 0:512], onesc[:, :],
                                                ptb[:, js * 512:(js + 1) * 512],
                                                start=(js == 0), stop=(js == nch - 1))
                                mm2.then_inc(eAV.h, 1)
                                eAV.inc()
                                if h >= 1:
                                    bcast(tensor, h - 1)
                            bcast(tensor, NH - 1)

                        @block.vector
                        def _(vector):
                            def sums_evac(vector, hb):
                                vector.wait_ge(eAV.h, base_eAV + hb + 1)
                                vector.tensor_copy(
                                    sums_sb[0:1, (hb % 2) * 512:(hb % 2) * 512 + 512],
                                    sm_ps[0:1, 0:512]).then_inc(eSv.h, 1)
                                eSv.inc()

                            def tail(vector, hb):
                                vector.wait_ge(eB2.h, base_eB2 + hb + 1)
                                if hb >= 4:
                                    vector.wait_ge(eSp4[hb % 4].h,
                                                   base_eSp4[hb % 4] + 16 * ((hb - 4) // 4 + 1))
                                vector.tensor_copy(
                                    rb_sb[:, (hb % 2) * 512:(hb % 2) * 512 + 512],
                                    rb_ps[:, 0:512])
                                vector.drain()
                                vector.tensor_tensor(
                                    ogs[:, (hb % 4) * 512:(hb % 4 + 1) * 512],
                                    o_ps[:, (hb % 2) * 512:(hb % 2) * 512 + 512],
                                    rb_sb[:, (hb % 2) * 512:(hb % 2) * 512 + 512],
                                    ALU.mult).then_inc(eNo.h, 1)
                                eNo.inc()

                            ich = 0
                            for h in range(NH):
                                if h >= 1:
                                    sums_evac(vector, h - 1)
                                for j in range(0, nch, 2):
                                    pr = ich // 2
                                    if pr >= 4:
                                        vector.wait_ge(eEx.h, base_eEx + pr - 3)
                                    po = (pr % 2) * 1024
                                    so = (pr % 4) * 1024
                                    if j + 1 < 4 * g:
                                        vector.wait_ge(eC.h, base_eC + ich + 2)
                                        vector.tensor_copy(
                                            scr[:, so:so + 1024], sc_ps[:, po:po + 1024]
                                        ).then_inc(eEv.h, 1)
                                        eEv.inc()
                                        ich += 2
                                        continue
                                    for jc in (j, j + 1):
                                        vector.wait_ge(eC.h, base_eC + ich + 1)
                                        poc = po + (ich % 2) * 512
                                        soc = so + (ich % 2) * 512
                                        if jc >= 4 * g:
                                            k = jc - 4 * g
                                            op = vector.tensor_tensor(
                                                scr[:, soc:soc + 512], sc_ps[:, poc:poc + 512],
                                                masks[:, k * 512:(k + 1) * 512], ALU.add)
                                        else:
                                            op = vector.tensor_copy(
                                                scr[:, soc:soc + 512], sc_ps[:, poc:poc + 512])
                                        if ich % 2 == 1:
                                            op.then_inc(eEv.h, 1)
                                            eEv.inc()
                                        ich += 1
                                if h >= 1:
                                    tail(vector, h - 1)
                            sums_evac(vector, NH - 1)
                            tail(vector, NH - 1)

                        @block.scalar
                        def _(scalar):
                            def recip_chain(scalar, hb):
                                if hb >= 2:
                                    scalar.wait_ge(eB2.h, base_eB2 + hb - 1)
                                scalar.wait_ge(eSv.h, base_eSv + hb + 1)
                                so = (hb % 2) * 512
                                scalar.activation(recip[0:1, so:so + 512],
                                                  sums_sb[0:1, so:so + 512], AF.Ln)
                                scalar.drain()
                                scalar.activation(recip[0:1, so:so + 512],
                                                  recip[0:1, so:so + 512], AF.Exp,
                                                  scale=-1.0).then_inc(eSm.h, 1)
                                eSm.inc()

                            ich = 0
                            for h in range(NH):
                                for p in range(nch // 2):
                                    pr = ich // 2
                                    scalar.wait_ge(eEv.h, base_eEv + pr + 1)
                                    j0 = 2 * p
                                    col = 4 * h + g
                                    scalar.activation(
                                        ptb[:, j0 * 512:(j0 + 2) * 512],
                                        scr[:, (pr % 4) * 1024:(pr % 4) * 1024 + 1024],
                                        AF.Exp,
                                        bias=bneg_b[:, col:col + 1]).then_inc(eEx.h, 1)
                                    eEx.inc()
                                    ich += 2
                                if h >= 1:
                                    recip_chain(scalar, h - 1)
                            recip_chain(scalar, NH - 1)

        if go("p5"):
            with ExitStack() as S5:
                sfm = S5.enter_context(nc.sbuf_tensor("p5_sfm", [128, NEC * S], BF16))
                S5a = S5.enter_context(ExitStack())
                sb = lambda n, sh, dt: S5a.enter_context(nc.sbuf_tensor(n, sh, dt))
                osb = sb("p5_osb", [128, NH * S], BF16)
                wob = sb("p5_wob", [128, 2 * E], BF16)
                sps = S5a.enter_context(nc.psum_tensor("p5_sps", [128, 2 * 2048], F32))
                fR = sem("fR"); fWp = [sem("fWa"), sem("fWb")]; fM = sem("fM"); fE = sem("fE")
                with nc.Block("p5") as block:
                    @block.sync
                    def _(sync):
                        sync.dma_start(
                            out=osb[:, :],
                            in_=_ap(spo_h, 0, [[S, 128], [128 * S, NH], [1, S]]),
                        ).then_inc(fR.h, 16)
                        fR.inc(16)

                    @block.gpsimd
                    def _(gpsimd):
                        for oc in range(NEC):
                            if oc >= 2:
                                gpsimd.wait_ge(fM.h, oc - 1)
                            gpsimd.dma_start(
                                out=wob[:, (oc % 2) * E:(oc % 2) * E + E],
                                in_=_ap(wo_h, oc * E * 128,
                                        [[128, 128], [128 * 128, NEC], [1, 128]]),
                            ).then_inc(fWp[oc % 2].h, 16)
                            fWp[oc % 2].inc(16)

                    @block.tensor
                    def _(tensor):
                        tensor.wait_ge(fR.h, 16)
                        for oc in range(NEC):
                            tensor.wait_ge(fWp[oc % 2].h, 16 * (oc // 2 + 1))
                            if oc >= 2:
                                tensor.wait_ge(fE.h, oc - 1)
                            po = (oc % 2) * 2048
                            wv_ = wob[:, (oc % 2) * E:(oc % 2) * E + E]
                            for ec in range(NEC):
                                for q4 in range(4):
                                    mm = tensor.matmul(
                                        sps[:, po + q4 * 512:po + (q4 + 1) * 512],
                                        wv_[:, ec * 128:(ec + 1) * 128],
                                        osb[:, ec * S + q4 * 512: ec * S + (q4 + 1) * 512],
                                        start=(ec == 0), stop=(ec == NEC - 1))
                            mm.then_inc(fM.h, 1)
                            fM.inc()

                    @block.scalar
                    def _(scalar):
                        for oc in range(NEC):
                            scalar.wait_ge(fM.h, oc + 1)
                            po = (oc % 2) * 2048
                            scalar.copy(sfm[:, oc * S:(oc + 1) * S], sps[:, po:po + 2048]
                                        ).then_inc(fE.h, 1)
                            fE.inc()

                S5a.close()

                if ts_h is not None:
                    tS2 = sem("tS2")
                    with nc.Block("p5tap") as block:
                        @block.sync
                        def _(sync):
                            sync.dma_start(out=ts_h.ap(), in_=sfm[:, :]).then_inc(tS2.h, 16)
                            sync.wait_ge(tS2.h, tS2.inc(16))

                if go("p6"):
                    with ExitStack() as S6:
                        sb6 = lambda n, sh, dt: S6.enter_context(nc.sbuf_tensor(n, sh, dt))
                        gsb = sb6("p6_gsb", [128, NEC * S], BF16)
                        xt = sb6("p6_xt", [128, 2 * E], F32)
                        ssb = sb6("p6_ssb", [128, 2 * E], BF16)
                        psb = sb6("p6_psb", [128, E], F32)
                        outb = sb6("p6_outb", [128, 2 * E], F32)
                        s_ps = S6.enter_context(nc.psum_tensor("p6_sps", [128, 2 * E], BF16))
                        g_ps = S6.enter_context(nc.psum_tensor("p6_gps", [128, 2 * E], BF16))
                        gR = sem("gR"); gXp = [sem("gXa"), sem("gXb")]
                        gT = sem("gT"); gE = sem("gE")
                        gV = sem("gV"); gStp = [sem("gSta"), sem("gStb")]
                        with nc.Block("p6") as block:
                            @block.sync
                            def _(sync):
                                sync.dma_start(
                                    out=gsb[:, :],
                                    in_=_ap(spg_h, 0, [[S, 128], [128 * S, NEC], [1, S]]),
                                ).then_inc(gR.h, 16)
                                gR.inc(16)
                                for tt in range(NTT):
                                    if tt >= 2:
                                        sync.wait_ge(gV.h, 2 * (tt - 1))
                                    sync.dma_start(
                                        out=xt[:, (tt % 2) * E:(tt % 2) * E + E],
                                        in_=_ap(x_h, tt * 128 * E, [[E, 128], [1, E]]),
                                    ).then_inc(gXp[tt % 2].h, 16)
                                    gXp[tt % 2].inc(16)
                            @block.gpsimd
                            def _(gpsimd):
                                for tt in range(2, NTT):
                                    gpsimd.wait_ge(gV.h, 2 * (tt - 1))
                                    gpsimd.dma_start(
                                        out=_ap(y_h, (tt - 2) * 128 * E, [[E, 128], [1, E]]),
                                        in_=outb[:, (tt % 2) * E:(tt % 2) * E + E],
                                    ).then_inc(gStp[tt % 2].h, 16)
                                    gStp[tt % 2].inc(16)
                                for tt in (NTT - 2, NTT - 1):
                                    gpsimd.wait_ge(gV.h, 2 * (tt + 1))
                                    gpsimd.dma_start(
                                        out=_ap(y_h, tt * 128 * E, [[E, 128], [1, E]]),
                                        in_=outb[:, (tt % 2) * E:(tt % 2) * E + E],
                                    ).then_inc(gStp[tt % 2].h, 16)
                                    gStp[tt % 2].inc(16)
                                gpsimd.wait_ge(gStp[0].h, gStp[0].v)
                                gpsimd.wait_ge(gStp[1].h, gStp[1].v)

                            @block.tensor
                            def _(tensor):
                                tensor.wait_ge(gR.h, 16)
                                for tt in range(NTT):
                                    if tt >= 2:
                                        tensor.wait_ge(gV.h, 2 * (tt - 1))
                                    po = (tt % 2) * E
                                    for oc in range(NEC):
                                        tensor.transpose(
                                            s_ps[:, po + oc * 128:po + (oc + 1) * 128],
                                            sfm[:, oc * S + tt * 128: oc * S + (tt + 1) * 128],
                                            ident[:, :])
                                    for oc in range(NEC):
                                        mm = tensor.transpose(
                                            g_ps[:, po + oc * 128:po + (oc + 1) * 128],
                                            gsb[:, oc * S + tt * 128: oc * S + (tt + 1) * 128],
                                            ident[:, :])
                                    mm.then_inc(gT.h, 1)
                                    gT.inc()

                            @block.scalar
                            def _(scalar):
                                for tt in range(NTT):
                                    scalar.wait_ge(gT.h, tt + 1)
                                    if tt >= 2:
                                        scalar.wait_ge(gV.h, 2 * (tt - 2) + 1)
                                    po = (tt % 2) * E
                                    scalar.copy(ssb[:, po:po + E], s_ps[:, po:po + E]
                                                ).then_inc(gE.h, 1)
                                    gE.inc()

                            @block.vector
                            def _(vector):
                                for tt in range(NTT):
                                    po = (tt % 2) * E
                                    vector.wait_ge(gE.h, tt + 1)
                                    vector.drain()
                                    vector.tensor_tensor(psb[:, :], g_ps[:, po:po + E],
                                                         ssb[:, po:po + E], ALU.mult).then_inc(gV.h, 1)
                                    gV.inc()
                                    vector.wait_ge(gXp[tt % 2].h, 16 * (tt // 2 + 1))
                                    if tt >= 2:
                                        vector.wait_ge(gStp[tt % 2].h, 16 * (tt // 2))
                                    vector.drain()
                                    vector.tensor_tensor(outb[:, po:po + E], psb[:, :],
                                                         xt[:, po:po + E], ALU.add).then_inc(gV.h, 1)
                                    gV.inc()

    return nc


def _prep_host(inputs):
    f32 = np.float32
    g = {}
    Wq = np.asarray(inputs["L0_attn_Wq"], f32)
    Wk = np.asarray(inputs["L0_attn_Wk"], f32)
    Wv = np.asarray(inputs["L0_attn_Wv"], f32)
    Wo = np.asarray(inputs["L0_attn_Wo"], f32)
    Wrq = np.asarray(inputs["L0_attn_Wrq"], f32)
    Wrk = np.asarray(inputs["L0_attn_Wrk"], f32)
    gw = np.asarray(inputs["L0_ffn_gate_w"], f32)
    uw = np.asarray(inputs["L0_ffn_up_w"], f32)
    dw = np.asarray(inputs["L0_ffn_down_w"], f32)

    stripes = lambda WT, n: np.ascontiguousarray(
        WT.reshape(WT.shape[0], n, 128).transpose(1, 0, 2))
    g["wq"] = stripes(Wq.T, NH)
    g["wk"] = stripes(Wk.T, NKV)
    g["wv"] = np.ascontiguousarray(Wv.T)
    g["wo"] = stripes(Wo.T, NEC)
    g["wrq"] = stripes(Wrq.T, 3)
    g["wrk"] = stripes(Wrk.T, 3)
    g["wg"] = stripes(gw.T, NIC)
    g["wu"] = stripes(uw.T, NIC)
    g["wd"] = np.ascontiguousarray(
        dw.T.reshape(NIC, 128, 4, 512).transpose(2, 0, 1, 3))
    g["qnw"] = np.ascontiguousarray(
        np.asarray(inputs["L0_attn_q_norm_w"], f32).reshape(128, 1))
    g["knw"] = np.ascontiguousarray(
        np.asarray(inputs["L0_attn_k_norm_w"], f32).reshape(128, 1))
    gs = np.asarray(inputs["L0_attn_graph_scale"], f32)
    g["gsrep"] = np.ascontiguousarray(np.repeat(gs, 32).reshape(3, 128))
    g["gsrow"] = np.ascontiguousarray(gs.reshape(1, 12))
    g["gates_c"] = np.ascontiguousarray(
        np.asarray(inputs["L0_ffn_gate_s"], f32).reshape(NIC, 128).T)
    g["ups_c"] = np.ascontiguousarray(
        np.asarray(inputs["L0_ffn_up_s"], f32).reshape(NIC, 128).T)
    g["downs_c"] = np.ascontiguousarray(
        np.asarray(inputs["L0_ffn_down_s"], f32).reshape(NEC, 128).T)
    g["ident"] = np.eye(128, dtype=ml_dtypes.bfloat16)
    rt = np.zeros((128, 128), dtype=ml_dtypes.bfloat16)
    for d in range(64):
        rt[d + 64, d] = -1.0
        rt[d, d + 64] = 1.0
    g["rtm"] = rt
    t = np.arange(S, dtype=np.float64)
    invf = 1.0 / (10000.0 ** (np.arange(0, 128, 2, dtype=np.float64) / 128.0))
    fr = np.outer(t, invf)
    emb = np.concatenate([fr, fr], axis=-1)
    g["cosf"] = np.ascontiguousarray(np.cos(emb).T.astype(f32))
    g["sinf"] = np.ascontiguousarray(np.sin(emb).T.astype(f32))
    m = np.zeros((128, 4 * 512), f32)
    p = np.arange(128)[:, None]
    f = np.arange(512)[None, :]
    for k in range(4):
        m[:, k * 512:(k + 1) * 512] = np.where(128 * k + p > f, NEG_BIG, 0.0)
    g["masks"] = m
    g["ones_col"] = np.ones((128, 1), dtype=ml_dtypes.bfloat16)
    g["ones_rowf"] = np.ones((1, 128), f32)
    b4 = np.zeros((128, 4), dtype=ml_dtypes.bfloat16)
    for l in range(4):
        b4[32 * l:32 * (l + 1), l] = 1.0
    g["blk4"] = b4
    return g


def kernel(**inputs):
    shared = _prep_host(inputs)
    x = np.asarray(inputs["L0_attn_x"], np.float32)
    nc = build_program()
    in_maps = []
    for c in range(8):
        m = dict(shared)
        m["x"] = np.ascontiguousarray(x[c])
        in_maps.append(m)
    res = run_bass_kernel_spmd(nc, in_maps, core_ids=list(range(8)))
    return np.stack([res.results[c]["y"] for c in range(8)], axis=0)
